# version 23
# baseline (speedup 1.0000x reference)
import math
import numpy as np
import ml_dtypes
import concourse.bass as bass
import concourse.mybir as mybir
from concourse.bass_utils import run_bass_kernel_spmd

F32 = mybir.dt.float32
BF16 = mybir.dt.bfloat16
AF = mybir.ActivationFunctionType
ALU = mybir.AluOpType
AX = mybir.AxisListType.X

NB = 512


def cfg_full():
    return dict(DM=2048, DI=4096, NS=16, RK=128, HID=5461, HIDP=5504, T=2048)


def cfg_small():
    return dict(DM=256, DI=512, NS=16, RK=128, HID=330, HIDP=384, T=512)


def derived(c):
    d = dict(c)
    d["TP"] = c["T"] + 3
    d["TPAD"] = ((d["TP"] + 127) // 128) * 128
    d["DMC"] = c["DM"] // 128
    d["DIC"] = c["DI"] // 128
    d["HC"] = c["HIDP"] // 128
    d["TB"] = c["T"] // NB
    d["EC"] = 2 * d["DIC"]
    d["HCH"] = (d["HC"] + 1) // 2
    d["EWX"] = c["RK"] + 2 * c["NS"]
    assert c["T"] % NB == 0 and c["DM"] % 128 == 0 and c["DI"] % 128 == 0
    assert c["HIDP"] % 128 == 0 and c["RK"] == 128 and 2 * c["NS"] <= 128
    return d


class _SemPool:
    def __init__(self, nc, n):
        self.nc = nc
        self.n_eng = n
        self.handles = [nc.alloc_semaphore(f"sp{i}") for i in range(n)]
        self.counts = [0] * n
        self.i = 0
        self.ndma = 0
        self.pending = set()

    def get(self):
        idx = self.i % self.n_eng
        self.i += 1
        return _PSem(self, idx)

    def get_dma(self):
        self.handles.append(self.nc.alloc_semaphore(f"dm{self.ndma}"))
        self.counts.append(0)
        self.ndma += 1
        return _PSem(self, len(self.handles) - 1)


class _PSem:
    def __init__(self, pool, idx):
        self.pool = pool
        self.idx = idx
        self.h = pool.handles[idx]
        self.base = pool.counts[idx]
        self.loc = 0

    def inc(self, instr, k=1):
        instr.then_inc(self.h, k)
        self.loc += k
        self.pool.counts[self.idx] = self.base + self.loc
        if k >= 16:
            self.pool.pending.add(self.idx)

    def tgt(self, local):
        return self.base + local


def build(c):
    c = derived(c)
    DM, DI, NS, RK, HID, HIDP = (c["DM"], c["DI"], c["NS"], c["RK"], c["HID"],
                                 c["HIDP"])
    T, TP, TPAD = c["T"], c["TP"], c["TPAD"]
    DMC, DIC, HC, TB, EC, HCH, EWX = (c["DMC"], c["DIC"], c["HC"], c["TB"],
                                      c["EC"], c["HCH"], c["EWX"])

    nc = bass.Bass("TRN2", target_bir_lowering=False)
    pool = _SemPool(nc, 32)

    XPAD = nc.dram_tensor("XPAD", [TPAD, DM], F32, kind="ExternalInput")
    XT = nc.dram_tensor("XT", [DM, T], F32, kind="ExternalInput")
    WIN = nc.dram_tensor("WIN", [DM, 2 * DI], F32, kind="ExternalInput")
    CW = nc.dram_tensor("CW", [DI, 4], F32, kind="ExternalInput")
    CBI = nc.dram_tensor("CBI", [DI], F32, kind="ExternalInput")
    WX = nc.dram_tensor("WX", [EWX, DI], F32, kind="ExternalInput")
    WDT = nc.dram_tensor("WDT", [DI, RK], F32, kind="ExternalInput")
    DTB = nc.dram_tensor("DTB", [DI], F32, kind="ExternalInput")
    ALOG = nc.dram_tensor("ALOG", [DI, NS], F32, kind="ExternalInput")
    DD = nc.dram_tensor("DD", [DI], F32, kind="ExternalInput")
    WOUT = nc.dram_tensor("WOUT", [DI, DM], F32, kind="ExternalInput")
    W1P = nc.dram_tensor("W1P", [DM, HIDP], F32, kind="ExternalInput")
    W3P = nc.dram_tensor("W3P", [DM, HIDP], F32, kind="ExternalInput")
    W2P = nc.dram_tensor("W2P", [HIDP, DM], F32, kind="ExternalInput")
    HCB = nc.dram_tensor("HCB", [DIC, DI, 128], BF16, kind="ExternalInput")
    OUT = nc.dram_tensor("OUT", [DM, T], F32, kind="ExternalOutput")

    XC = nc.dram_tensor("XCs", [DIC, 128, T], F32)
    XCB = nc.dram_tensor("XCBs", [DIC, 128, T], BF16)
    SZG = nc.dram_tensor("SZGs", [DIC, 128, T], F32)
    DLT = nc.dram_tensor("DLTs", [DIC, 128, T], F32)
    YG = nc.dram_tensor("YGs", [DIC, 128, T], BF16)
    GS = nc.dram_tensor("GSs", [HC, TB, 128, T // TB], BF16)
    FF = nc.dram_tensor("FFs", [DMC, TB, 128, T // TB], F32)
    W1T = nc.dram_tensor("W1Ts", [HC, 128, DMC * 128], BF16)
    W3T = nc.dram_tensor("W3Ts", [HC, 128, DMC * 128], BF16)
    W2T = nc.dram_tensor("W2Ts", [DMC, 128, HC * 128], BF16)
    HWQ = nc.dram_tensor("HWQs", [DMC, DIC, 128, 128], BF16)
    BCD = nc.dram_tensor("BCDs", [2 * NS, T], F32)
    BCD16 = nc.dram_tensor("BCD16s", [2 * NS, T], BF16)

    ident_bf_d = nc.inline_tensor(np.eye(128, dtype=ml_dtypes.bfloat16),
                                  "identbf")
    ident_ng_d = nc.inline_tensor(-np.eye(128, dtype=ml_dtypes.bfloat16),
                                  "identng")
    ident_f32_d = nc.inline_tensor(np.eye(128, dtype=np.float32), "identf32")
    ones_col_d = nc.inline_tensor(np.ones((128, 1), dtype=np.float32),
                                  "onescol")
    ones_row_d = nc.inline_tensor(np.ones((1, 128), dtype=np.float32),
                                  "onesrow")

    BASE = 16512
    TOPB = 229344
    off = [BASE]

    def nbytes_of(shape, dtype):
        return int(np.prod(shape[1:])) * (2 if dtype == BF16 else 4)

    def salloc(name, shape, dtype):
        at = ((off[0] + 31) // 32) * 32
        t = nc.alloc_sbuf_tensor_at(name, shape, dtype, offset=at,
                                    align_bytes=32)
        off[0] = at + nbytes_of(shape, dtype)
        return t

    ident_bf = salloc("identbf_sb", [128, 128], BF16)
    ident_ng = salloc("identng_sb", [128, 128], BF16)
    ident_f32 = salloc("identf32_sb", [128, 128], F32)
    a_sb = salloc("a_sb", [128, DIC * NS], F32)
    state = salloc("state_sb", [128, DIC * NS], F32)
    bc_sb = salloc("bc_sb", [32, T], F32)
    cw_sb = salloc("cw_sb", [128, DIC * 4], F32)
    cb_sb = salloc("cb_sb", [128, DIC], F32)
    db_sb = salloc("db_sb", [128, DIC], F32)
    d_sb = salloc("d_sb", [128, DIC], F32)
    dtr_sb = salloc("dtr_sb", [128, T], BF16)
    wxt_sb = salloc("wxt_sb", [128, DIC * EWX], BF16)
    wdtt_sb = salloc("wdtt_sb", [128, DI], BF16)
    xhalo_sb = salloc("xhalo_sb", [128, DIC * 3], F32)
    ones_col = salloc("ones_col_sb", [128, 1], F32)
    ones_row = salloc("ones_row_sb", [1, 128], F32)
    s1_col = salloc("s1_col", [128, 1], F32)
    s23_col = salloc("s23_col", [128, 1], F32)
    r1_col = salloc("r1_col", [128, 1], F32)
    r3_col = salloc("r3_col", [128, 1], F32)
    r2_col = salloc("r2_col", [128, 1], F32)
    sc_tmp = salloc("sc_tmp", [1, 8], F32)
    msums = salloc("msums", [128, 2 * HC + 4 * DMC + 8], F32)
    ln_cols = salloc("ln_cols", [128, 12], F32)
    BIG = ((off[0] + 31) // 32) * 32

    pA = nc.alloc_psum_tensor("pA", [128, 4 * NB], F32)
    pB = nc.alloc_psum_tensor("pB", [128, 4 * NB], F32)

    def big(name, shape, dtype, at):
        at = BIG + at
        assert at % 32 == 0, (name, at)
        t = nc.alloc_sbuf_tensor_at(name, shape, dtype, offset=at,
                                    align_bytes=32)
        assert at + nbytes_of(shape, dtype) <= TOPB, (
            name, at + nbytes_of(shape, dtype), TOPB)
        return t

    def bank(i):
        ps = pA if (i % 8) < 4 else pB
        return ps, ((i % 8) % 4) * NB

    def drain():
        if not pool.pending:
            return
        targets = [(pool.handles[i], pool.counts[i]) for i in pool.pending]
        pool.pending.clear()
        with nc.Block() as dblk:
            @dblk.sync
            def _(sync):
                for h, v in targets:
                    sync.wait_ge(h, v)

    class _Solo:
        def __init__(self, name):
            self.h = nc.alloc_semaphore(name)
            self.loc = 0

        def inc(self, instr, k=1):
            instr.then_inc(self.h, k)
            self.loc += k

        def tgt(self, local):
            return local

    VCH = _Solo("vch")
    SCH = _Solo("sch")

    def vfence(eng, s):
        eng.wait_ge(s.h, s.tgt(s.loc))

    s_c = pool.get_dma()
    with nc.Block() as blk:
        @blk.sync
        def _(sync):
            s_c.inc(sync.dma_start(ident_bf[:, :], ident_bf_d.ap()), 16)
            s_c.inc(sync.dma_start(ident_ng[:, :], ident_ng_d.ap()), 16)
            s_c.inc(sync.dma_start(ident_f32[:, :], ident_f32_d.ap()), 16)
            s_c.inc(sync.dma_start(ones_col[:, :], ones_col_d.ap()), 16)
            s_c.inc(sync.dma_start(ones_row[:, :], ones_row_d.ap()), 16)
            s_c.inc(sync.dma_start(
                a_sb[:, :].rearrange("p (g n) -> p g n", n=NS),
                ALOG.ap().rearrange("(g p) n -> p g n", p=128)), 16)
            s_c.inc(sync.dma_start(
                cw_sb[:, :].rearrange("p (g k) -> p g k", k=4),
                CW.ap().rearrange("(g p) k -> p g k", p=128)), 16)
            s_c.inc(sync.dma_start(
                cb_sb[:, :], CBI.ap().rearrange("(g p) -> p g", p=128),
                allow_slow_non_contiguous=True), 16)
            s_c.inc(sync.dma_start(
                db_sb[:, :], DTB.ap().rearrange("(g p) -> p g", p=128),
                allow_slow_non_contiguous=True), 16)
            s_c.inc(sync.dma_start(
                d_sb[:, :], DD.ap().rearrange("(g p) -> p g", p=128),
                allow_slow_non_contiguous=True), 16)

        @blk.scalar
        def _(scalar):
            scalar.wait_ge(s_c.h, s_c.tgt(10 * 16))
            scalar.activation(a_sb[:, :], a_sb[:, :], AF.Exp)

        @blk.gpsimd
        def _(gpsimd):
            gpsimd.memset(state[:, :], 0.0)

    with nc.Block() as blk:
        @blk.vector
        def _(vector):
            vector.tensor_scalar_mul(db_sb[:, :], db_sb[:, :], -1.0)

    drain()

    wot = big("wot", [128, DIC * DM], BF16, 0)
    WOTB = DIC * DM * 2

    s_l1 = pool.get_dma()
    with nc.Block() as blk:
        @blk.gpsimd
        def _(gpsimd):
            wv = WOUT.ap().rearrange("(j p) m -> p j m", p=128)
            wd = wot[:, :].rearrange("p (j m) -> p j m", m=DM)
            q1 = max(DIC // 4, 1)
            for k in range(0, DIC, q1):
                s_l1.inc(gpsimd.dma_start(wd[:, k:k + q1], wv[:, k:k + q1]),
                         16)

    drain()

    hcb_buf = [big(f"hcb_{i}", [128, DIC * 128], BF16,
                   WOTB + i * DIC * 128 * 2) for i in range(2)]
    hw_st = [big(f"hwst_{i}", [128, DM], BF16,
                 WOTB + 2 * DIC * 128 * 2 + i * DM * 2) for i in range(2)]

    s_l2 = [pool.get_dma(), pool.get_dma()]
    s_m2 = pool.get()
    s_e2 = pool.get()
    s_s2 = [pool.get_dma(), pool.get_dma()]
    dmb = [(s, min(NB, DM - s)) for s in range(0, DM, NB)]
    with nc.Block() as blk:
        @blk.sync
        def _(sync):
            for ci in range(DIC):
                if ci >= 2:
                    sync.wait_ge(s_m2.h, s_m2.tgt(ci - 1))
                s_l2[ci % 2].inc(sync.dma_start(
                    hcb_buf[ci % 2][:, :].rearrange("p (cc m) -> p cc m",
                                                    m=128),
                    HCB.ap()[ci].rearrange("(cc p) m -> p cc m", p=128)), 16)

        @blk.tensor
        def _(tensor):
            tensor.wait_ge(s_l1.h, s_l1.tgt(s_l1.loc))
            for ci in range(DIC):
                tensor.wait_ge(s_l2[ci % 2].h,
                               s_l2[ci % 2].tgt((ci // 2 + 1) * 16))
                if ci >= 2:
                    tensor.wait_ge(s_e2.h, s_e2.tgt(ci - 1))
                ps = pA if ci % 2 == 0 else pB
                last = None
                for cc in range(DIC):
                    st = hcb_buf[ci % 2][:, cc * 128:(cc + 1) * 128]
                    for (bs, bn) in dmb:
                        last = tensor.matmul(
                            ps[:, bs:bs + bn], st,
                            wot[:, cc * DM + bs:cc * DM + bs + bn],
                            start=(cc == 0), stop=(cc == DIC - 1))
                s_m2.inc(last)

        @blk.scalar
        def _(scalar):
            for ci in range(DIC):
                scalar.wait_ge(s_m2.h, s_m2.tgt(ci + 1))
                if ci >= 2:
                    scalar.wait_ge(s_s2[ci % 2].h,
                                   s_s2[ci % 2].tgt((ci - 2) // 2 * 16 + 16))
                ps = pA if ci % 2 == 0 else pB
                s_e2.inc(scalar.activation(hw_st[ci % 2][:, :], ps[:, :DM],
                                           AF.Copy))

        @blk.gpsimd
        def _(gpsimd):
            for ci in range(DIC):
                gpsimd.wait_ge(s_e2.h, s_e2.tgt(ci + 1))
                s_s2[ci % 2].inc(gpsimd.dma_start(
                    HWQ.ap()[:, ci].rearrange("cc p m -> p cc m"),
                    hw_st[ci % 2][:, :].rearrange("p (cc m) -> p cc m", m=128),
                ), 16)

    drain()

    xn_f = big("xn_f", [128, DMC * TPAD], BF16, 0)
    XNO = DMC * TPAD * 2
    QW = max(HIDP, DI, DIC * RK, DM)
    QFW = max(DM, 128 * ((HIDP // 128 + 2) // 3))
    QPW = max(QFW, DM)
    NQB = 4
    QF = [big(f"q_f32_{i}", [128, QFW], F32, XNO + i * QFW * 4)
          for i in range(NQB)]
    O1 = XNO + NQB * QFW * 4
    QPOS = [big(f"q_pos{i}", [128, QPW], F32, O1 + i * QPW * 4)
            for i in range(3)]
    TSW = max(HC * 128, DMC * 128, DIC * RK)
    TSTG = [big(f"tstg{i}", [128, TSW], BF16,
                O1 + 3 * QPW * 4 + i * ((TSW * 2 + 31) // 32) * 32)
            for i in range(2)]
    WXP1 = big("wxp1", [128, DI], F32, XNO)
    WXP2 = big("wxp2", [128, DI], F32, XNO + DI * 4)
    WDTH = big("wdth", [128, DIC * RK], F32,
               XNO + 2 * DI * 4 if 2 * DI * 4 + DIC * RK * 4 <= NQB * QFW * 4
               else O1)
    QEND = O1 + 3 * QPW * 4 + 2 * ((TSW * 2 + 31) // 32) * 32
    OVL = O1
    if OVL + 5 * DM * 4 > QEND:
        OVL = QEND
    x_tile = [big(f"x_t{i}", [128, DM], F32, OVL + i * DM * 4)
              for i in range(2)]
    xm_tile = big("xm_t", [128, DM], F32, OVL + 2 * DM * 4)
    xnt_tile = [big(f"xnt_{i}", [128, DM], F32, OVL + (3 + i) * DM * 4)
                for i in range(2)]

    NT = TPAD // 128
    s_l3 = [pool.get_dma(), pool.get_dma()]
    s_v3 = pool.get()
    s_a3 = pool.get()
    s_n3 = pool.get()
    s_m3 = pool.get()
    s_e3 = pool.get()
    egroups = [(j0, min(4, DMC - j0)) for j0 in range(0, DMC, 4)]
    NEV = len(egroups)
    B3_EV_TOTAL = NT * NEV

    qch = []
    for ch in range(HC):
        qch.append((W1P, W1T, ch, "1", r1_col))
    for ch in range(HC):
        qch.append((W3P, W3T, ch, "3", r3_col))
    for ch in range(HIDP // 128):
        qch.append((W2P, W2T, ch, "2", r2_col))
    NMEAN = len(qch)
    NSCHED = 2 * NMEAN
    FD = DM

    def qload_src(Ws, ch, kind):
        if kind == "2":
            return Ws.ap()[ch * 128:(ch + 1) * 128, :]
        v = Ws.ap().rearrange("(j p) h -> p j h", p=128)
        return v[:, :, ch * 128:(ch + 1) * 128]

    s_ql = [pool.get_dma() for _ in range(NQB)]
    s_qdone = pool.get()
    s_m4 = pool.get()
    s_v4 = pool.get()
    s_scl = pool.get()
    s_ts = [pool.get_dma(), pool.get_dma()]
    s_wx = pool.get_dma()
    s_wm = pool.get()
    s_we = pool.get()

    with nc.Block() as blk:
        @blk.sync
        def _(sync):
            for i in range(NT):
                if i >= 2:
                    sync.wait_ge(s_v3.h, s_v3.tgt((i - 1) * 2))
                s_l3[i % 2].inc(
                    sync.dma_start(x_tile[i % 2][:, :],
                                   XPAD.ap()[i * 128:(i + 1) * 128, :]), 16)
            for rep in range(2):
                for ci_, (Ws, Wd, ch, kind, rc) in enumerate(qch):
                    gidx = rep * NMEAN + ci_
                    if gidx % 2 == 1:
                        continue
                    if gidx >= NQB:
                        sync.wait_ge(s_qdone.h,
                                     s_qdone.tgt(gidx - NQB + 1))
                    dst = QF[gidx % NQB][:, :FD]
                    if kind != "2":
                        dst = dst.rearrange("p (j h) -> p j h", h=128)
                    s_ql[gidx % NQB].inc(
                        sync.dma_start(dst, qload_src(Ws, ch, kind)), 16)
            sync.wait_ge(s_qdone.h, s_qdone.tgt(NSCHED))
            s_wx.inc(sync.dma_start(WXP1[:, :], WX.ap()[0:128, :]), 16)
            s_wx.inc(sync.dma_start(WXP2[0:EWX - 128, :],
                                    WX.ap()[128:EWX, :]), 16)
            s_wx.inc(sync.dma_start(
                WDTH[:, :].rearrange("p (g r) -> p g r", r=RK),
                WDT.ap().rearrange("(g p) r -> p g r", p=128)), 16)

        @blk.vector
        def _(vector):
            for i in range(NT):
                p4 = (i % 2) * 4
                xt = x_tile[i % 2]
                vector.wait_ge(s_l3[i % 2].h,
                               s_l3[i % 2].tgt((i // 2 + 1) * 16))
                if i >= 2:
                    vector.wait_ge(s_a3.h, s_a3.tgt(i - 1))
                VCH.inc(vector.tensor_reduce(ln_cols[:, p4:p4 + 1], xt[:, :],
                                             AX, ALU.add))
                vfence(vector, VCH)
                VCH.inc(vector.tensor_scalar_mul(ln_cols[:, p4 + 1:p4 + 2],
                                                 ln_cols[:, p4:p4 + 1],
                                                 1.0 / DM))
                vfence(vector, VCH)
                s_v3.inc(vector.tensor_scalar(xm_tile[:, :], xt[:, :],
                                              ln_cols[:, p4 + 1:p4 + 2], None,
                                              ALU.subtract))
                vfence(vector, s_v3)
                VCH.inc(vector.scalar_tensor_tensor(
                    x_tile[i % 2][:, :], xm_tile[:, :], 1.0, xm_tile[:, :],
                    ALU.mult, ALU.mult, accum_out=ln_cols[:, p4 + 2:p4 + 3]))
                vfence(vector, VCH)
                s_v3.inc(vector.tensor_scalar(ln_cols[:, p4 + 3:p4 + 4],
                                              ln_cols[:, p4 + 2:p4 + 3],
                                              1.0 / DM, 1e-5,
                                              ALU.mult, ALU.add))
                vector.wait_ge(s_a3.h, s_a3.tgt(i + 1))
                if i >= 2:
                    vector.wait_ge(s_m3.h, s_m3.tgt((i - 1) * DMC))
                s_n3.inc(vector.tensor_scalar(xnt_tile[i % 2][:, :],
                                              xm_tile[:, :],
                                              ln_cols[:, 8 + i % 2:9 + i % 2],
                                              None, ALU.mult))
            for gidx in range(NMEAN):
                vector.wait_ge(s_ql[gidx % NQB].h,
                               s_ql[gidx % NQB].tgt((gidx // NQB + 1) * 16))
                s_qdone.inc(vector.tensor_reduce(
                    msums[:, gidx:gidx + 1], QF[gidx % NQB][:, :FD], AX,
                    ALU.add, apply_absolute_value=True))
            vfence(vector, s_qdone)
            base = 0
            last = None
            for wi, nch in enumerate([HC, HC, HIDP // 128]):
                last = vector.tensor_reduce(
                    msums[:, NMEAN + wi:NMEAN + wi + 1],
                    msums[:, base:base + nch], AX, ALU.add)
                base += nch
            s_v4.inc(last)
            vector.wait_ge(s_m4.h, s_m4.tgt(1))
            VCH.inc(vector.tensor_scalar_mul(sc_tmp[0:1, 0:3], pA[0:1, 0:3],
                                             1.0 / (HID * DM)))
            vfence(vector, VCH)
            VCH.inc(vector.tensor_mul(sc_tmp[0:1, 3:4], sc_tmp[0:1, 1:2],
                                      sc_tmp[0:1, 2:3]))
            vfence(vector, VCH)
            s_v4.inc(vector.reciprocal(sc_tmp[0:1, 4:7], sc_tmp[0:1, 0:3]))
            vector.wait_ge(s_e3.h, s_e3.tgt(B3_EV_TOTAL))
            vector.wait_ge(s_scl.h, s_scl.tgt(1))
            for qi, (Ws, Wd, ch, kind, rc) in enumerate(qch):
                gidx = NMEAN + qi
                vector.wait_ge(s_ql[gidx % NQB].h,
                               s_ql[gidx % NQB].tgt((gidx // NQB + 1) * 16))
                if qi >= 2:
                    pp = qi % 2
                    vector.wait_ge(s_ts[pp].h,
                                   s_ts[pp].tgt(((qi - 2) // 2 + 1) * 16))
                if qi >= 3:
                    vector.wait_ge(s_qdone.h, s_qdone.tgt(NMEAN + qi - 2))
                VCH.inc(vector.tensor_scalar(QPOS[qi % 3][:, :FD],
                                             QF[gidx % NQB][:, :FD],
                                             rc[:, 0:1], 0.5,
                                             ALU.mult, ALU.is_gt))
                VCH.inc(vector.tensor_scalar(QF[gidx % NQB][:, :FD],
                                             QF[gidx % NQB][:, :FD],
                                             rc[:, 0:1], -0.5,
                                             ALU.mult, ALU.is_lt))
                vfence(vector, VCH)
                s_qdone.inc(vector.tensor_sub(TSTG[qi % 2][:, :FD],
                                              QPOS[qi % 3][:, :FD],
                                              QF[gidx % NQB][:, :FD]))
            nb_ = 0
            for g in range(DIC):
                for part in range(3):
                    ps, sl = bank(nb_ % 4)
                    vector.wait_ge(s_wm.h, s_wm.tgt(nb_ + 1))
                    if part == 0:
                        s_we.inc(vector.tensor_copy(
                            wxt_sb[:, g * EWX:g * EWX + 128],
                            ps[:, sl:sl + 128]))
                    elif part == 1:
                        s_we.inc(vector.tensor_copy(
                            wxt_sb[:, g * EWX + 128:(g + 1) * EWX],
                            ps[:, sl:sl + EWX - 128]))
                    else:
                        s_we.inc(vector.tensor_copy(
                            wdtt_sb[:, g * 128:(g + 1) * 128],
                            ps[:, sl:sl + 128]))
                    nb_ += 1

        @blk.scalar
        def _(scalar):
            def evac3(i):
                for gi_, (j0, jn) in enumerate(egroups):
                    ev = i * NEV + gi_
                    sl0 = (ev % 2) * NB
                    scalar.wait_ge(s_m3.h, s_m3.tgt(i * DMC + j0 + jn))
                    dst = xn_f[:, :].rearrange("p (j t) -> p j t", t=TPAD)
                    s_e3.inc(scalar.activation(
                        dst[:, j0:j0 + jn, i * 128:(i + 1) * 128],
                        pB[:, sl0:sl0 + jn * 128].rearrange(
                            "p (j m) -> p j m", m=128),
                        AF.Copy))

            for i in range(NT):
                p4 = (i % 2) * 4
                scalar.wait_ge(s_v3.h, s_v3.tgt(i * 2 + 2))
                SCH.inc(scalar.activation(ln_cols[:, 10:11],
                                          ln_cols[:, p4 + 3:p4 + 4], AF.Ln))
                vfence(scalar, SCH)
                s_a3.inc(scalar.activation(ln_cols[:, 8 + i % 2:9 + i % 2],
                                           ln_cols[:, 10:11], AF.Exp,
                                           scale=-0.5))
                if i >= 1:
                    evac3(i - 1)
            evac3(NT - 1)
            scalar.wait_ge(s_m4.h, s_m4.tgt(2))
            scalar.activation(s1_col[:, :], pB[:, 2 * NB:2 * NB + 1], AF.Copy)
            scalar.activation(s23_col[:, :], pB[:, 2 * NB + 1:2 * NB + 2],
                              AF.Copy)
            scalar.activation(r1_col[:, :], pB[:, 2 * NB + 2:2 * NB + 3],
                              AF.Copy)
            scalar.activation(r3_col[:, :], pB[:, 2 * NB + 3:2 * NB + 4],
                              AF.Copy)
            s_scl.inc(scalar.activation(r2_col[:, :],
                                        pB[:, 2 * NB + 4:2 * NB + 5],
                                        AF.Copy))

        @blk.tensor
        def _(tensor):
            for i in range(NT):
                tensor.wait_ge(s_n3.h, s_n3.tgt(i + 1))
                for gi_, (j0, jn) in enumerate(egroups):
                    ev = i * NEV + gi_
                    sl0 = (ev % 2) * NB
                    if ev >= 2:
                        tensor.wait_ge(s_e3.h, s_e3.tgt(ev - 1))
                    for j in range(j0, j0 + jn):
                        s_m3.inc(tensor.transpose(
                            pB[:, sl0 + (j - j0) * 128:sl0 + (j - j0 + 1) * 128],
                            xnt_tile[i % 2][:, j * 128:(j + 1) * 128],
                            ident_f32[:, :]))
            tensor.wait_ge(s_v4.h, s_v4.tgt(1))
            s_m4.inc(tensor.matmul(pA[0:1, 0:3], ones_col[:, :],
                                   msums[:, NMEAN:NMEAN + 3],
                                   start=True, stop=True))
            tensor.wait_ge(s_v4.h, s_v4.tgt(2))
            last = None
            for k, src_i in enumerate([0, 3, 4, 5, 6]):
                last = tensor.matmul(pB[:, 2 * NB + k:2 * NB + k + 1],
                                     ones_row[:, :],
                                     sc_tmp[0:1, src_i:src_i + 1],
                                     start=True, stop=True)
            s_m4.inc(last)
            tensor.wait_ge(s_wx.h, s_wx.tgt(48))
            nb_ = 0
            for g in range(DIC):
                for part in range(3):
                    ps, sl = bank(nb_ % 4)
                    if nb_ >= 4:
                        tensor.wait_ge(s_we.h, s_we.tgt(nb_ - 3))
                    if part == 0:
                        s_wm.inc(tensor.transpose(
                            ps[:, sl:sl + 128],
                            WXP1[:, g * 128:(g + 1) * 128], ident_f32[:, :]))
                    elif part == 1:
                        s_wm.inc(tensor.transpose(
                            ps[:, sl:sl + EWX - 128],
                            WXP2[0:EWX - 128, g * 128:(g + 1) * 128],
                            ident_f32[0:EWX - 128, 0:EWX - 128]))
                    else:
                        s_wm.inc(tensor.transpose(
                            ps[:, sl:sl + 128],
                            WDTH[:, g * RK:(g + 1) * RK], ident_f32[:, :]))
                    nb_ += 1

        @blk.gpsimd
        def _(gpsimd):
            def qload(gidx):
                ci_ = gidx % NMEAN
                (Ws, Wd, ch, kind, rc) = qch[ci_]
                if gidx >= NQB:
                    gpsimd.wait_ge(s_qdone.h, s_qdone.tgt(gidx - NQB + 1))
                dst = QF[gidx % NQB][:, :FD]
                if kind != "2":
                    dst = dst.rearrange("p (j h) -> p j h", h=128)
                s_ql[gidx % NQB].inc(
                    gpsimd.dma_start(dst, qload_src(Ws, ch, kind)), 16)

            def qstore(qi):
                (Ws, Wd, ch, kind, rc) = qch[qi]
                gpsimd.wait_ge(s_qdone.h, s_qdone.tgt(NMEAN + qi + 1))
                if kind == "2":
                    dst = Wd.ap()[:, :, ch * 128:(ch + 1) * 128].rearrange(
                        "c p m -> p c m")
                    srcv = TSTG[qi % 2][:, :FD].rearrange(
                        "p (c m) -> p c m", m=128)
                    s_ts[qi % 2].inc(gpsimd.dma_start(dst, srcv), 16)
                else:
                    s_ts[qi % 2].inc(gpsimd.dma_start(
                        Wd.ap()[ch], TSTG[qi % 2][:, :FD]), 16)

            for gidx in range(NSCHED):
                if gidx % 2 == 1:
                    qload(gidx)
                if gidx >= NMEAN:
                    qstore(gidx - NMEAN)

    drain()

    drain()

    st_buf = [big(f"p3st_{i}", [128, DMC * 128], BF16, XNO + i * DMC * 128 * 2)
              for i in range(2)]
    P3O = XNO + 2 * DMC * 128 * 2
    XBW = ((TP + 7) // 8) * 8
    xbuf = [big(f"p3xb_{i}", [128, XBW], F32, P3O + i * XBW * 4)
            for i in range(2)]
    P3O2 = P3O + 2 * XBW * 4
    xcacc = [big(f"p3acc_{i}", [128, T], F32, P3O2 + i * T * 4)
             for i in range(2)]
    xcf = [big(f"p3xc_{i}", [128, T], F32, P3O2 + (2 + i) * T * 4)
           for i in range(2)]
    xcb = [big(f"p3xcb_{i}", [128, T], BF16, P3O2 + 4 * T * 4 + i * T * 2)
           for i in range(2)]
    zst = [big(f"p3z_{i}", [128, T], F32,
               P3O2 + 4 * T * 4 + 2 * T * 2 + i * T * 4) for i in range(2)]

    s_l6 = [pool.get_dma(), pool.get_dma()]
    s_m6 = pool.get()
    s_f6 = pool.get()
    s_cv6 = pool.get()
    s_si6 = pool.get()
    s_st6x = [pool.get_dma(), pool.get_dma()]
    s_st6z = [pool.get_dma(), pool.get_dma()]
    s_h6 = pool.get()
    s_hv6 = pool.get()
    with nc.Block() as blk:
        @blk.tensor
        def _(tensor):
            xnr = xn_f[:, :].rearrange("p (j t) -> p j t", t=TPAD)
            for ci in range(EC):
                tensor.wait_ge(s_l6[ci % 2].h,
                               s_l6[ci % 2].tgt((ci // 2 + 1) * 16))
                if ci >= 2:
                    tensor.wait_ge(s_f6.h, s_f6.tgt(ci - 1))
                if 1 <= ci <= DIC:
                    tensor.wait_ge(s_hv6.h, s_hv6.tgt(ci))
                ps = pA if ci % 2 == 0 else pB
                last = None
                for j in range(DMC):
                    st = st_buf[ci % 2][:, j * 128:(j + 1) * 128]
                    for b in range(TB):
                        last = tensor.matmul(
                            ps[:, b * NB:(b + 1) * NB], st,
                            xnr[:, j, 3 + b * NB:3 + (b + 1) * NB],
                            start=(j == 0), stop=(j == DMC - 1))
                s_m6.inc(last)
                if ci < DIC:
                    po = pB if ci % 2 == 0 else pA
                    if ci >= 1:
                        tensor.wait_ge(s_f6.h, s_f6.tgt(ci))
                    if ci >= 2:
                        tensor.wait_ge(s_hv6.h, s_hv6.tgt(ci - 1))
                    lh = None
                    for j in range(DMC):
                        lh = tensor.matmul(
                            po[:, 0:3], st_buf[ci % 2][:, j * 128:(j + 1) * 128],
                            xnr[:, j, 0:3],
                            start=(j == 0), stop=(j == DMC - 1))
                    s_h6.inc(lh)

        @blk.scalar
        def _(scalar):
            nz = 0
            for ci in range(EC):
                scalar.wait_ge(s_m6.h, s_m6.tgt(ci + 1))
                ps = pA if ci % 2 == 0 else pB
                if ci < DIC:
                    if ci >= 2:
                        scalar.wait_ge(s_cv6.h, s_cv6.tgt(ci - 1))
                    scalar.wait_ge(s_h6.h, s_h6.tgt(ci + 1))
                    po = pB if ci % 2 == 0 else pA
                    s_hv6.inc(scalar.activation(xbuf[ci % 2][:, 0:3],
                                                po[:, 0:3], AF.Copy))
                    s_f6.inc(scalar.activation(xbuf[ci % 2][:, 3:3 + T],
                                               ps[:, :T], AF.Copy))
                    scalar.wait_ge(s_cv6.h, s_cv6.tgt(ci + 1))
                    if ci >= 2:
                        scalar.wait_ge(s_st6x[ci % 2].h,
                                       s_st6x[ci % 2].tgt((ci - 2) // 2 * 32
                                                          + 32))
                    scalar.activation(xcf[ci % 2][:, :], xcacc[ci % 2][:, :],
                                      AF.Silu, bias=cb_sb[:, ci:ci + 1])
                    s_si6.inc(scalar.activation(xcb[ci % 2][:, :],
                                                xcacc[ci % 2][:, :], AF.Silu,
                                                bias=cb_sb[:, ci:ci + 1]))
                else:
                    if nz >= 2:
                        scalar.wait_ge(s_st6z[nz % 2].h,
                                       s_st6z[nz % 2].tgt((nz - 2) // 2 * 16
                                                          + 16))
                    s_f6.inc(scalar.activation(zst[nz % 2][:, :], ps[:, :T],
                                               AF.Silu))
                    nz += 1

        @blk.vector
        def _(vector):
            for ci in range(DIC):
                vector.wait_ge(s_f6.h, s_f6.tgt(ci + 1))
                if ci >= 2:
                    vector.wait_ge(s_si6.h, s_si6.tgt(ci - 1))
                xb = xbuf[ci % 2]
                acc = xcacc[ci % 2]
                VCH.inc(vector.tensor_scalar_mul(acc[:, :], xb[:, 0:T],
                                                 cw_sb[:, ci * 4:ci * 4 + 1]))
                for k in range(1, 4):
                    vfence(vector, VCH)
                    st_ = vector.scalar_tensor_tensor(
                        acc[:, :], xb[:, k:k + T],
                        cw_sb[:, ci * 4 + k:ci * 4 + k + 1],
                        acc[:, :], ALU.mult, ALU.add)
                    if k < 3:
                        VCH.inc(st_)
                    else:
                        s_cv6.inc(st_)

        @blk.gpsimd
        def _(gpsimd):
            winv6 = WIN.ap().rearrange("(j p) e -> p j e", p=128)
            for ci in range(min(2, EC)):
                s_l6[ci % 2].inc(gpsimd.dma_start(
                    st_buf[ci % 2][:, :].rearrange("p (j m) -> p j m", m=128),
                    winv6[:, :, ci * 128:(ci + 1) * 128]), 16)
            nz = 0
            for ci in range(EC):
                if ci + 2 < EC:
                    cl = ci + 2
                    gpsimd.wait_ge(s_m6.h, s_m6.tgt(cl - 1))
                    if cl - 2 < DIC:
                        gpsimd.wait_ge(s_h6.h, s_h6.tgt(cl - 1))
                    s_l6[cl % 2].inc(gpsimd.dma_start(
                        st_buf[cl % 2][:, :].rearrange("p (j m) -> p j m",
                                                       m=128),
                        winv6[:, :, cl * 128:(cl + 1) * 128]), 16)
                if ci < DIC:
                    gpsimd.wait_ge(s_si6.h, s_si6.tgt(ci + 1))
                    s_st6x[ci % 2].inc(
                        gpsimd.dma_start(XC.ap()[ci], xcf[ci % 2][:, :]), 16)
                    s_st6x[ci % 2].inc(
                        gpsimd.dma_start(XCB.ap()[ci], xcb[ci % 2][:, :]), 16)
                else:
                    gpsimd.wait_ge(s_f6.h, s_f6.tgt(ci + 1))
                    s_st6z[nz % 2].inc(
                        gpsimd.dma_start(SZG.ap()[ci - DIC],
                                         zst[nz % 2][:, :]), 16)
                    nz += 1

    drain()

    xcb_buf = [big(f"p4x_{i}", [128, T], BF16, XNO + i * T * 2)
               for i in range(2)]

    s_l9 = [pool.get_dma(), pool.get_dma()]
    s_m9 = pool.get()
    s_bc9 = pool.get_dma()
    with nc.Block() as blk:
        @blk.sync
        def _(sync):
            for g in range(DIC):
                if g >= 2:
                    sync.wait_ge(s_m9.h, s_m9.tgt(g - 1))
                s_l9[g % 2].inc(
                    sync.dma_start(xcb_buf[g % 2][:, :], XCB.ap()[g]), 16)

        @blk.tensor
        def _(tensor):
            for g in range(DIC):
                tensor.wait_ge(s_l9[g % 2].h,
                               s_l9[g % 2].tgt((g // 2 + 1) * 16))
                st1 = wxt_sb[:, g * EWX:g * EWX + 128]
                st2 = wxt_sb[:, g * EWX + 128:(g + 1) * EWX]
                last = None
                for b in range(TB):
                    tensor.matmul(pA[:, b * NB:(b + 1) * NB], st1,
                                  xcb_buf[g % 2][:, b * NB:(b + 1) * NB],
                                  start=(g == 0), stop=(g == DIC - 1))
                    last = tensor.matmul(
                        pB[0:2 * NS, b * NB:(b + 1) * NB], st2,
                        xcb_buf[g % 2][:, b * NB:(b + 1) * NB],
                        start=(g == 0), stop=(g == DIC - 1))
                s_m9.inc(last)

        @blk.scalar
        def _(scalar):
            scalar.wait_ge(s_m9.h, s_m9.tgt(DIC))
            scalar.activation(dtr_sb[:, :], pA[:, :T], AF.Copy)

        @blk.vector
        def _(vector):
            vector.wait_ge(s_m9.h, s_m9.tgt(DIC))
            s_m9.inc(vector.tensor_copy(bc_sb[:, :], pB[0:2 * NS, :T]))

        @blk.gpsimd
        def _(gpsimd):
            gpsimd.wait_ge(s_m9.h, s_m9.tgt(DIC + 1))
            s_bc9.inc(gpsimd.dma_start(BCD.ap(), bc_sb[0:2 * NS, :]), 16)
            s_bc9.inc(gpsimd.dma_start(BCD16.ap(), bc_sb[0:2 * NS, :]), 16)

    drain()

    dlt_stg = [big(f"p5d_{i}", [128, T], F32, XNO + 2 * T * 2 + i * T * 4)
               for i in range(2)]
    sg_stg = big("p5sg", [128, T], F32, XNO + 2 * T * 2 + 2 * T * 4)
    s_m10 = pool.get()
    s_e10 = pool.get()
    s_s10 = [pool.get_dma(), pool.get_dma()]
    with nc.Block() as blk:
        @blk.tensor
        def _(tensor):
            for g in range(DIC):
                if g >= 2:
                    tensor.wait_ge(s_e10.h, s_e10.tgt(g - 1))
                ps = pA if g % 2 == 0 else pB
                st = wdtt_sb[:, g * 128:(g + 1) * 128]
                last = None
                for b in range(TB):
                    last = tensor.matmul(ps[:, b * NB:(b + 1) * NB], st,
                                         dtr_sb[:, b * NB:(b + 1) * NB],
                                         start=True, stop=True)
                s_m10.inc(last)

        @blk.scalar
        def _(scalar):
            for g in range(DIC):
                scalar.wait_ge(s_m10.h, s_m10.tgt(g + 1))
                if g >= 2:
                    scalar.wait_ge(s_s10[g % 2].h,
                                   s_s10[g % 2].tgt((g - 2) // 2 * 16 + 16))
                ps = pA if g % 2 == 0 else pB
                vfence(scalar, s_e10)
                SCH.inc(scalar.activation(sg_stg[:, :], ps[:, :T],
                                          AF.Sigmoid, bias=db_sb[:, g:g + 1],
                                          scale=-1.0))
                vfence(scalar, SCH)
                s_e10.inc(scalar.activation(dlt_stg[g % 2][:, :],
                                            sg_stg[:, :], AF.Ln))

        @blk.sync
        def _(sync):
            for g in range(DIC):
                sync.wait_ge(s_e10.h, s_e10.tgt(g + 1))
                s_s10[g % 2].inc(
                    sync.dma_start(DLT.ap()[g], dlt_stg[g % 2][:, :]), 16)

    drain()

    assert DIC % 2 == 0
    TH1 = T // 4
    NU1 = 4 * HC
    CI0 = (HC + 1) // 2
    NU2 = TB * DMC
    NG = TB * DIC
    F1_END = (NU1 - 1) // 2 + 3
    def sf1(u):
        return min(u // 2, NG - 1)

    slot_f1 = [[] for _ in range(NG)]
    for u in range(NU1):
        slot_f1[sf1(u)].append(u)
    slot_f2 = [[] for _ in range(NG)]
    for v in range(NU2):
        slot_f2[min(F1_END + 2 + v // 2, NG - 1)].append(v)
    slot_w1 = [[] for _ in range(NG)]
    slot_w3 = [[] for _ in range(NG)]
    for ch in range(2, HC):
        slot_w1[max(sf1(4 * ch) - 1, 0)].append(ch)
        slot_w3[min(sf1(4 * ch) + 1, NG - 1)].append(ch)
    def sislot(u):
        return min(sf1(u) + 1, NG - 1)

    def h3slot(u):
        return min(sf1(u) + 2, NG - 1)

    def gmslot(u):
        return min(sf1(u) + 3, NG - 1)

    def evslot(v):
        return min(F1_END + 4 + v // 2, NG - 1)

    def f2slot(v):
        return min(F1_END + 2 + v // 2, NG - 1)

    slot_gload = [[] for _ in range(NG)]
    for tc in range(TB):
        slot_gload[f2slot(tc * DMC)].append(tc)
    slot_w2load = [[] for _ in range(NG)]
    for v in range(NU2):
        slot_w2load[f2slot(v)].append(v)

    slot_si = [[] for _ in range(NG)]
    for u in range(NU1):
        slot_si[sislot(u)].append(u)
    slot_h3 = [[] for _ in range(NG)]
    for u in range(NU1):
        slot_h3[h3slot(u)].append(u)
    slot_gm = [[] for _ in range(NG)]
    for u in range(NU1):
        slot_gm[gmslot(u)].append(u)
    slot_gst = slot_gm
    slot_ev2 = [[] for _ in range(NG)]
    for v in range(NU2):
        slot_ev2[evslot(v)].append(v)

    def gm_pre_(u):
        return sislot(u) < gmslot(u) and h3slot(u) < gmslot(u)

    def h3_pre(u):
        if u < 2:
            return True
        gs = gmslot(u - 2)
        return gs < h3slot(u) or (gs == h3slot(u) and gm_pre_(u - 2))

    def si_pre(u):
        if sf1(u) >= sislot(u):
            return False
        if u < 8:
            return True
        gs = gmslot(u - 8)
        return gs < sislot(u) or (gs == sislot(u) and gm_pre_(u - 8))

    def gm_pre(u):
        return gm_pre_(u)

    def ev_pre(v):
        return f2slot(v) < evslot(v)

    def h1_pre(u):
        if u < 2:
            return True
        s = sf1(u)
        return sislot(u - 2) < s or (sislot(u - 2) == s and si_pre(u - 2))

    F1B = XNO
    w1t_buf = [big(f"f1a_{i}", [128, DMC * 128], BF16, F1B + i * DMC * 256)
               for i in range(2)]
    w3t_buf = [big(f"f1b_{i}", [128, DMC * 128], BF16,
                   F1B + (2 + i) * DMC * 256) for i in range(2)]
    h1s = [big(f"f1h_{i}", [128, TH1], BF16,
               F1B + 4 * DMC * 256 + i * TH1 * 2) for i in range(8)]
    g_stg = [big(f"f1g_{i}", [128, TH1], BF16,
                 F1B + 4 * DMC * 256 + (8 + i) * TH1 * 2) for i in range(2)]
    F1SZ = 4 * DMC * 256 + 10 * TH1 * 2
    SRO = XNO + F1SZ
    Bb = big("p6bb", [128, NS * NB], BF16, SRO)
    Cb = big("p6cb", [128, NS * NB], BF16, SRO + NS * NB * 2)
    P6O = SRO + 2 * NS * NB * 2
    dlt_b = [big(f"p6dl_{i}", [128, NB], F32, P6O + i * NB * 4)
             for i in range(2)]
    xc_b = [big(f"p6xc_{i}", [128, NB], F32, P6O + (2 + i) * NB * 4)
            for i in range(2)]
    szg_b = [big(f"p6sz_{i}", [128, NB], F32, P6O + (4 + i) * NB * 4)
             for i in range(2)]
    t2_b = [big(f"p6t2_{i}", [128, NB], F32, P6O + (6 + i) * NB * 4)
            for i in range(2)]
    e_b = [big(f"p6e_{i}", [128, NB], F32, P6O + (8 + i) * NB * 4)
           for i in range(3)]
    s_bK = [big(f"p6s_{i}", [128, NB], BF16, P6O + 11 * NB * 4 + i * NB * 2)
            for i in range(3)]
    dtu_b = [big(f"p6du_{i}", [128, NB], BF16,
                 P6O + 11 * NB * 4 + (3 + i) * NB * 2) for i in range(2)]
    dbu_b = [big(f"p6db_{i}", [128, NB], BF16,
                 P6O + 11 * NB * 4 + (5 + i) * NB * 2) for i in range(2)]
    cs_b = [big(f"p6cs_{i}", [128, NB], BF16,
                P6O + 11 * NB * 4 + (7 + i) * NB * 2)
            for i in range(8)]
    yg_stg = [big(f"p6yg_{i}", [128, NB], BF16,
                  P6O + 11 * NB * 4 + (15 + i) * NB * 2)
              for i in range(2)]
    SEND = P6O + 11 * NB * 4 + 17 * NB * 2
    F2SZ = HC * NB * 2 + 2 * CI0 * 128 * 2 + 2 * NB * 4
    F2B = 0 if F2SZ <= XNO else SEND
    g_res = big("f2g", [128, HC * NB], BF16, F2B)
    w2t_buf = [big(f"f2w_{i}", [128, CI0 * 128], BF16,
                   F2B + HC * NB * 2 + i * CI0 * 128 * 2) for i in range(2)]
    f_stg = [big(f"f2f_{i}", [128, NB], F32,
                 F2B + HC * NB * 2 + 2 * CI0 * 128 * 2 + i * NB * 4)
             for i in range(2)]

    s_bb = [pool.get_dma() for _ in range(4)]
    s_bc = [pool.get_dma() for _ in range(4)]
    s_ldq = [pool.get_dma(), pool.get_dma()]
    s_ee = pool.get()
    s_sc = pool.get()
    s_cs = pool.get()
    s_im = pool.get()
    s_gc = pool.get()
    s_gt = pool.get()
    s_sy = [pool.get_dma(), pool.get_dma()]
    s_fw = [pool.get_dma(), pool.get_dma()]
    s_fw3 = [pool.get_dma(), pool.get_dma()]
    s_m7 = pool.get()
    s_h3 = pool.get()
    s_a7 = pool.get()
    s_g7 = pool.get()
    s_gs = [pool.get_dma(), pool.get_dma()]
    s_fg = pool.get_dma()
    s_f2w = [pool.get_dma(), pool.get_dma()]
    s_f2m = pool.get()
    s_f2e = pool.get()
    s_ff = [pool.get_dma(), pool.get_dma()]

    def h_blocks(W):
        return [(s, min(NB, W - s)) for s in range(0, W, NB)]

    H3O = max(TH1, NB)

    NCH6 = TB
    with nc.Block() as blk:
        @blk.gpsimd
        def _(gpsimd):
            for cc in range(NCH6):
                if cc >= 1:
                    gpsimd.wait_ge(s_cs.h, s_cs.tgt(cc * DIC * NS))
                for n in range(NS):
                    s_bb[n * 4 // NS].inc(gpsimd.dma_start(
                        Bb[:, n * NB:(n + 1) * NB],
                        BCD.ap()[n:n + 1,
                                 cc * NB:(cc + 1) * NB].broadcast_to(
                            [128, NB])), 16)
                for g in range(DIC):
                    gi = cc * DIC + g

                    def gs_store(u):
                        ch, qt = u // 4, u % 4
                        gpsimd.wait_ge(s_g7.h, s_g7.tgt(u + 1))
                        gsv = GS.ap()[ch].rearrange("q p t -> p q t")
                        if TB % 4 == 0:
                            nq = TB // 4
                            dst = gsv[:, qt * nq:(qt + 1) * nq]
                        else:
                            nq = 1
                            dst = gsv[:, 0:1, qt * TH1:(qt + 1) * TH1]
                        s_gs[u % 2].inc(gpsimd.dma_start(
                            dst,
                            g_stg[u % 2][:, :].rearrange(
                                "p (q t) -> p q t", q=nq)), 16)

                    def ff_store(v):
                        tc, c2 = v // DMC, v % DMC
                        gpsimd.wait_ge(s_f2e.h, s_f2e.tgt(v + 1))
                        s_ff[v % 2].inc(gpsimd.dma_start(FF.ap()[c2, tc],
                                                         f_stg[v % 2][:, :]),
                                        16)

                    for u in slot_gst[gi]:
                        if gm_pre(u):
                            gs_store(u)
                    for v in slot_ev2[gi]:
                        if ev_pre(v):
                            ff_store(v)
                    for n in range(NS):
                        k = gi * NS + n
                        gpsimd.wait_ge(s_sc.h, s_sc.tgt(k + 1))
                        s_gc.inc(gpsimd.tensor_copy(
                            state[:, g * NS + n:g * NS + n + 1],
                            s_bK[k % 3][:, NB - 1:NB]))
                    for u in slot_gst[gi]:
                        if not gm_pre(u):
                            gs_store(u)
                    for v in slot_ev2[gi]:
                        if not ev_pre(v):
                            ff_store(v)

        @blk.sync
        def _(sync):
            def store_yg(pg):
                sync.wait_ge(s_gt.h, s_gt.tgt(pg + 1))
                s_sy[pg % 2].inc(sync.dma_start(
                    YG.ap()[pg % DIC, :,
                            (pg // DIC) * NB:(pg // DIC + 1) * NB],
                    yg_stg[pg % 2][:, :]), 16)

            for ch in range(min(2, HC)):
                s_fw[ch % 2].inc(sync.dma_start(w1t_buf[ch % 2][:, :],
                                                W1T.ap()[ch]), 16)
                s_fw3[ch % 2].inc(sync.dma_start(w3t_buf[ch % 2][:, :],
                                                 W3T.ap()[ch]), 16)
            for cc in range(NCH6):
                for g in range(DIC):
                    gi = cc * DIC + g
                    if gi >= 2:
                        sync.wait_ge(s_gt.h, s_gt.tgt(gi - 1))
                    s_ldq[g % 2].inc(sync.dma_start(
                        dlt_b[g % 2][:, :],
                        DLT.ap()[g, :, cc * NB:(cc + 1) * NB]), 16)
                    s_ldq[g % 2].inc(sync.dma_start(
                        xc_b[g % 2][:, :],
                        XC.ap()[g, :, cc * NB:(cc + 1) * NB]), 16)
                    s_ldq[g % 2].inc(sync.dma_start(
                        szg_b[g % 2][:, :],
                        SZG.ap()[g, :, cc * NB:(cc + 1) * NB]), 16)
                    if gi >= 1:
                        store_yg(gi - 1)
                    for ch in slot_w1[gi]:
                        sync.wait_ge(s_m7.h, s_m7.tgt(4 * ch - 4))
                        s_fw[ch % 2].inc(sync.dma_start(
                            w1t_buf[ch % 2][:, :], W1T.ap()[ch]), 16)
                    for ch in slot_w3[gi]:
                        sync.wait_ge(s_h3.h, s_h3.tgt(4 * ch - 4))
                        s_fw3[ch % 2].inc(sync.dma_start(
                            w3t_buf[ch % 2][:, :], W3T.ap()[ch]), 16)
                    for tc in slot_gload[gi]:
                        if tc == 0:
                            sync.wait_ge(s_m7.h, s_m7.tgt(NU1))
                            sync.wait_ge(s_h3.h, s_h3.tgt(NU1))
                            sync.wait_ge(s_gs[0].h,
                                         s_gs[0].tgt(((NU1 + 1) // 2) * 16))
                            sync.wait_ge(s_gs[1].h,
                                         s_gs[1].tgt((NU1 // 2) * 16))
                        else:
                            sync.wait_ge(s_f2m.h, s_f2m.tgt(2 * tc * DMC))
                        for ci in range(HC):
                            s_fg.inc(sync.dma_start(
                                g_res[:, ci * NB:(ci + 1) * NB],
                                GS.ap()[ci, tc]), 16)
                    for v in slot_w2load[gi]:
                        tc, c2 = v // DMC, v % DMC
                        for p in range(2):
                            q = 2 * v + p
                            nci = CI0 if p == 0 else HC - CI0
                            if q < 2:
                                sync.wait_ge(s_m7.h, s_m7.tgt(NU1))
                                sync.wait_ge(s_h3.h, s_h3.tgt(NU1))
                            else:
                                sync.wait_ge(s_f2m.h, s_f2m.tgt(q - 1))
                            s_f2w[p].inc(sync.dma_start(
                                w2t_buf[p][:, :nci * 128],
                                W2T.ap()[c2, :,
                                         p * CI0 * 128:
                                         (p * CI0 + nci) * 128]), 16)
            store_yg(NG - 1)

        @blk.scalar
        def _(scalar):
            def silu(u):
                scalar.wait_ge(s_m7.h, s_m7.tgt(u + 1))
                if u >= 8:
                    scalar.wait_ge(s_g7.h, s_g7.tgt(u - 7))
                h1o = (u % 2) * NB
                s_a7.inc(scalar.activation(h1s[u % 8][:, :],
                                           pB[:, h1o:h1o + TH1], AF.Silu,
                                           scale=s1_col[:, 0:1]))

            def f2evac(v):
                scalar.wait_ge(s_f2m.h, s_f2m.tgt(2 * v + 2))
                if v >= 2:
                    scalar.wait_ge(s_ff[v % 2].h,
                                   s_ff[v % 2].tgt((v // 2) * 16))
                psl = (v % 4) * NB
                s_f2e.inc(scalar.activation(f_stg[v % 2][:, :],
                                            pB[:, psl:psl + NB], AF.Copy,
                                            scale=s23_col[:, 0:1]))

            for cc in range(NCH6):
                if cc >= 1:
                    scalar.wait_ge(s_cs.h, s_cs.tgt(cc * DIC * NS))
                for n in range(NS):
                    s_bc[n * 4 // NS].inc(scalar.dma_start(
                        Cb[:, n * NB:(n + 1) * NB],
                        BCD16.ap()[NS + n:NS + n + 1,
                                   cc * NB:(cc + 1) * NB].broadcast_to(
                            [128, NB])), 16)
                for g in range(DIC):
                    gi = cc * DIC + g
                    for u in slot_si[gi]:
                        if si_pre(u):
                            silu(u)
                    for v in slot_ev2[gi]:
                        if ev_pre(v):
                            f2evac(v)
                    scalar.wait_ge(s_ldq[g % 2].h,
                                   s_ldq[g % 2].tgt((gi // 2 + 1) * 48))
                    for n in range(NS):
                        k = gi * NS + n
                        if k >= 3:
                            scalar.wait_ge(s_sc.h, s_sc.tgt(k - 2))
                        s_ee.inc(scalar.activation(
                            e_b[k % 3][:, :], dlt_b[g % 2][:, :], AF.Exp,
                            scale=a_sb[:, g * NS + n:g * NS + n + 1]))
                    for u in slot_si[gi]:
                        if not si_pre(u):
                            silu(u)
                    for v in slot_ev2[gi]:
                        if not ev_pre(v):
                            f2evac(v)

        @blk.vector
        def _(vector):
            def h3reg(u):
                off = 2 * NB + (u % 2) * NB
                return pB[:, off:off + TH1]

            def gmul(u):
                vector.wait_ge(s_a7.h, s_a7.tgt(u + 1))
                vector.wait_ge(s_h3.h, s_h3.tgt(u + 1))
                if u >= 2:
                    vector.wait_ge(s_gs[u % 2].h,
                                   s_gs[u % 2].tgt((u // 2) * 16))
                s_g7.inc(vector.tensor_mul(g_stg[u % 2][:, :],
                                           h1s[u % 8][:, :], h3reg(u)))

            for cc in range(NCH6):
                for g in range(DIC):
                    gi = cc * DIC + g
                    for u in slot_gm[gi]:
                        if gm_pre(u):
                            gmul(u)
                    vector.wait_ge(s_ldq[g % 2].h,
                                   s_ldq[g % 2].tgt((gi // 2 + 1) * 48))

                    VCH.inc(vector.tensor_mul(dtu_b[g % 2][:, :],
                                              dlt_b[g % 2][:, :],
                                              xc_b[g % 2][:, :]))
                    vfence(vector, VCH)
                    for n in range(NS):
                        k = gi * NS + n
                        if g == 0 and n % (NS // 4) == 0:
                            q4 = n * 4 // NS
                            vector.wait_ge(
                                s_bb[q4].h,
                                s_bb[q4].tgt((cc + 1) * (NS // 4) * 16))
                        VCH.inc(vector.tensor_mul(dbu_b[k % 2][:, :],
                                                  dtu_b[g % 2][:, :],
                                                  Bb[:, n * NB:(n + 1) * NB]))
                        vector.wait_ge(s_ee.h, s_ee.tgt(k + 1))
                        if k >= 3:
                            vector.wait_ge(s_gc.h, s_gc.tgt(k - 2))
                        if cc >= 1:
                            vector.wait_ge(
                                s_gc.h,
                                s_gc.tgt((cc - 1) * DIC * NS + g * NS + n + 1))
                        vfence(vector, VCH)
                        s_sc.inc(vector.tensor_tensor_scan(
                            s_bK[k % 3][:, :], e_b[k % 3][:, :],
                            dbu_b[k % 2][:, :],
                            (state[:, g * NS + n:g * NS + n + 1]
                             if cc >= 1 else 0.0),
                            ALU.mult, ALU.add))
                        if k >= 8:
                            vector.wait_ge(s_im.h, s_im.tgt(k - 7))
                        if g == 0 and n % (NS // 4) == 0:
                            q4 = n * 4 // NS
                            vector.wait_ge(
                                s_bc[q4].h,
                                s_bc[q4].tgt((cc + 1) * (NS // 4) * 16))
                        vfence(vector, s_sc)
                        s_cs.inc(vector.tensor_mul(
                            cs_b[k % 8][:, :], s_bK[k % 3][:, :],
                            Cb[:, n * NB:(n + 1) * NB]))
                    vector.wait_ge(s_im.h, s_im.tgt((gi + 1) * NS))
                    vector.wait_ge(s_ldq[g % 2].h,
                                   s_ldq[g % 2].tgt((gi // 2 + 1) * 48))
                    ysl = (gi % 2) * NB
                    VCH.inc(vector.scalar_tensor_tensor(
                        t2_b[gi % 2][:, :], xc_b[g % 2][:, :],
                        d_sb[:, g:g + 1], pA[:, ysl:ysl + NB],
                        ALU.mult, ALU.add))
                    if gi >= 2:
                        vector.wait_ge(s_sy[gi % 2].h,
                                       s_sy[gi % 2].tgt((gi - 2) // 2 * 16
                                                        + 16))
                    vfence(vector, VCH)
                    s_gt.inc(vector.tensor_mul(yg_stg[gi % 2][:, :],
                                               t2_b[gi % 2][:, :],
                                               szg_b[g % 2][:, :]))
                    for u in slot_gm[gi]:
                        if not gm_pre(u):
                            gmul(u)

        @blk.tensor
        def _(tensor):
            xnr = xn_f[:, :].rearrange("p (j t) -> p j t", t=TPAD)

            def h3steps(u):
                ch, qt = u // 4, u % 4
                po = 2 * NB + (u % 2) * NB
                steps = []

                def mk(j, bs, bn, first):
                    def go():
                        if first:
                            tensor.wait_ge(
                                s_fw3[ch % 2].h,
                                s_fw3[ch % 2].tgt((ch // 2 + 1) * 16))
                            if u >= 2:
                                tensor.wait_ge(s_g7.h, s_g7.tgt(u - 1))
                        mm = tensor.matmul(
                            pB[:, po + bs:po + bs + bn],
                            w3t_buf[ch % 2][:, j * 128:(j + 1) * 128],
                            xnr[:, j, 3 + qt * TH1 + bs:
                                3 + qt * TH1 + bs + bn],
                            start=(j == 0), stop=(j == DMC - 1))
                        if j == DMC - 1 and bs == h_blocks(TH1)[-1][0]:
                            s_h3.inc(mm)
                    return go
                for j in range(DMC):
                    for (bs, bn) in h_blocks(TH1):
                        steps.append(mk(j, bs, bn, j == 0 and bs == 0))
                return steps

            def h3chain(u):
                for stp in h3steps(u):
                    stp()

            def h1steps(u):
                ch, qt = u // 4, u % 4
                po = (u % 2) * NB
                steps = []

                def mk(j, bs, bn, first):
                    def go():
                        if first:
                            tensor.wait_ge(
                                s_fw[ch % 2].h,
                                s_fw[ch % 2].tgt((ch // 2 + 1) * 16))
                            if u >= 2:
                                tensor.wait_ge(s_a7.h, s_a7.tgt(u - 1))
                        mm = tensor.matmul(
                            pB[:, po + bs:po + bs + bn],
                            w1t_buf[ch % 2][:, j * 128:(j + 1) * 128],
                            xnr[:, j, 3 + qt * TH1 + bs:
                                3 + qt * TH1 + bs + bn],
                            start=(j == 0), stop=(j == DMC - 1))
                        if j == DMC - 1 and bs == h_blocks(TH1)[-1][0]:
                            s_m7.inc(mm)
                    return go
                for j in range(DMC):
                    for (bs, bn) in h_blocks(TH1):
                        steps.append(mk(j, bs, bn, j == 0 and bs == 0))
                return steps

            def h1chain(u):
                for stp in h1steps(u):
                    stp()

            def f2steps(v):
                tc, c2 = v // DMC, v % DMC
                psl = (v % 4) * NB
                steps = []

                def mk(p, ci, first):
                    def go():
                        if first and p == 0:
                            if v == 0:
                                tensor.wait_ge(s_a7.h, s_a7.tgt(NU1))
                                tensor.wait_ge(s_g7.h, s_g7.tgt(NU1))
                            if v >= 4:
                                tensor.wait_ge(s_f2e.h, s_f2e.tgt(v - 3))
                            tensor.wait_ge(s_fg.h,
                                           s_fg.tgt((tc + 1) * HC * 16))
                        if first:
                            tensor.wait_ge(s_f2w[p].h,
                                           s_f2w[p].tgt((v + 1) * 16))
                        ci_lo = 0 if p == 0 else CI0
                        mm = tensor.matmul(
                            pB[:, psl:psl + NB],
                            w2t_buf[p][:, (ci - ci_lo) * 128:
                                       (ci - ci_lo + 1) * 128],
                            g_res[:, ci * NB:(ci + 1) * NB],
                            start=(ci == 0), stop=(ci == HC - 1))
                        if ci == (CI0 - 1 if p == 0 else HC - 1):
                            s_f2m.inc(mm)
                    return go
                for p in range(2):
                    ci_lo = 0 if p == 0 else CI0
                    ci_hi = CI0 if p == 0 else HC
                    for ci in range(ci_lo, ci_hi):
                        steps.append(mk(p, ci, ci == ci_lo))
                return steps

            for cc in range(NCH6):
                for g in range(DIC):
                    gi = cc * DIC + g
                    ysl = (gi % 2) * NB
                    qs = []
                    for u in slot_f1[gi]:
                        if h1_pre(u):
                            qs += h1steps(u)
                    for u in slot_h3[gi]:
                        if h3_pre(u):
                            qs += h3steps(u)
                    for v in slot_f2[gi]:
                        if f2slot(v) < NG - 1:
                            qs += f2steps(v)
                    qp = 0
                    rr = -(-len(qs) // NS) if qs else 0
                    if gi >= 2:
                        tensor.wait_ge(s_gt.h, s_gt.tgt(gi - 1))
                    for n in range(NS):
                        k = gi * NS + n
                        for _ in range(rr):
                            if qp < len(qs):
                                qs[qp]()
                                qp += 1
                        tensor.wait_ge(s_cs.h, s_cs.tgt(k + 1))
                        s_im.inc(tensor.matmul(
                            pA[:, ysl:ysl + NB], ident_ng[:, :],
                            cs_b[k % 8][:, :],
                            start=(n == 0), stop=(n == NS - 1)))
                    while qp < len(qs):
                        qs[qp]()
                        qp += 1
                    for u in slot_f1[gi]:
                        if not h1_pre(u):
                            h1chain(u)
                    for u in slot_h3[gi]:
                        if not h3_pre(u):
                            h3chain(u)
                    for v in slot_f2[gi]:
                        if not (f2slot(v) < NG - 1):
                            for stp in f2steps(v):
                                stp()

    drain()

    TH = T // 2
    thb = [(s, min(NB, TH - s)) for s in range(0, TH, NB)]
    yg_sb = big("p7yg", [128, DIC * TH], BF16, 0)
    P7O = DIC * TH * 2
    hwq_buf = [big(f"p7hw_{i}", [128, DIC * 128], BF16,
                   P7O + i * DIC * 128 * 2) for i in range(2)]
    P7O2 = P7O + 2 * DIC * 128 * 2
    f_tile = [big(f"p7f_{i}", [128, TH], F32, P7O2 + i * TH * 4)
              for i in range(2)]
    xt_tile = [big(f"p7x_{i}", [128, TH], F32, P7O2 + (2 + i) * TH * 4)
               for i in range(2)]
    o_stg = [big(f"p7o_{i}", [128, TH], F32, P7O2 + (4 + i) * TH * 4)
             for i in range(2)]

    s_ly12 = pool.get_dma()
    s_lh12 = [pool.get_dma(), pool.get_dma()]
    s_lf12 = [pool.get_dma(), pool.get_dma()]
    s_mm12 = pool.get()
    s_ev12 = pool.get()
    s_fin = [pool.get_dma(), pool.get_dma()]
    for hi in range(2):
        t0 = hi * TH
        b_ly = s_ly12.loc
        b_lh = [s.loc for s in s_lh12]
        b_lf = [s.loc for s in s_lf12]
        b_mm = s_mm12.loc
        b_ev = s_ev12.loc
        b_fin = [s.loc for s in s_fin]
        if hi == 1:
            drain()
        with nc.Block() as blk:
            @blk.sync
            def _(sync):
                for g in range(DIC):
                    s_ly12.inc(sync.dma_start(yg_sb[:, g * TH:(g + 1) * TH],
                                              YG.ap()[g, :, t0:t0 + TH]), 16)
                for cc in range(DMC):
                    if cc >= 2:
                        sync.wait_ge(s_mm12.h, s_mm12.tgt(b_mm + cc - 1))
                    s_lh12[cc % 2].inc(sync.dma_start(
                        hwq_buf[cc % 2][:, :].rearrange("p (j m) -> p j m",
                                                        m=128),
                        HWQ.ap()[cc].rearrange("j p m -> p j m")), 16)
                    if cc >= 2:
                        sync.wait_ge(s_ev12.h, s_ev12.tgt(b_ev + cc - 1))
                    ffv = FF.ap()[cc].rearrange("q p t -> p q t")
                    if TB % 2 == 0:
                        fsrc = ffv[:, hi * (TB // 2):(hi + 1) * (TB // 2)]
                        fq = TB // 2
                    else:
                        fsrc = ffv[:, 0:1, t0:t0 + TH]
                        fq = 1
                    s_lf12[cc % 2].inc(
                        sync.dma_start(f_tile[cc % 2][:, :].rearrange(
                            "p (q t) -> p q t", q=fq), fsrc), 16)
                    s_lf12[cc % 2].inc(sync.dma_start(
                        xt_tile[cc % 2][:, :],
                        XT.ap()[cc * 128:(cc + 1) * 128, t0:t0 + TH]), 16)

            @blk.tensor
            def _(tensor):
                tensor.wait_ge(s_ly12.h, s_ly12.tgt(b_ly + DIC * 16))
                for cc in range(DMC):
                    tensor.wait_ge(s_lh12[cc % 2].h,
                                   s_lh12[cc % 2].tgt(b_lh[cc % 2]
                                                      + (cc // 2 + 1) * 16))
                    if cc >= 2:
                        tensor.wait_ge(s_ev12.h, s_ev12.tgt(b_ev + cc - 1))
                    ps = pA if cc % 2 == 0 else pB
                    last = None
                    for j in range(DIC):
                        st = hwq_buf[cc % 2][:, j * 128:(j + 1) * 128]
                        for (bs, bn) in thb:
                            last = tensor.matmul(
                                ps[:, bs:bs + bn], st,
                                yg_sb[:, j * TH + bs:j * TH + bs + bn],
                                start=(j == 0), stop=(j == DIC - 1))
                    s_mm12.inc(last)

            @blk.vector
            def _(vector):
                for cc in range(DMC):
                    vector.wait_ge(s_mm12.h, s_mm12.tgt(b_mm + cc + 1))
                    vector.wait_ge(s_lf12[cc % 2].h,
                                   s_lf12[cc % 2].tgt(b_lf[cc % 2]
                                                      + (cc // 2 + 1) * 32))
                    if cc >= 2:
                        vector.wait_ge(s_fin[cc % 2].h,
                                       s_fin[cc % 2].tgt(b_fin[cc % 2]
                                                         + (cc - 2) // 2 * 16
                                                         + 16))
                    ps = pA if cc % 2 == 0 else pB
                    VCH.inc(vector.tensor_add(o_stg[cc % 2][:, :], ps[:, :TH],
                                              f_tile[cc % 2][:, :]))
                    vfence(vector, VCH)
                    s_ev12.inc(vector.tensor_add(o_stg[cc % 2][:, :],
                                                 o_stg[cc % 2][:, :],
                                                 xt_tile[cc % 2][:, :]))

            @blk.gpsimd
            def _(gpsimd):
                for cc in range(DMC):
                    gpsimd.wait_ge(s_ev12.h, s_ev12.tgt(b_ev + cc + 1))
                    s_fin[cc % 2].inc(gpsimd.dma_start(
                        OUT.ap()[cc * 128:(cc + 1) * 128, t0:t0 + TH],
                        o_stg[cc % 2][:, :]), 16)

    drain()

    nc.finalize()
    return nc


def _hadamard_cb(DI):
    H = np.array([[1.0]], dtype=np.float64)
    while H.shape[0] < DI:
        H = np.block([[H, H], [H, -H]])
    H = (H / math.sqrt(DI)).astype(ml_dtypes.bfloat16)
    DIC = DI // 128
    Hcb = np.empty((DIC, DI, 128), dtype=ml_dtypes.bfloat16)
    for cidx in range(DIC):
        Hcb[cidx] = np.ascontiguousarray(H[cidx * 128:(cidx + 1) * 128, :].T)
    return Hcb


def make_core_inputs(c, x_full, W, b, h):
    cd = derived(c)
    T, TPAD, DM, DI, HIDP, HID = (cd["T"], cd["TPAD"], cd["DM"], cd["DI"],
                                  cd["HIDP"], cd["HID"])
    xb = np.asarray(x_full[b], np.float32)
    xpad = np.zeros((TPAD, DM), dtype=np.float32)
    t0 = h * T
    if t0 >= 3:
        xpad[0:3] = xb[t0 - 3:t0]
    xpad[3:3 + T] = xb[t0:t0 + T]
    xt = np.ascontiguousarray(xb[t0:t0 + T].T)

    w1p = np.zeros((DM, HIDP), dtype=np.float32)
    w1p[:, :HID] = np.ascontiguousarray(W["w1"].T)
    w3p = np.zeros((DM, HIDP), dtype=np.float32)
    w3p[:, :HID] = np.ascontiguousarray(W["w3"].T)
    w2p = np.zeros((HIDP, DM), dtype=np.float32)
    w2p[:HID] = np.ascontiguousarray(W["w2"].T)
    win_t = np.ascontiguousarray(W["W_in"].T)

    return {
        "XPAD": xpad, "XT": xt,
        "WIN": win_t, "CW": W["conv_w"], "CBI": W["conv_b"],
        "WX": W["W_x"], "WDT": W["W_dt"], "DTB": W["dt_bias"],
        "ALOG": W["A_log"], "DD": W["D"],
        "WOUT": np.ascontiguousarray(W["W_out"].T),
        "W1P": w1p, "W3P": w3p, "W2P": w2p,
        "HCB": _hadamard_cb(DI),
    }


_NC_CACHE = {}


def kernel(L0_attn_hidden_states, L0_attn_W_in, L0_attn_conv_w,
           L0_attn_conv_b, L0_attn_W_x, L0_attn_W_dt, L0_attn_dt_bias,
           L0_attn_A_log, L0_attn_D, L0_attn_W_out, L0_ffn_w1, L0_ffn_w3,
           L0_ffn_w2):
    c = cfg_full()
    cd = derived(c)
    T, DM = cd["T"], cd["DM"]
    x = np.asarray(L0_attn_hidden_states, dtype=np.float32)
    W = dict(W_in=np.asarray(L0_attn_W_in, np.float32),
             conv_w=np.asarray(L0_attn_conv_w, np.float32),
             conv_b=np.asarray(L0_attn_conv_b, np.float32),
             W_x=np.asarray(L0_attn_W_x, np.float32),
             W_dt=np.asarray(L0_attn_W_dt, np.float32),
             dt_bias=np.asarray(L0_attn_dt_bias, np.float32),
             A_log=np.asarray(L0_attn_A_log, np.float32),
             D=np.asarray(L0_attn_D, np.float32),
             W_out=np.asarray(L0_attn_W_out, np.float32),
             w1=np.asarray(L0_ffn_w1, np.float32),
             w3=np.asarray(L0_ffn_w3, np.float32),
             w2=np.asarray(L0_ffn_w2, np.float32))

    if "full" not in _NC_CACHE:
        _NC_CACHE["full"] = build(c)
    nc = _NC_CACHE["full"]

    in_maps = [make_core_inputs(c, x, W, core // 2, core % 2)
               for core in range(8)]
    res = run_bass_kernel_spmd(nc, in_maps, core_ids=list(range(8)))
    out = np.empty((4, 2 * T, DM), dtype=np.float32)
    for core in range(8):
        b, h = core // 2, core % 2
        out[b, h * T:(h + 1) * T, :] = np.ascontiguousarray(
            res.results[core]["OUT"]).T
    return out


# version 24
# speedup vs baseline: 1.0650x; 1.0650x over previous
import math
import numpy as np
import ml_dtypes
import concourse.bass as bass
import concourse.mybir as mybir
from concourse.bass_utils import run_bass_kernel_spmd

F32 = mybir.dt.float32
BF16 = mybir.dt.bfloat16
AF = mybir.ActivationFunctionType
ALU = mybir.AluOpType
AX = mybir.AxisListType.X

NB = 512


def cfg_full():
    return dict(DM=2048, DI=4096, NS=16, RK=128, HID=5461, HIDP=5504, T=2048)


def cfg_small():
    return dict(DM=256, DI=512, NS=16, RK=128, HID=330, HIDP=384, T=512)


def derived(c):
    d = dict(c)
    d["TP"] = c["T"] + 3
    d["TPAD"] = ((d["TP"] + 127) // 128) * 128
    d["DMC"] = c["DM"] // 128
    d["DIC"] = c["DI"] // 128
    d["HC"] = c["HIDP"] // 128
    d["TB"] = c["T"] // NB
    d["EC"] = 2 * d["DIC"]
    d["HCH"] = (d["HC"] + 1) // 2
    d["EWX"] = c["RK"] + 2 * c["NS"]
    assert c["T"] % NB == 0 and c["DM"] % 128 == 0 and c["DI"] % 128 == 0
    assert c["HIDP"] % 128 == 0 and c["RK"] == 128 and 2 * c["NS"] <= 128
    return d


class _SemPool:
    def __init__(self, nc, n):
        self.nc = nc
        self.n_eng = n
        self.handles = [nc.alloc_semaphore(f"sp{i}") for i in range(n)]
        self.counts = [0] * n
        self.i = 0
        self.ndma = 0
        self.pending = set()

    def get(self):
        idx = self.i % self.n_eng
        self.i += 1
        return _PSem(self, idx)

    def get_dma(self):
        self.handles.append(self.nc.alloc_semaphore(f"dm{self.ndma}"))
        self.counts.append(0)
        self.ndma += 1
        return _PSem(self, len(self.handles) - 1)


class _PSem:
    def __init__(self, pool, idx):
        self.pool = pool
        self.idx = idx
        self.h = pool.handles[idx]
        self.base = pool.counts[idx]
        self.loc = 0

    def inc(self, instr, k=1):
        instr.then_inc(self.h, k)
        self.loc += k
        self.pool.counts[self.idx] = self.base + self.loc
        if k >= 16:
            self.pool.pending.add(self.idx)

    def tgt(self, local):
        return self.base + local


def build(c):
    c = derived(c)
    DM, DI, NS, RK, HID, HIDP = (c["DM"], c["DI"], c["NS"], c["RK"], c["HID"],
                                 c["HIDP"])
    T, TP, TPAD = c["T"], c["TP"], c["TPAD"]
    DMC, DIC, HC, TB, EC, HCH, EWX = (c["DMC"], c["DIC"], c["HC"], c["TB"],
                                      c["EC"], c["HCH"], c["EWX"])

    nc = bass.Bass("TRN2", target_bir_lowering=False)
    pool = _SemPool(nc, 32)

    XPAD = nc.dram_tensor("XPAD", [TPAD, DM], F32, kind="ExternalInput")
    XT = nc.dram_tensor("XT", [DM, T], F32, kind="ExternalInput")
    WIN = nc.dram_tensor("WIN", [DM, 2 * DI], F32, kind="ExternalInput")
    CW = nc.dram_tensor("CW", [DI, 4], F32, kind="ExternalInput")
    CBI = nc.dram_tensor("CBI", [DI], F32, kind="ExternalInput")
    WX = nc.dram_tensor("WX", [EWX, DI], F32, kind="ExternalInput")
    WDT = nc.dram_tensor("WDT", [DI, RK], F32, kind="ExternalInput")
    DTB = nc.dram_tensor("DTB", [DI], F32, kind="ExternalInput")
    ALOG = nc.dram_tensor("ALOG", [DI, NS], F32, kind="ExternalInput")
    DD = nc.dram_tensor("DD", [DI], F32, kind="ExternalInput")
    WOUT = nc.dram_tensor("WOUT", [DI, DM], F32, kind="ExternalInput")
    W1P = nc.dram_tensor("W1P", [DM, HIDP], F32, kind="ExternalInput")
    W3P = nc.dram_tensor("W3P", [DM, HIDP], F32, kind="ExternalInput")
    W2P = nc.dram_tensor("W2P", [HIDP, DM], F32, kind="ExternalInput")
    HCB = nc.dram_tensor("HCB", [DIC, DI, 128], BF16, kind="ExternalInput")
    OUT = nc.dram_tensor("OUT", [DM, T], F32, kind="ExternalOutput")

    XC = nc.dram_tensor("XCs", [DIC, 128, T], F32)
    XCB = nc.dram_tensor("XCBs", [DIC, 128, T], BF16)
    SZG = nc.dram_tensor("SZGs", [DIC, 128, T], F32)
    DLT = nc.dram_tensor("DLTs", [DIC, 128, T], F32)
    YG = nc.dram_tensor("YGs", [DIC, 128, T], BF16)
    GS = nc.dram_tensor("GSs", [HC, TB, 128, T // TB], BF16)
    FF = nc.dram_tensor("FFs", [DMC, TB, 128, T // TB], F32)
    W1T = nc.dram_tensor("W1Ts", [HC, 128, DMC * 128], BF16)
    W3T = nc.dram_tensor("W3Ts", [HC, 128, DMC * 128], BF16)
    W2T = nc.dram_tensor("W2Ts", [DMC, 128, HC * 128], BF16)
    HWQ = nc.dram_tensor("HWQs", [DMC, DIC, 128, 128], BF16)
    BCD = nc.dram_tensor("BCDs", [2 * NS, T], F32)
    BCD16 = nc.dram_tensor("BCD16s", [2 * NS, T], BF16)

    ident_bf_d = nc.inline_tensor(np.eye(128, dtype=ml_dtypes.bfloat16),
                                  "identbf")
    ident_ng_d = nc.inline_tensor(-np.eye(128, dtype=ml_dtypes.bfloat16),
                                  "identng")
    ident_f32_d = nc.inline_tensor(np.eye(128, dtype=np.float32), "identf32")
    ones_col_d = nc.inline_tensor(np.ones((128, 1), dtype=np.float32),
                                  "onescol")
    ones_row_d = nc.inline_tensor(np.ones((1, 128), dtype=np.float32),
                                  "onesrow")

    BASE = 16512
    TOPB = 229344
    off = [BASE]

    def nbytes_of(shape, dtype):
        return int(np.prod(shape[1:])) * (2 if dtype == BF16 else 4)

    def salloc(name, shape, dtype):
        at = ((off[0] + 31) // 32) * 32
        t = nc.alloc_sbuf_tensor_at(name, shape, dtype, offset=at,
                                    align_bytes=32)
        off[0] = at + nbytes_of(shape, dtype)
        return t

    ident_bf = salloc("identbf_sb", [128, 128], BF16)
    ident_ng = salloc("identng_sb", [128, 128], BF16)
    ident_f32 = salloc("identf32_sb", [128, 128], F32)
    a_sb = salloc("a_sb", [128, DIC * NS], F32)
    state = salloc("state_sb", [128, DIC * NS], F32)
    bc_sb = salloc("bc_sb", [32, T], F32)
    cw_sb = salloc("cw_sb", [128, DIC * 4], F32)
    cb_sb = salloc("cb_sb", [128, DIC], F32)
    db_sb = salloc("db_sb", [128, DIC], F32)
    d_sb = salloc("d_sb", [128, DIC], F32)
    dtr_sb = salloc("dtr_sb", [128, T], BF16)
    wxt_sb = salloc("wxt_sb", [128, DIC * EWX], BF16)
    wdtt_sb = salloc("wdtt_sb", [128, DI], BF16)
    xhalo_sb = salloc("xhalo_sb", [128, DIC * 3], F32)
    ones_col = salloc("ones_col_sb", [128, 1], F32)
    ones_row = salloc("ones_row_sb", [1, 128], F32)
    s1_col = salloc("s1_col", [128, 1], F32)
    s23_col = salloc("s23_col", [128, 1], F32)
    r1_col = salloc("r1_col", [128, 1], F32)
    r3_col = salloc("r3_col", [128, 1], F32)
    r2_col = salloc("r2_col", [128, 1], F32)
    sc_tmp = salloc("sc_tmp", [1, 8], F32)
    msums = salloc("msums", [128, 2 * HC + 4 * DMC + 8], F32)
    ln_cols = salloc("ln_cols", [128, 12], F32)
    BIG = ((off[0] + 31) // 32) * 32

    pA = nc.alloc_psum_tensor("pA", [128, 4 * NB], F32)
    pB = nc.alloc_psum_tensor("pB", [128, 4 * NB], F32)

    def big(name, shape, dtype, at):
        at = BIG + at
        assert at % 32 == 0, (name, at)
        t = nc.alloc_sbuf_tensor_at(name, shape, dtype, offset=at,
                                    align_bytes=32)
        assert at + nbytes_of(shape, dtype) <= TOPB, (
            name, at + nbytes_of(shape, dtype), TOPB)
        return t

    def bank(i):
        ps = pA if (i % 8) < 4 else pB
        return ps, ((i % 8) % 4) * NB

    def drain():
        if not pool.pending:
            return
        targets = [(pool.handles[i], pool.counts[i]) for i in pool.pending]
        pool.pending.clear()
        with nc.Block() as dblk:
            @dblk.sync
            def _(sync):
                for h, v in targets:
                    sync.wait_ge(h, v)

    class _Solo:
        def __init__(self, name):
            self.h = nc.alloc_semaphore(name)
            self.loc = 0

        def inc(self, instr, k=1):
            instr.then_inc(self.h, k)
            self.loc += k

        def tgt(self, local):
            return local

    VCH = _Solo("vch")
    SCH = _Solo("sch")

    def vfence(eng, s):
        eng.wait_ge(s.h, s.tgt(s.loc))

    s_c = pool.get_dma()
    with nc.Block() as blk:
        @blk.sync
        def _(sync):
            s_c.inc(sync.dma_start(ident_bf[:, :], ident_bf_d.ap()), 16)
            s_c.inc(sync.dma_start(ident_ng[:, :], ident_ng_d.ap()), 16)
            s_c.inc(sync.dma_start(ident_f32[:, :], ident_f32_d.ap()), 16)
            s_c.inc(sync.dma_start(ones_col[:, :], ones_col_d.ap()), 16)
            s_c.inc(sync.dma_start(ones_row[:, :], ones_row_d.ap()), 16)
            s_c.inc(sync.dma_start(
                a_sb[:, :].rearrange("p (g n) -> p g n", n=NS),
                ALOG.ap().rearrange("(g p) n -> p g n", p=128)), 16)
            s_c.inc(sync.dma_start(
                cw_sb[:, :].rearrange("p (g k) -> p g k", k=4),
                CW.ap().rearrange("(g p) k -> p g k", p=128)), 16)
            s_c.inc(sync.dma_start(
                cb_sb[:, :], CBI.ap().rearrange("(g p) -> p g", p=128),
                allow_slow_non_contiguous=True), 16)
            s_c.inc(sync.dma_start(
                db_sb[:, :], DTB.ap().rearrange("(g p) -> p g", p=128),
                allow_slow_non_contiguous=True), 16)
            s_c.inc(sync.dma_start(
                d_sb[:, :], DD.ap().rearrange("(g p) -> p g", p=128),
                allow_slow_non_contiguous=True), 16)

        @blk.scalar
        def _(scalar):
            scalar.wait_ge(s_c.h, s_c.tgt(10 * 16))
            scalar.activation(a_sb[:, :], a_sb[:, :], AF.Exp)

        @blk.gpsimd
        def _(gpsimd):
            gpsimd.memset(state[:, :], 0.0)

    with nc.Block() as blk:
        @blk.vector
        def _(vector):
            vector.tensor_scalar_mul(db_sb[:, :], db_sb[:, :], -1.0)

    drain()

    wot = big("wot", [128, DIC * DM], BF16, 0)
    WOTB = DIC * DM * 2

    s_l1 = pool.get_dma()
    with nc.Block() as blk:
        @blk.gpsimd
        def _(gpsimd):
            wv = WOUT.ap().rearrange("(j p) m -> p j m", p=128)
            wd = wot[:, :].rearrange("p (j m) -> p j m", m=DM)
            q1 = max(DIC // 4, 1)
            for k in range(0, DIC, q1):
                s_l1.inc(gpsimd.dma_start(wd[:, k:k + q1], wv[:, k:k + q1]),
                         16)

    drain()

    hcb_buf = [big(f"hcb_{i}", [128, DIC * 128], BF16,
                   WOTB + i * DIC * 128 * 2) for i in range(2)]
    hw_st = [big(f"hwst_{i}", [128, DM], BF16,
                 WOTB + 2 * DIC * 128 * 2 + i * DM * 2) for i in range(2)]

    s_l2 = [pool.get_dma(), pool.get_dma()]
    s_m2 = pool.get()
    s_e2 = pool.get()
    s_s2 = [pool.get_dma(), pool.get_dma()]
    dmb = [(s, min(NB, DM - s)) for s in range(0, DM, NB)]
    with nc.Block() as blk:
        @blk.sync
        def _(sync):
            for ci in range(DIC):
                if ci >= 2:
                    sync.wait_ge(s_m2.h, s_m2.tgt(ci - 1))
                s_l2[ci % 2].inc(sync.dma_start(
                    hcb_buf[ci % 2][:, :].rearrange("p (cc m) -> p cc m",
                                                    m=128),
                    HCB.ap()[ci].rearrange("(cc p) m -> p cc m", p=128)), 16)

        @blk.tensor
        def _(tensor):
            tensor.wait_ge(s_l1.h, s_l1.tgt(s_l1.loc))
            for ci in range(DIC):
                tensor.wait_ge(s_l2[ci % 2].h,
                               s_l2[ci % 2].tgt((ci // 2 + 1) * 16))
                if ci >= 2:
                    tensor.wait_ge(s_e2.h, s_e2.tgt(ci - 1))
                ps = pA if ci % 2 == 0 else pB
                last = None
                for cc in range(DIC):
                    st = hcb_buf[ci % 2][:, cc * 128:(cc + 1) * 128]
                    for (bs, bn) in dmb:
                        last = tensor.matmul(
                            ps[:, bs:bs + bn], st,
                            wot[:, cc * DM + bs:cc * DM + bs + bn],
                            start=(cc == 0), stop=(cc == DIC - 1))
                s_m2.inc(last)

        @blk.scalar
        def _(scalar):
            for ci in range(DIC):
                scalar.wait_ge(s_m2.h, s_m2.tgt(ci + 1))
                if ci >= 2:
                    scalar.wait_ge(s_s2[ci % 2].h,
                                   s_s2[ci % 2].tgt((ci - 2) // 2 * 16 + 16))
                ps = pA if ci % 2 == 0 else pB
                s_e2.inc(scalar.activation(hw_st[ci % 2][:, :], ps[:, :DM],
                                           AF.Copy))

        @blk.gpsimd
        def _(gpsimd):
            for ci in range(DIC):
                gpsimd.wait_ge(s_e2.h, s_e2.tgt(ci + 1))
                s_s2[ci % 2].inc(gpsimd.dma_start(
                    HWQ.ap()[:, ci].rearrange("cc p m -> p cc m"),
                    hw_st[ci % 2][:, :].rearrange("p (cc m) -> p cc m", m=128),
                ), 16)

    drain()

    xn_f = big("xn_f", [128, DMC * TPAD], BF16, 0)
    XNO = DMC * TPAD * 2
    QW = max(HIDP, DI, DIC * RK, DM)
    QFW = max(DM, 128 * ((HIDP // 128 + 2) // 3))
    QPW = max(QFW, DM)
    NQB = 4
    QF = [big(f"q_f32_{i}", [128, QFW], F32, XNO + i * QFW * 4)
          for i in range(NQB)]
    O1 = XNO + NQB * QFW * 4
    QPOS = [big(f"q_pos{i}", [128, QPW], F32, O1 + i * QPW * 4)
            for i in range(3)]
    TSW = max(HC * 128, DMC * 128, DIC * RK)
    TSTG = [big(f"tstg{i}", [128, TSW], BF16,
                O1 + 3 * QPW * 4 + i * ((TSW * 2 + 31) // 32) * 32)
            for i in range(2)]
    WXP1 = big("wxp1", [128, DI], F32, XNO)
    WXP2 = big("wxp2", [128, DI], F32, XNO + DI * 4)
    WDTH = big("wdth", [128, DIC * RK], F32,
               XNO + 2 * DI * 4 if 2 * DI * 4 + DIC * RK * 4 <= NQB * QFW * 4
               else O1)
    QEND = O1 + 3 * QPW * 4 + 2 * ((TSW * 2 + 31) // 32) * 32
    OVL = O1
    if OVL + 5 * DM * 4 > QEND:
        OVL = QEND
    x_tile = [big(f"x_t{i}", [128, DM], F32, OVL + i * DM * 4)
              for i in range(2)]
    xm_tile = big("xm_t", [128, DM], F32, OVL + 2 * DM * 4)
    xnt_tile = [big(f"xnt_{i}", [128, DM], F32, OVL + (3 + i) * DM * 4)
                for i in range(2)]

    NT = TPAD // 128
    s_l3 = [pool.get_dma(), pool.get_dma()]
    s_v3 = pool.get()
    s_a3 = pool.get()
    s_n3 = pool.get()
    s_m3 = pool.get()
    s_e3 = pool.get()
    egroups = [(j0, min(4, DMC - j0)) for j0 in range(0, DMC, 4)]
    NEV = len(egroups)
    B3_EV_TOTAL = NT * NEV

    qch = []
    for ch in range(HC):
        qch.append((W1P, W1T, ch, "1", r1_col))
    for ch in range(HC):
        qch.append((W3P, W3T, ch, "3", r3_col))
    for ch in range(HIDP // 128):
        qch.append((W2P, W2T, ch, "2", r2_col))
    NMEAN = len(qch)
    NSCHED = 2 * NMEAN
    FD = DM

    def qload_src(Ws, ch, kind):
        if kind == "2":
            return Ws.ap()[ch * 128:(ch + 1) * 128, :]
        v = Ws.ap().rearrange("(j p) h -> p j h", p=128)
        return v[:, :, ch * 128:(ch + 1) * 128]

    s_ql = [pool.get_dma() for _ in range(NQB)]
    s_qdone = pool.get()
    s_m4 = pool.get()
    s_v4 = pool.get()
    s_scl = pool.get()
    s_ts = [pool.get_dma(), pool.get_dma()]
    s_wx = pool.get_dma()
    s_wm = pool.get()
    s_we = pool.get()

    with nc.Block() as blk:
        @blk.sync
        def _(sync):
            for i in range(NT):
                if i >= 2:
                    sync.wait_ge(s_v3.h, s_v3.tgt((i - 1) * 2))
                s_l3[i % 2].inc(
                    sync.dma_start(x_tile[i % 2][:, :],
                                   XPAD.ap()[i * 128:(i + 1) * 128, :]), 16)
            for rep in range(2):
                for ci_, (Ws, Wd, ch, kind, rc) in enumerate(qch):
                    gidx = rep * NMEAN + ci_
                    if gidx >= NQB:
                        sync.wait_ge(s_qdone.h,
                                     s_qdone.tgt(gidx - NQB + 1))
                    dst = QF[gidx % NQB][:, :FD]
                    if kind != "2":
                        dst = dst.rearrange("p (j h) -> p j h", h=128)
                    s_ql[gidx % NQB].inc(
                        sync.dma_start(dst, qload_src(Ws, ch, kind)), 16)
            sync.wait_ge(s_qdone.h, s_qdone.tgt(NSCHED))
            s_wx.inc(sync.dma_start(WXP1[:, :], WX.ap()[0:128, :]), 16)
            s_wx.inc(sync.dma_start(WXP2[0:EWX - 128, :],
                                    WX.ap()[128:EWX, :]), 16)
            s_wx.inc(sync.dma_start(
                WDTH[:, :].rearrange("p (g r) -> p g r", r=RK),
                WDT.ap().rearrange("(g p) r -> p g r", p=128)), 16)

        @blk.vector
        def _(vector):
            for i in range(NT):
                p4 = (i % 2) * 4
                xt = x_tile[i % 2]
                vector.wait_ge(s_l3[i % 2].h,
                               s_l3[i % 2].tgt((i // 2 + 1) * 16))
                if i >= 2:
                    vector.wait_ge(s_a3.h, s_a3.tgt(i - 1))
                VCH.inc(vector.tensor_reduce(ln_cols[:, p4:p4 + 1], xt[:, :],
                                             AX, ALU.add))
                vfence(vector, VCH)
                VCH.inc(vector.tensor_scalar_mul(ln_cols[:, p4 + 1:p4 + 2],
                                                 ln_cols[:, p4:p4 + 1],
                                                 1.0 / DM))
                vfence(vector, VCH)
                s_v3.inc(vector.tensor_scalar(xm_tile[:, :], xt[:, :],
                                              ln_cols[:, p4 + 1:p4 + 2], None,
                                              ALU.subtract))
                vfence(vector, s_v3)
                VCH.inc(vector.scalar_tensor_tensor(
                    x_tile[i % 2][:, :], xm_tile[:, :], 1.0, xm_tile[:, :],
                    ALU.mult, ALU.mult, accum_out=ln_cols[:, p4 + 2:p4 + 3]))
                vfence(vector, VCH)
                s_v3.inc(vector.tensor_scalar(ln_cols[:, p4 + 3:p4 + 4],
                                              ln_cols[:, p4 + 2:p4 + 3],
                                              1.0 / DM, 1e-5,
                                              ALU.mult, ALU.add))
                vector.wait_ge(s_a3.h, s_a3.tgt(i + 1))
                if i >= 2:
                    vector.wait_ge(s_m3.h, s_m3.tgt((i - 1) * DMC))
                s_n3.inc(vector.tensor_scalar(xnt_tile[i % 2][:, :],
                                              xm_tile[:, :],
                                              ln_cols[:, 8 + i % 2:9 + i % 2],
                                              None, ALU.mult))
            for gidx in range(NMEAN):
                vector.wait_ge(s_ql[gidx % NQB].h,
                               s_ql[gidx % NQB].tgt((gidx // NQB + 1) * 16))
                s_qdone.inc(vector.tensor_reduce(
                    msums[:, gidx:gidx + 1], QF[gidx % NQB][:, :FD], AX,
                    ALU.add, apply_absolute_value=True))
            vfence(vector, s_qdone)
            base = 0
            last = None
            for wi, nch in enumerate([HC, HC, HIDP // 128]):
                last = vector.tensor_reduce(
                    msums[:, NMEAN + wi:NMEAN + wi + 1],
                    msums[:, base:base + nch], AX, ALU.add)
                base += nch
            s_v4.inc(last)
            vector.wait_ge(s_m4.h, s_m4.tgt(1))
            VCH.inc(vector.tensor_scalar_mul(sc_tmp[0:1, 0:3], pA[0:1, 0:3],
                                             1.0 / (HID * DM)))
            vfence(vector, VCH)
            VCH.inc(vector.tensor_mul(sc_tmp[0:1, 3:4], sc_tmp[0:1, 1:2],
                                      sc_tmp[0:1, 2:3]))
            vfence(vector, VCH)
            s_v4.inc(vector.reciprocal(sc_tmp[0:1, 4:7], sc_tmp[0:1, 0:3]))
            vector.wait_ge(s_e3.h, s_e3.tgt(B3_EV_TOTAL))
            vector.wait_ge(s_scl.h, s_scl.tgt(1))
            for qi, (Ws, Wd, ch, kind, rc) in enumerate(qch):
                gidx = NMEAN + qi
                vector.wait_ge(s_ql[gidx % NQB].h,
                               s_ql[gidx % NQB].tgt((gidx // NQB + 1) * 16))
                if qi >= 2:
                    pp = qi % 2
                    vector.wait_ge(s_ts[pp].h,
                                   s_ts[pp].tgt(((qi - 2) // 2 + 1) * 16))
                if qi >= 3:
                    vector.wait_ge(s_qdone.h, s_qdone.tgt(NMEAN + qi - 2))
                VCH.inc(vector.tensor_scalar(QPOS[qi % 3][:, :FD],
                                             QF[gidx % NQB][:, :FD],
                                             rc[:, 0:1], 0.5,
                                             ALU.mult, ALU.is_gt))
                VCH.inc(vector.tensor_scalar(QF[gidx % NQB][:, :FD],
                                             QF[gidx % NQB][:, :FD],
                                             rc[:, 0:1], -0.5,
                                             ALU.mult, ALU.is_lt))
                vfence(vector, VCH)
                s_qdone.inc(vector.tensor_sub(TSTG[qi % 2][:, :FD],
                                              QPOS[qi % 3][:, :FD],
                                              QF[gidx % NQB][:, :FD]))
            nb_ = 0
            for g in range(DIC):
                for part in range(3):
                    ps, sl = bank(nb_ % 4)
                    vector.wait_ge(s_wm.h, s_wm.tgt(nb_ + 1))
                    if part == 0:
                        s_we.inc(vector.tensor_copy(
                            wxt_sb[:, g * EWX:g * EWX + 128],
                            ps[:, sl:sl + 128]))
                    elif part == 1:
                        s_we.inc(vector.tensor_copy(
                            wxt_sb[:, g * EWX + 128:(g + 1) * EWX],
                            ps[:, sl:sl + EWX - 128]))
                    else:
                        s_we.inc(vector.tensor_copy(
                            wdtt_sb[:, g * 128:(g + 1) * 128],
                            ps[:, sl:sl + 128]))
                    nb_ += 1

        @blk.scalar
        def _(scalar):
            def evac3(i):
                for gi_, (j0, jn) in enumerate(egroups):
                    ev = i * NEV + gi_
                    sl0 = (ev % 2) * NB
                    scalar.wait_ge(s_m3.h, s_m3.tgt(i * DMC + j0 + jn))
                    dst = xn_f[:, :].rearrange("p (j t) -> p j t", t=TPAD)
                    s_e3.inc(scalar.activation(
                        dst[:, j0:j0 + jn, i * 128:(i + 1) * 128],
                        pB[:, sl0:sl0 + jn * 128].rearrange(
                            "p (j m) -> p j m", m=128),
                        AF.Copy))

            for i in range(NT):
                p4 = (i % 2) * 4
                scalar.wait_ge(s_v3.h, s_v3.tgt(i * 2 + 2))
                SCH.inc(scalar.activation(ln_cols[:, 10:11],
                                          ln_cols[:, p4 + 3:p4 + 4], AF.Ln))
                vfence(scalar, SCH)
                s_a3.inc(scalar.activation(ln_cols[:, 8 + i % 2:9 + i % 2],
                                           ln_cols[:, 10:11], AF.Exp,
                                           scale=-0.5))
                if i >= 1:
                    evac3(i - 1)
            evac3(NT - 1)
            scalar.wait_ge(s_m4.h, s_m4.tgt(2))
            scalar.activation(s1_col[:, :], pB[:, 2 * NB:2 * NB + 1], AF.Copy)
            scalar.activation(s23_col[:, :], pB[:, 2 * NB + 1:2 * NB + 2],
                              AF.Copy)
            scalar.activation(r1_col[:, :], pB[:, 2 * NB + 2:2 * NB + 3],
                              AF.Copy)
            scalar.activation(r3_col[:, :], pB[:, 2 * NB + 3:2 * NB + 4],
                              AF.Copy)
            s_scl.inc(scalar.activation(r2_col[:, :],
                                        pB[:, 2 * NB + 4:2 * NB + 5],
                                        AF.Copy))

        @blk.tensor
        def _(tensor):
            for i in range(NT):
                tensor.wait_ge(s_n3.h, s_n3.tgt(i + 1))
                for gi_, (j0, jn) in enumerate(egroups):
                    ev = i * NEV + gi_
                    sl0 = (ev % 2) * NB
                    if ev >= 2:
                        tensor.wait_ge(s_e3.h, s_e3.tgt(ev - 1))
                    for j in range(j0, j0 + jn):
                        s_m3.inc(tensor.transpose(
                            pB[:, sl0 + (j - j0) * 128:sl0 + (j - j0 + 1) * 128],
                            xnt_tile[i % 2][:, j * 128:(j + 1) * 128],
                            ident_f32[:, :]))
            tensor.wait_ge(s_v4.h, s_v4.tgt(1))
            s_m4.inc(tensor.matmul(pA[0:1, 0:3], ones_col[:, :],
                                   msums[:, NMEAN:NMEAN + 3],
                                   start=True, stop=True))
            tensor.wait_ge(s_v4.h, s_v4.tgt(2))
            last = None
            for k, src_i in enumerate([0, 3, 4, 5, 6]):
                last = tensor.matmul(pB[:, 2 * NB + k:2 * NB + k + 1],
                                     ones_row[:, :],
                                     sc_tmp[0:1, src_i:src_i + 1],
                                     start=True, stop=True)
            s_m4.inc(last)
            tensor.wait_ge(s_wx.h, s_wx.tgt(48))
            nb_ = 0
            for g in range(DIC):
                for part in range(3):
                    ps, sl = bank(nb_ % 4)
                    if nb_ >= 4:
                        tensor.wait_ge(s_we.h, s_we.tgt(nb_ - 3))
                    if part == 0:
                        s_wm.inc(tensor.transpose(
                            ps[:, sl:sl + 128],
                            WXP1[:, g * 128:(g + 1) * 128], ident_f32[:, :]))
                    elif part == 1:
                        s_wm.inc(tensor.transpose(
                            ps[:, sl:sl + EWX - 128],
                            WXP2[0:EWX - 128, g * 128:(g + 1) * 128],
                            ident_f32[0:EWX - 128, 0:EWX - 128]))
                    else:
                        s_wm.inc(tensor.transpose(
                            ps[:, sl:sl + 128],
                            WDTH[:, g * RK:(g + 1) * RK], ident_f32[:, :]))
                    nb_ += 1

        @blk.gpsimd
        def _(gpsimd):
            for qi, (Ws, Wd, ch, kind, rc) in enumerate(qch):
                gpsimd.wait_ge(s_qdone.h, s_qdone.tgt(NMEAN + qi + 1))
                if kind == "2":
                    dst = Wd.ap()[:, :, ch * 128:(ch + 1) * 128].rearrange(
                        "c p m -> p c m")
                    srcv = TSTG[qi % 2][:, :FD].rearrange(
                        "p (c m) -> p c m", m=128)
                    s_ts[qi % 2].inc(gpsimd.dma_start(dst, srcv), 16)
                else:
                    s_ts[qi % 2].inc(gpsimd.dma_start(
                        Wd.ap()[ch], TSTG[qi % 2][:, :FD]), 16)

    drain()

    drain()

    st_buf = [big(f"p3st_{i}", [128, DMC * 128], BF16, XNO + i * DMC * 128 * 2)
              for i in range(2)]
    P3O = XNO + 2 * DMC * 128 * 2
    XBW = ((TP + 7) // 8) * 8
    xbuf = [big(f"p3xb_{i}", [128, XBW], F32, P3O + i * XBW * 4)
            for i in range(2)]
    P3O2 = P3O + 2 * XBW * 4
    xcacc = [big(f"p3acc_{i}", [128, T], F32, P3O2 + i * T * 4)
             for i in range(2)]
    xcf = [big(f"p3xc_{i}", [128, T], F32, P3O2 + (2 + i) * T * 4)
           for i in range(2)]
    xcb = [big(f"p3xcb_{i}", [128, T], BF16, P3O2 + 4 * T * 4 + i * T * 2)
           for i in range(2)]
    zst = [big(f"p3z_{i}", [128, T], F32,
               P3O2 + 4 * T * 4 + 2 * T * 2 + i * T * 4) for i in range(2)]

    s_l6 = [pool.get_dma(), pool.get_dma()]
    s_m6 = pool.get()
    s_f6 = pool.get()
    s_cv6 = pool.get()
    s_si6 = pool.get()
    s_st6x = [pool.get_dma(), pool.get_dma()]
    s_st6z = [pool.get_dma(), pool.get_dma()]
    s_h6 = pool.get()
    s_hv6 = pool.get()
    with nc.Block() as blk:
        @blk.tensor
        def _(tensor):
            xnr = xn_f[:, :].rearrange("p (j t) -> p j t", t=TPAD)
            for ci in range(EC):
                tensor.wait_ge(s_l6[ci % 2].h,
                               s_l6[ci % 2].tgt((ci // 2 + 1) * 16))
                if ci >= 2:
                    tensor.wait_ge(s_f6.h, s_f6.tgt(ci - 1))
                if 1 <= ci <= DIC:
                    tensor.wait_ge(s_hv6.h, s_hv6.tgt(ci))
                ps = pA if ci % 2 == 0 else pB
                last = None
                for j in range(DMC):
                    st = st_buf[ci % 2][:, j * 128:(j + 1) * 128]
                    for b in range(TB):
                        last = tensor.matmul(
                            ps[:, b * NB:(b + 1) * NB], st,
                            xnr[:, j, 3 + b * NB:3 + (b + 1) * NB],
                            start=(j == 0), stop=(j == DMC - 1))
                s_m6.inc(last)
                if ci < DIC:
                    po = pB if ci % 2 == 0 else pA
                    if ci >= 1:
                        tensor.wait_ge(s_f6.h, s_f6.tgt(ci))
                    if ci >= 2:
                        tensor.wait_ge(s_hv6.h, s_hv6.tgt(ci - 1))
                    lh = None
                    for j in range(DMC):
                        lh = tensor.matmul(
                            po[:, 0:3], st_buf[ci % 2][:, j * 128:(j + 1) * 128],
                            xnr[:, j, 0:3],
                            start=(j == 0), stop=(j == DMC - 1))
                    s_h6.inc(lh)

        @blk.scalar
        def _(scalar):
            nz = 0
            for ci in range(EC):
                scalar.wait_ge(s_m6.h, s_m6.tgt(ci + 1))
                ps = pA if ci % 2 == 0 else pB
                if ci < DIC:
                    if ci >= 2:
                        scalar.wait_ge(s_cv6.h, s_cv6.tgt(ci - 1))
                    scalar.wait_ge(s_h6.h, s_h6.tgt(ci + 1))
                    po = pB if ci % 2 == 0 else pA
                    s_hv6.inc(scalar.activation(xbuf[ci % 2][:, 0:3],
                                                po[:, 0:3], AF.Copy))
                    s_f6.inc(scalar.activation(xbuf[ci % 2][:, 3:3 + T],
                                               ps[:, :T], AF.Copy))
                    scalar.wait_ge(s_cv6.h, s_cv6.tgt(ci + 1))
                    if ci >= 2:
                        scalar.wait_ge(s_st6x[ci % 2].h,
                                       s_st6x[ci % 2].tgt((ci - 2) // 2 * 32
                                                          + 32))
                    scalar.activation(xcf[ci % 2][:, :], xcacc[ci % 2][:, :],
                                      AF.Silu, bias=cb_sb[:, ci:ci + 1])
                    s_si6.inc(scalar.activation(xcb[ci % 2][:, :],
                                                xcacc[ci % 2][:, :], AF.Silu,
                                                bias=cb_sb[:, ci:ci + 1]))
                else:
                    if nz >= 2:
                        scalar.wait_ge(s_st6z[nz % 2].h,
                                       s_st6z[nz % 2].tgt((nz - 2) // 2 * 16
                                                          + 16))
                    s_f6.inc(scalar.activation(zst[nz % 2][:, :], ps[:, :T],
                                               AF.Silu))
                    nz += 1

        @blk.vector
        def _(vector):
            for ci in range(DIC):
                vector.wait_ge(s_f6.h, s_f6.tgt(ci + 1))
                if ci >= 2:
                    vector.wait_ge(s_si6.h, s_si6.tgt(ci - 1))
                xb = xbuf[ci % 2]
                acc = xcacc[ci % 2]
                VCH.inc(vector.tensor_scalar_mul(acc[:, :], xb[:, 0:T],
                                                 cw_sb[:, ci * 4:ci * 4 + 1]))
                for k in range(1, 4):
                    vfence(vector, VCH)
                    st_ = vector.scalar_tensor_tensor(
                        acc[:, :], xb[:, k:k + T],
                        cw_sb[:, ci * 4 + k:ci * 4 + k + 1],
                        acc[:, :], ALU.mult, ALU.add)
                    if k < 3:
                        VCH.inc(st_)
                    else:
                        s_cv6.inc(st_)

        @blk.gpsimd
        def _(gpsimd):
            winv6 = WIN.ap().rearrange("(j p) e -> p j e", p=128)
            for ci in range(min(2, EC)):
                s_l6[ci % 2].inc(gpsimd.dma_start(
                    st_buf[ci % 2][:, :].rearrange("p (j m) -> p j m", m=128),
                    winv6[:, :, ci * 128:(ci + 1) * 128]), 16)
            nz = 0
            for ci in range(EC):
                if ci + 2 < EC:
                    cl = ci + 2
                    gpsimd.wait_ge(s_m6.h, s_m6.tgt(cl - 1))
                    if cl - 2 < DIC:
                        gpsimd.wait_ge(s_h6.h, s_h6.tgt(cl - 1))
                    s_l6[cl % 2].inc(gpsimd.dma_start(
                        st_buf[cl % 2][:, :].rearrange("p (j m) -> p j m",
                                                       m=128),
                        winv6[:, :, cl * 128:(cl + 1) * 128]), 16)
                if ci < DIC:
                    gpsimd.wait_ge(s_si6.h, s_si6.tgt(ci + 1))
                    s_st6x[ci % 2].inc(
                        gpsimd.dma_start(XC.ap()[ci], xcf[ci % 2][:, :]), 16)
                    s_st6x[ci % 2].inc(
                        gpsimd.dma_start(XCB.ap()[ci], xcb[ci % 2][:, :]), 16)
                else:
                    gpsimd.wait_ge(s_f6.h, s_f6.tgt(ci + 1))
                    s_st6z[nz % 2].inc(
                        gpsimd.dma_start(SZG.ap()[ci - DIC],
                                         zst[nz % 2][:, :]), 16)
                    nz += 1

    drain()

    xcb_buf = [big(f"p4x_{i}", [128, T], BF16, XNO + i * T * 2)
               for i in range(2)]

    s_l9 = [pool.get_dma(), pool.get_dma()]
    s_m9 = pool.get()
    s_bc9 = pool.get_dma()
    with nc.Block() as blk:
        @blk.sync
        def _(sync):
            for g in range(DIC):
                if g >= 2:
                    sync.wait_ge(s_m9.h, s_m9.tgt(g - 1))
                s_l9[g % 2].inc(
                    sync.dma_start(xcb_buf[g % 2][:, :], XCB.ap()[g]), 16)

        @blk.tensor
        def _(tensor):
            for g in range(DIC):
                tensor.wait_ge(s_l9[g % 2].h,
                               s_l9[g % 2].tgt((g // 2 + 1) * 16))
                st1 = wxt_sb[:, g * EWX:g * EWX + 128]
                st2 = wxt_sb[:, g * EWX + 128:(g + 1) * EWX]
                last = None
                for b in range(TB):
                    tensor.matmul(pA[:, b * NB:(b + 1) * NB], st1,
                                  xcb_buf[g % 2][:, b * NB:(b + 1) * NB],
                                  start=(g == 0), stop=(g == DIC - 1))
                    last = tensor.matmul(
                        pB[0:2 * NS, b * NB:(b + 1) * NB], st2,
                        xcb_buf[g % 2][:, b * NB:(b + 1) * NB],
                        start=(g == 0), stop=(g == DIC - 1))
                s_m9.inc(last)

        @blk.scalar
        def _(scalar):
            scalar.wait_ge(s_m9.h, s_m9.tgt(DIC))
            scalar.activation(dtr_sb[:, :], pA[:, :T], AF.Copy)

        @blk.vector
        def _(vector):
            vector.wait_ge(s_m9.h, s_m9.tgt(DIC))
            s_m9.inc(vector.tensor_copy(bc_sb[:, :], pB[0:2 * NS, :T]))

        @blk.gpsimd
        def _(gpsimd):
            gpsimd.wait_ge(s_m9.h, s_m9.tgt(DIC + 1))
            s_bc9.inc(gpsimd.dma_start(BCD.ap(), bc_sb[0:2 * NS, :]), 16)
            s_bc9.inc(gpsimd.dma_start(BCD16.ap(), bc_sb[0:2 * NS, :]), 16)

    drain()

    dlt_stg = [big(f"p5d_{i}", [128, T], F32, XNO + 2 * T * 2 + i * T * 4)
               for i in range(2)]
    sg_stg = big("p5sg", [128, T], F32, XNO + 2 * T * 2 + 2 * T * 4)
    s_m10 = pool.get()
    s_e10 = pool.get()
    s_s10 = [pool.get_dma(), pool.get_dma()]
    with nc.Block() as blk:
        @blk.tensor
        def _(tensor):
            for g in range(DIC):
                if g >= 2:
                    tensor.wait_ge(s_e10.h, s_e10.tgt(g - 1))
                ps = pA if g % 2 == 0 else pB
                st = wdtt_sb[:, g * 128:(g + 1) * 128]
                last = None
                for b in range(TB):
                    last = tensor.matmul(ps[:, b * NB:(b + 1) * NB], st,
                                         dtr_sb[:, b * NB:(b + 1) * NB],
                                         start=True, stop=True)
                s_m10.inc(last)

        @blk.scalar
        def _(scalar):
            for g in range(DIC):
                scalar.wait_ge(s_m10.h, s_m10.tgt(g + 1))
                if g >= 2:
                    scalar.wait_ge(s_s10[g % 2].h,
                                   s_s10[g % 2].tgt((g - 2) // 2 * 16 + 16))
                ps = pA if g % 2 == 0 else pB
                vfence(scalar, s_e10)
                SCH.inc(scalar.activation(sg_stg[:, :], ps[:, :T],
                                          AF.Sigmoid, bias=db_sb[:, g:g + 1],
                                          scale=-1.0))
                vfence(scalar, SCH)
                s_e10.inc(scalar.activation(dlt_stg[g % 2][:, :],
                                            sg_stg[:, :], AF.Ln))

        @blk.sync
        def _(sync):
            for g in range(DIC):
                sync.wait_ge(s_e10.h, s_e10.tgt(g + 1))
                s_s10[g % 2].inc(
                    sync.dma_start(DLT.ap()[g], dlt_stg[g % 2][:, :]), 16)

    drain()

    assert DIC % 2 == 0
    TH1 = T // 4
    NU1 = 4 * HC
    CI0 = (HC + 1) // 2
    NU2 = TB * DMC
    NG = TB * DIC
    F1_END = (NU1 - 1) // 2 + 3
    def sf1(u):
        return min(u // 2, NG - 1)

    slot_f1 = [[] for _ in range(NG)]
    for u in range(NU1):
        slot_f1[sf1(u)].append(u)
    slot_f2 = [[] for _ in range(NG)]
    for v in range(NU2):
        slot_f2[min(F1_END + 2 + v // 2, NG - 1)].append(v)
    slot_w1 = [[] for _ in range(NG)]
    slot_w3 = [[] for _ in range(NG)]
    for ch in range(2, HC):
        slot_w1[max(sf1(4 * ch) - 1, 0)].append(ch)
        slot_w3[min(sf1(4 * ch) + 1, NG - 1)].append(ch)
    def sislot(u):
        return min(sf1(u) + 1, NG - 1)

    def h3slot(u):
        return min(sf1(u) + 2, NG - 1)

    def gmslot(u):
        return min(sf1(u) + 3, NG - 1)

    def evslot(v):
        return min(F1_END + 4 + v // 2, NG - 1)

    def f2slot(v):
        return min(F1_END + 2 + v // 2, NG - 1)

    slot_gload = [[] for _ in range(NG)]
    for tc in range(TB):
        slot_gload[f2slot(tc * DMC)].append(tc)
    slot_w2load = [[] for _ in range(NG)]
    for v in range(NU2):
        slot_w2load[f2slot(v)].append(v)

    slot_si = [[] for _ in range(NG)]
    for u in range(NU1):
        slot_si[sislot(u)].append(u)
    slot_h3 = [[] for _ in range(NG)]
    for u in range(NU1):
        slot_h3[h3slot(u)].append(u)
    slot_gm = [[] for _ in range(NG)]
    for u in range(NU1):
        slot_gm[gmslot(u)].append(u)
    slot_gst = slot_gm
    slot_ev2 = [[] for _ in range(NG)]
    for v in range(NU2):
        slot_ev2[evslot(v)].append(v)

    def gm_pre_(u):
        return sislot(u) < gmslot(u) and h3slot(u) < gmslot(u)

    def h3_pre(u):
        if u < 2:
            return True
        gs = gmslot(u - 2)
        return gs < h3slot(u) or (gs == h3slot(u) and gm_pre_(u - 2))

    def si_pre(u):
        if sf1(u) >= sislot(u):
            return False
        if u < 8:
            return True
        gs = gmslot(u - 8)
        return gs < sislot(u) or (gs == sislot(u) and gm_pre_(u - 8))

    def gm_pre(u):
        return gm_pre_(u)

    def ev_pre(v):
        return f2slot(v) < evslot(v)

    def h1_pre(u):
        if u < 2:
            return True
        s = sf1(u)
        return sislot(u - 2) < s or (sislot(u - 2) == s and si_pre(u - 2))

    F1B = XNO
    w1t_buf = [big(f"f1a_{i}", [128, DMC * 128], BF16, F1B + i * DMC * 256)
               for i in range(2)]
    w3t_buf = [big(f"f1b_{i}", [128, DMC * 128], BF16,
                   F1B + (2 + i) * DMC * 256) for i in range(2)]
    h1s = [big(f"f1h_{i}", [128, TH1], BF16,
               F1B + 4 * DMC * 256 + i * TH1 * 2) for i in range(8)]
    g_stg = [big(f"f1g_{i}", [128, TH1], BF16,
                 F1B + 4 * DMC * 256 + (8 + i) * TH1 * 2) for i in range(2)]
    F1SZ = 4 * DMC * 256 + 10 * TH1 * 2
    SRO = XNO + F1SZ
    Bb = big("p6bb", [128, NS * NB], BF16, SRO)
    Cb = big("p6cb", [128, NS * NB], BF16, SRO + NS * NB * 2)
    P6O = SRO + 2 * NS * NB * 2
    dlt_b = [big(f"p6dl_{i}", [128, NB], F32, P6O + i * NB * 4)
             for i in range(2)]
    xc_b = [big(f"p6xc_{i}", [128, NB], F32, P6O + (2 + i) * NB * 4)
            for i in range(2)]
    szg_b = [big(f"p6sz_{i}", [128, NB], F32, P6O + (4 + i) * NB * 4)
             for i in range(2)]
    t2_b = [big(f"p6t2_{i}", [128, NB], F32, P6O + (6 + i) * NB * 4)
            for i in range(2)]
    e_b = [big(f"p6e_{i}", [128, NB], F32, P6O + (8 + i) * NB * 4)
           for i in range(3)]
    s_bK = [big(f"p6s_{i}", [128, NB], BF16, P6O + 11 * NB * 4 + i * NB * 2)
            for i in range(6)]
    dtu_b = [big(f"p6du_{i}", [128, NB], BF16,
                 P6O + 11 * NB * 4 + (6 + i) * NB * 2) for i in range(2)]
    dbu_b = [big(f"p6db_{i}", [128, NB], BF16,
                 P6O + 11 * NB * 4 + (8 + i) * NB * 2) for i in range(2)]
    cs_b = [big(f"p6cs_{i}", [128, NB], BF16,
                P6O + 11 * NB * 4 + (10 + i) * NB * 2)
            for i in range(8)]
    yg_stg = [big(f"p6yg_{i}", [128, NB], BF16,
                  P6O + 11 * NB * 4 + (18 + i) * NB * 2)
              for i in range(2)]
    SEND = P6O + 11 * NB * 4 + 20 * NB * 2
    F2SZ = HC * NB * 2 + 2 * CI0 * 128 * 2 + 2 * NB * 4
    F2B = 0 if F2SZ <= XNO else SEND
    g_res = big("f2g", [128, HC * NB], BF16, F2B)
    w2t_buf = [big(f"f2w_{i}", [128, CI0 * 128], BF16,
                   F2B + HC * NB * 2 + i * CI0 * 128 * 2) for i in range(2)]
    f_stg = [big(f"f2f_{i}", [128, NB], F32,
                 F2B + HC * NB * 2 + 2 * CI0 * 128 * 2 + i * NB * 4)
             for i in range(2)]

    s_bb = [pool.get_dma() for _ in range(4)]
    s_bc = [pool.get_dma() for _ in range(4)]
    s_ldq = [pool.get_dma(), pool.get_dma()]
    s_ee = pool.get()
    s_sc = pool.get()
    s_cs = pool.get()
    s_im = pool.get()
    s_gc = pool.get()
    s_gt = pool.get()
    s_sy = [pool.get_dma(), pool.get_dma()]
    s_fw = [pool.get_dma(), pool.get_dma()]
    s_fw3 = [pool.get_dma(), pool.get_dma()]
    s_m7 = pool.get()
    s_h3 = pool.get()
    s_a7 = pool.get()
    s_g7 = pool.get()
    s_gs = [pool.get_dma(), pool.get_dma()]
    s_fg = pool.get_dma()
    s_f2w = [pool.get_dma(), pool.get_dma()]
    s_f2m = pool.get()
    s_f2e = pool.get()
    s_ff = [pool.get_dma(), pool.get_dma()]

    def h_blocks(W):
        return [(s, min(NB, W - s)) for s in range(0, W, NB)]

    H3O = max(TH1, NB)

    NCH6 = TB
    with nc.Block() as blk:
        @blk.gpsimd
        def _(gpsimd):
            for cc in range(NCH6):
                if cc >= 1:
                    gpsimd.wait_ge(s_cs.h, s_cs.tgt(cc * DIC * NS))
                for n in range(NS):
                    s_bb[n * 4 // NS].inc(gpsimd.dma_start(
                        Bb[:, n * NB:(n + 1) * NB],
                        BCD.ap()[n:n + 1,
                                 cc * NB:(cc + 1) * NB].broadcast_to(
                            [128, NB])), 16)
                for g in range(DIC):
                    gi = cc * DIC + g

                    def gs_store(u):
                        ch, qt = u // 4, u % 4
                        gpsimd.wait_ge(s_g7.h, s_g7.tgt(u + 1))
                        gsv = GS.ap()[ch].rearrange("q p t -> p q t")
                        if TB % 4 == 0:
                            nq = TB // 4
                            dst = gsv[:, qt * nq:(qt + 1) * nq]
                        else:
                            nq = 1
                            dst = gsv[:, 0:1, qt * TH1:(qt + 1) * TH1]
                        s_gs[u % 2].inc(gpsimd.dma_start(
                            dst,
                            g_stg[u % 2][:, :].rearrange(
                                "p (q t) -> p q t", q=nq)), 16)

                    def ff_store(v):
                        tc, c2 = v // DMC, v % DMC
                        gpsimd.wait_ge(s_f2e.h, s_f2e.tgt(v + 1))
                        s_ff[v % 2].inc(gpsimd.dma_start(FF.ap()[c2, tc],
                                                         f_stg[v % 2][:, :]),
                                        16)

                    for u in slot_gst[gi]:
                        if gm_pre(u):
                            gs_store(u)
                    for v in slot_ev2[gi]:
                        if ev_pre(v):
                            ff_store(v)
                    for n in range(NS):
                        k = gi * NS + n
                        gpsimd.wait_ge(s_sc.h, s_sc.tgt(k + 1))
                        s_gc.inc(gpsimd.tensor_copy(
                            state[:, g * NS + n:g * NS + n + 1],
                            s_bK[k % 6][:, NB - 1:NB]))
                    for u in slot_gst[gi]:
                        if not gm_pre(u):
                            gs_store(u)
                    for v in slot_ev2[gi]:
                        if not ev_pre(v):
                            ff_store(v)

        @blk.sync
        def _(sync):
            def store_yg(pg):
                sync.wait_ge(s_gt.h, s_gt.tgt(pg + 1))
                s_sy[pg % 2].inc(sync.dma_start(
                    YG.ap()[pg % DIC, :,
                            (pg // DIC) * NB:(pg // DIC + 1) * NB],
                    yg_stg[pg % 2][:, :]), 16)

            for ch in range(min(2, HC)):
                s_fw[ch % 2].inc(sync.dma_start(w1t_buf[ch % 2][:, :],
                                                W1T.ap()[ch]), 16)
                s_fw3[ch % 2].inc(sync.dma_start(w3t_buf[ch % 2][:, :],
                                                 W3T.ap()[ch]), 16)
            for cc in range(NCH6):
                for g in range(DIC):
                    gi = cc * DIC + g
                    if gi >= 2:
                        sync.wait_ge(s_gt.h, s_gt.tgt(gi - 1))
                    s_ldq[g % 2].inc(sync.dma_start(
                        dlt_b[g % 2][:, :],
                        DLT.ap()[g, :, cc * NB:(cc + 1) * NB]), 16)
                    s_ldq[g % 2].inc(sync.dma_start(
                        xc_b[g % 2][:, :],
                        XC.ap()[g, :, cc * NB:(cc + 1) * NB]), 16)
                    s_ldq[g % 2].inc(sync.dma_start(
                        szg_b[g % 2][:, :],
                        SZG.ap()[g, :, cc * NB:(cc + 1) * NB]), 16)
                    if gi >= 1:
                        store_yg(gi - 1)
                    for ch in slot_w1[gi]:
                        sync.wait_ge(s_m7.h, s_m7.tgt(4 * ch - 4))
                        s_fw[ch % 2].inc(sync.dma_start(
                            w1t_buf[ch % 2][:, :], W1T.ap()[ch]), 16)
                    for ch in slot_w3[gi]:
                        sync.wait_ge(s_h3.h, s_h3.tgt(4 * ch - 4))
                        s_fw3[ch % 2].inc(sync.dma_start(
                            w3t_buf[ch % 2][:, :], W3T.ap()[ch]), 16)
                    for tc in slot_gload[gi]:
                        if tc == 0:
                            sync.wait_ge(s_m7.h, s_m7.tgt(NU1))
                            sync.wait_ge(s_h3.h, s_h3.tgt(NU1))
                            sync.wait_ge(s_gs[0].h,
                                         s_gs[0].tgt(((NU1 + 1) // 2) * 16))
                            sync.wait_ge(s_gs[1].h,
                                         s_gs[1].tgt((NU1 // 2) * 16))
                        else:
                            sync.wait_ge(s_f2m.h, s_f2m.tgt(2 * tc * DMC))
                        for ci in range(HC):
                            s_fg.inc(sync.dma_start(
                                g_res[:, ci * NB:(ci + 1) * NB],
                                GS.ap()[ci, tc]), 16)
                    for v in slot_w2load[gi]:
                        tc, c2 = v // DMC, v % DMC
                        for p in range(2):
                            q = 2 * v + p
                            nci = CI0 if p == 0 else HC - CI0
                            if q < 2:
                                sync.wait_ge(s_m7.h, s_m7.tgt(NU1))
                                sync.wait_ge(s_h3.h, s_h3.tgt(NU1))
                            else:
                                sync.wait_ge(s_f2m.h, s_f2m.tgt(q - 1))
                            s_f2w[p].inc(sync.dma_start(
                                w2t_buf[p][:, :nci * 128],
                                W2T.ap()[c2, :,
                                         p * CI0 * 128:
                                         (p * CI0 + nci) * 128]), 16)
            store_yg(NG - 1)

        @blk.scalar
        def _(scalar):
            def silu(u):
                scalar.wait_ge(s_m7.h, s_m7.tgt(u + 1))
                if u >= 8:
                    scalar.wait_ge(s_g7.h, s_g7.tgt(u - 7))
                h1o = (u % 2) * NB
                s_a7.inc(scalar.activation(h1s[u % 8][:, :],
                                           pB[:, h1o:h1o + TH1], AF.Silu,
                                           scale=s1_col[:, 0:1]))

            def f2evac(v):
                scalar.wait_ge(s_f2m.h, s_f2m.tgt(2 * v + 2))
                if v >= 2:
                    scalar.wait_ge(s_ff[v % 2].h,
                                   s_ff[v % 2].tgt((v // 2) * 16))
                psl = (v % 4) * NB
                s_f2e.inc(scalar.activation(f_stg[v % 2][:, :],
                                            pB[:, psl:psl + NB], AF.Copy,
                                            scale=s23_col[:, 0:1]))

            for cc in range(NCH6):
                if cc >= 1:
                    scalar.wait_ge(s_cs.h, s_cs.tgt(cc * DIC * NS))
                for n in range(NS):
                    s_bc[n * 4 // NS].inc(scalar.dma_start(
                        Cb[:, n * NB:(n + 1) * NB],
                        BCD16.ap()[NS + n:NS + n + 1,
                                   cc * NB:(cc + 1) * NB].broadcast_to(
                            [128, NB])), 16)
                for g in range(DIC):
                    gi = cc * DIC + g
                    for u in slot_si[gi]:
                        if si_pre(u):
                            silu(u)
                    for v in slot_ev2[gi]:
                        if ev_pre(v):
                            f2evac(v)
                    scalar.wait_ge(s_ldq[g % 2].h,
                                   s_ldq[g % 2].tgt((gi // 2 + 1) * 48))
                    for n in range(NS):
                        k = gi * NS + n
                        if k >= 3:
                            scalar.wait_ge(s_sc.h, s_sc.tgt(k - 2))
                        s_ee.inc(scalar.activation(
                            e_b[k % 3][:, :], dlt_b[g % 2][:, :], AF.Exp,
                            scale=a_sb[:, g * NS + n:g * NS + n + 1]))
                    for u in slot_si[gi]:
                        if not si_pre(u):
                            silu(u)
                    for v in slot_ev2[gi]:
                        if not ev_pre(v):
                            f2evac(v)

        @blk.vector
        def _(vector):
            def h3reg(u):
                off = 2 * NB + (u % 2) * NB
                return pB[:, off:off + TH1]

            def gmul(u):
                vector.wait_ge(s_a7.h, s_a7.tgt(u + 1))
                vector.wait_ge(s_h3.h, s_h3.tgt(u + 1))
                if u >= 2:
                    vector.wait_ge(s_gs[u % 2].h,
                                   s_gs[u % 2].tgt((u // 2) * 16))
                s_g7.inc(vector.tensor_mul(g_stg[u % 2][:, :],
                                           h1s[u % 8][:, :], h3reg(u)))

            for cc in range(NCH6):
                for g in range(DIC):
                    gi = cc * DIC + g
                    for u in slot_gm[gi]:
                        if gm_pre(u):
                            gmul(u)
                    vector.wait_ge(s_ldq[g % 2].h,
                                   s_ldq[g % 2].tgt((gi // 2 + 1) * 48))

                    VCH.inc(vector.tensor_mul(dtu_b[g % 2][:, :],
                                              dlt_b[g % 2][:, :],
                                              xc_b[g % 2][:, :]))
                    vfence(vector, VCH)
                    for n in range(NS):
                        k = gi * NS + n
                        if g == 0 and n % (NS // 4) == 0:
                            q4 = n * 4 // NS
                            vector.wait_ge(
                                s_bb[q4].h,
                                s_bb[q4].tgt((cc + 1) * (NS // 4) * 16))
                        VCH.inc(vector.tensor_mul(dbu_b[k % 2][:, :],
                                                  dtu_b[g % 2][:, :],
                                                  Bb[:, n * NB:(n + 1) * NB]))
                        vector.wait_ge(s_ee.h, s_ee.tgt(k + 1))
                        if n % 4 == 0 and k >= 3:
                            vector.wait_ge(s_gc.h, s_gc.tgt(k - 2))
                        if cc >= 1:
                            vector.wait_ge(
                                s_gc.h,
                                s_gc.tgt((cc - 1) * DIC * NS + g * NS + n + 1))
                        vfence(vector, VCH)
                        s_sc.inc(vector.tensor_tensor_scan(
                            s_bK[k % 6][:, :], e_b[k % 3][:, :],
                            dbu_b[k % 2][:, :],
                            (state[:, g * NS + n:g * NS + n + 1]
                             if cc >= 1 else 0.0),
                            ALU.mult, ALU.add))
                        if n % 4 == 0 and k >= 8:
                            vector.wait_ge(s_im.h, s_im.tgt(k - 4))
                        if g == 0 and n % (NS // 4) == 0:
                            q4 = n * 4 // NS
                            vector.wait_ge(
                                s_bc[q4].h,
                                s_bc[q4].tgt((cc + 1) * (NS // 4) * 16))
                        vfence(vector, s_sc)
                        s_cs.inc(vector.tensor_mul(
                            cs_b[k % 8][:, :], s_bK[k % 6][:, :],
                            Cb[:, n * NB:(n + 1) * NB]))
                    vector.wait_ge(s_im.h, s_im.tgt((gi + 1) * NS))
                    vector.wait_ge(s_ldq[g % 2].h,
                                   s_ldq[g % 2].tgt((gi // 2 + 1) * 48))
                    ysl = (gi % 2) * NB
                    VCH.inc(vector.scalar_tensor_tensor(
                        t2_b[gi % 2][:, :], xc_b[g % 2][:, :],
                        d_sb[:, g:g + 1], pA[:, ysl:ysl + NB],
                        ALU.mult, ALU.add))
                    if gi >= 2:
                        vector.wait_ge(s_sy[gi % 2].h,
                                       s_sy[gi % 2].tgt((gi - 2) // 2 * 16
                                                        + 16))
                    vfence(vector, VCH)
                    s_gt.inc(vector.tensor_mul(yg_stg[gi % 2][:, :],
                                               t2_b[gi % 2][:, :],
                                               szg_b[g % 2][:, :]))
                    for u in slot_gm[gi]:
                        if not gm_pre(u):
                            gmul(u)

        @blk.tensor
        def _(tensor):
            xnr = xn_f[:, :].rearrange("p (j t) -> p j t", t=TPAD)

            def h3steps(u):
                ch, qt = u // 4, u % 4
                po = 2 * NB + (u % 2) * NB
                steps = []

                def mk(j, bs, bn, first):
                    def go():
                        if first:
                            tensor.wait_ge(
                                s_fw3[ch % 2].h,
                                s_fw3[ch % 2].tgt((ch // 2 + 1) * 16))
                            if u >= 2:
                                tensor.wait_ge(s_g7.h, s_g7.tgt(u - 1))
                        mm = tensor.matmul(
                            pB[:, po + bs:po + bs + bn],
                            w3t_buf[ch % 2][:, j * 128:(j + 1) * 128],
                            xnr[:, j, 3 + qt * TH1 + bs:
                                3 + qt * TH1 + bs + bn],
                            start=(j == 0), stop=(j == DMC - 1))
                        if j == DMC - 1 and bs == h_blocks(TH1)[-1][0]:
                            s_h3.inc(mm)
                    return go
                for j in range(DMC):
                    for (bs, bn) in h_blocks(TH1):
                        steps.append(mk(j, bs, bn, j == 0 and bs == 0))
                return steps

            def h3chain(u):
                for stp in h3steps(u):
                    stp()

            def h1steps(u):
                ch, qt = u // 4, u % 4
                po = (u % 2) * NB
                steps = []

                def mk(j, bs, bn, first):
                    def go():
                        if first:
                            tensor.wait_ge(
                                s_fw[ch % 2].h,
                                s_fw[ch % 2].tgt((ch // 2 + 1) * 16))
                            if u >= 2:
                                tensor.wait_ge(s_a7.h, s_a7.tgt(u - 1))
                        mm = tensor.matmul(
                            pB[:, po + bs:po + bs + bn],
                            w1t_buf[ch % 2][:, j * 128:(j + 1) * 128],
                            xnr[:, j, 3 + qt * TH1 + bs:
                                3 + qt * TH1 + bs + bn],
                            start=(j == 0), stop=(j == DMC - 1))
                        if j == DMC - 1 and bs == h_blocks(TH1)[-1][0]:
                            s_m7.inc(mm)
                    return go
                for j in range(DMC):
                    for (bs, bn) in h_blocks(TH1):
                        steps.append(mk(j, bs, bn, j == 0 and bs == 0))
                return steps

            def h1chain(u):
                for stp in h1steps(u):
                    stp()

            def f2steps(v):
                tc, c2 = v // DMC, v % DMC
                psl = (v % 4) * NB
                steps = []

                def mk(p, ci, first):
                    def go():
                        if first and p == 0:
                            if v == 0:
                                tensor.wait_ge(s_a7.h, s_a7.tgt(NU1))
                                tensor.wait_ge(s_g7.h, s_g7.tgt(NU1))
                            if v >= 4:
                                tensor.wait_ge(s_f2e.h, s_f2e.tgt(v - 3))
                            tensor.wait_ge(s_fg.h,
                                           s_fg.tgt((tc + 1) * HC * 16))
                        if first:
                            tensor.wait_ge(s_f2w[p].h,
                                           s_f2w[p].tgt((v + 1) * 16))
                        ci_lo = 0 if p == 0 else CI0
                        mm = tensor.matmul(
                            pB[:, psl:psl + NB],
                            w2t_buf[p][:, (ci - ci_lo) * 128:
                                       (ci - ci_lo + 1) * 128],
                            g_res[:, ci * NB:(ci + 1) * NB],
                            start=(ci == 0), stop=(ci == HC - 1))
                        if ci == (CI0 - 1 if p == 0 else HC - 1):
                            s_f2m.inc(mm)
                    return go
                for p in range(2):
                    ci_lo = 0 if p == 0 else CI0
                    ci_hi = CI0 if p == 0 else HC
                    for ci in range(ci_lo, ci_hi):
                        steps.append(mk(p, ci, ci == ci_lo))
                return steps

            for cc in range(NCH6):
                for g in range(DIC):
                    gi = cc * DIC + g
                    ysl = (gi % 2) * NB
                    qs = []
                    for u in slot_f1[gi]:
                        if h1_pre(u):
                            qs += h1steps(u)
                    for u in slot_h3[gi]:
                        if h3_pre(u):
                            qs += h3steps(u)
                    for v in slot_f2[gi]:
                        if f2slot(v) < NG - 1:
                            qs += f2steps(v)
                    qp = 0
                    rr = -(-len(qs) // NS) if qs else 0
                    if gi >= 2:
                        tensor.wait_ge(s_gt.h, s_gt.tgt(gi - 1))
                    for n in range(NS):
                        k = gi * NS + n
                        for _ in range(rr):
                            if qp < len(qs):
                                qs[qp]()
                                qp += 1
                        tensor.wait_ge(s_cs.h, s_cs.tgt(k + 1))
                        s_im.inc(tensor.matmul(
                            pA[:, ysl:ysl + NB], ident_ng[:, :],
                            cs_b[k % 8][:, :],
                            start=(n == 0), stop=(n == NS - 1)))
                    while qp < len(qs):
                        qs[qp]()
                        qp += 1
                    for u in slot_f1[gi]:
                        if not h1_pre(u):
                            h1chain(u)
                    for u in slot_h3[gi]:
                        if not h3_pre(u):
                            h3chain(u)
                    for v in slot_f2[gi]:
                        if not (f2slot(v) < NG - 1):
                            for stp in f2steps(v):
                                stp()

    drain()

    TH = T // 2
    thb = [(s, min(NB, TH - s)) for s in range(0, TH, NB)]
    yg_sb = big("p7yg", [128, DIC * TH], BF16, 0)
    P7O = DIC * TH * 2
    hwq_buf = [big(f"p7hw_{i}", [128, DIC * 128], BF16,
                   P7O + i * DIC * 128 * 2) for i in range(2)]
    P7O2 = P7O + 2 * DIC * 128 * 2
    f_tile = [big(f"p7f_{i}", [128, TH], F32, P7O2 + i * TH * 4)
              for i in range(2)]
    xt_tile = [big(f"p7x_{i}", [128, TH], F32, P7O2 + (2 + i) * TH * 4)
               for i in range(2)]
    o_stg = [big(f"p7o_{i}", [128, TH], F32, P7O2 + (4 + i) * TH * 4)
             for i in range(2)]

    s_ly12 = pool.get_dma()
    s_lh12 = [pool.get_dma(), pool.get_dma()]
    s_lf12 = [pool.get_dma(), pool.get_dma()]
    s_mm12 = pool.get()
    s_ev12 = pool.get()
    s_fin = [pool.get_dma(), pool.get_dma()]
    for hi in range(2):
        t0 = hi * TH
        b_ly = s_ly12.loc
        b_lh = [s.loc for s in s_lh12]
        b_lf = [s.loc for s in s_lf12]
        b_mm = s_mm12.loc
        b_ev = s_ev12.loc
        b_fin = [s.loc for s in s_fin]
        if hi == 1:
            drain()
        with nc.Block() as blk:
            @blk.sync
            def _(sync):
                for g in range(DIC):
                    s_ly12.inc(sync.dma_start(yg_sb[:, g * TH:(g + 1) * TH],
                                              YG.ap()[g, :, t0:t0 + TH]), 16)
                for cc in range(DMC):
                    if cc >= 2:
                        sync.wait_ge(s_mm12.h, s_mm12.tgt(b_mm + cc - 1))
                    s_lh12[cc % 2].inc(sync.dma_start(
                        hwq_buf[cc % 2][:, :].rearrange("p (j m) -> p j m",
                                                        m=128),
                        HWQ.ap()[cc].rearrange("j p m -> p j m")), 16)
                    if cc >= 2:
                        sync.wait_ge(s_ev12.h, s_ev12.tgt(b_ev + cc - 1))
                    ffv = FF.ap()[cc].rearrange("q p t -> p q t")
                    if TB % 2 == 0:
                        fsrc = ffv[:, hi * (TB // 2):(hi + 1) * (TB // 2)]
                        fq = TB // 2
                    else:
                        fsrc = ffv[:, 0:1, t0:t0 + TH]
                        fq = 1
                    s_lf12[cc % 2].inc(
                        sync.dma_start(f_tile[cc % 2][:, :].rearrange(
                            "p (q t) -> p q t", q=fq), fsrc), 16)
                    s_lf12[cc % 2].inc(sync.dma_start(
                        xt_tile[cc % 2][:, :],
                        XT.ap()[cc * 128:(cc + 1) * 128, t0:t0 + TH]), 16)

            @blk.tensor
            def _(tensor):
                tensor.wait_ge(s_ly12.h, s_ly12.tgt(b_ly + DIC * 16))
                for cc in range(DMC):
                    tensor.wait_ge(s_lh12[cc % 2].h,
                                   s_lh12[cc % 2].tgt(b_lh[cc % 2]
                                                      + (cc // 2 + 1) * 16))
                    if cc >= 2:
                        tensor.wait_ge(s_ev12.h, s_ev12.tgt(b_ev + cc - 1))
                    ps = pA if cc % 2 == 0 else pB
                    last = None
                    for j in range(DIC):
                        st = hwq_buf[cc % 2][:, j * 128:(j + 1) * 128]
                        for (bs, bn) in thb:
                            last = tensor.matmul(
                                ps[:, bs:bs + bn], st,
                                yg_sb[:, j * TH + bs:j * TH + bs + bn],
                                start=(j == 0), stop=(j == DIC - 1))
                    s_mm12.inc(last)

            @blk.vector
            def _(vector):
                for cc in range(DMC):
                    vector.wait_ge(s_mm12.h, s_mm12.tgt(b_mm + cc + 1))
                    vector.wait_ge(s_lf12[cc % 2].h,
                                   s_lf12[cc % 2].tgt(b_lf[cc % 2]
                                                      + (cc // 2 + 1) * 32))
                    if cc >= 2:
                        vector.wait_ge(s_fin[cc % 2].h,
                                       s_fin[cc % 2].tgt(b_fin[cc % 2]
                                                         + (cc - 2) // 2 * 16
                                                         + 16))
                    ps = pA if cc % 2 == 0 else pB
                    VCH.inc(vector.tensor_add(o_stg[cc % 2][:, :], ps[:, :TH],
                                              f_tile[cc % 2][:, :]))
                    vfence(vector, VCH)
                    s_ev12.inc(vector.tensor_add(o_stg[cc % 2][:, :],
                                                 o_stg[cc % 2][:, :],
                                                 xt_tile[cc % 2][:, :]))

            @blk.gpsimd
            def _(gpsimd):
                for cc in range(DMC):
                    gpsimd.wait_ge(s_ev12.h, s_ev12.tgt(b_ev + cc + 1))
                    s_fin[cc % 2].inc(gpsimd.dma_start(
                        OUT.ap()[cc * 128:(cc + 1) * 128, t0:t0 + TH],
                        o_stg[cc % 2][:, :]), 16)

    drain()

    nc.finalize()
    return nc


def _hadamard_cb(DI):
    H = np.array([[1.0]], dtype=np.float64)
    while H.shape[0] < DI:
        H = np.block([[H, H], [H, -H]])
    H = (H / math.sqrt(DI)).astype(ml_dtypes.bfloat16)
    DIC = DI // 128
    Hcb = np.empty((DIC, DI, 128), dtype=ml_dtypes.bfloat16)
    for cidx in range(DIC):
        Hcb[cidx] = np.ascontiguousarray(H[cidx * 128:(cidx + 1) * 128, :].T)
    return Hcb


def make_core_inputs(c, x_full, W, b, h):
    cd = derived(c)
    T, TPAD, DM, DI, HIDP, HID = (cd["T"], cd["TPAD"], cd["DM"], cd["DI"],
                                  cd["HIDP"], cd["HID"])
    xb = np.asarray(x_full[b], np.float32)
    xpad = np.zeros((TPAD, DM), dtype=np.float32)
    t0 = h * T
    if t0 >= 3:
        xpad[0:3] = xb[t0 - 3:t0]
    xpad[3:3 + T] = xb[t0:t0 + T]
    xt = np.ascontiguousarray(xb[t0:t0 + T].T)

    w1p = np.zeros((DM, HIDP), dtype=np.float32)
    w1p[:, :HID] = np.ascontiguousarray(W["w1"].T)
    w3p = np.zeros((DM, HIDP), dtype=np.float32)
    w3p[:, :HID] = np.ascontiguousarray(W["w3"].T)
    w2p = np.zeros((HIDP, DM), dtype=np.float32)
    w2p[:HID] = np.ascontiguousarray(W["w2"].T)
    win_t = np.ascontiguousarray(W["W_in"].T)

    return {
        "XPAD": xpad, "XT": xt,
        "WIN": win_t, "CW": W["conv_w"], "CBI": W["conv_b"],
        "WX": W["W_x"], "WDT": W["W_dt"], "DTB": W["dt_bias"],
        "ALOG": W["A_log"], "DD": W["D"],
        "WOUT": np.ascontiguousarray(W["W_out"].T),
        "W1P": w1p, "W3P": w3p, "W2P": w2p,
        "HCB": _hadamard_cb(DI),
    }


_NC_CACHE = {}


def kernel(L0_attn_hidden_states, L0_attn_W_in, L0_attn_conv_w,
           L0_attn_conv_b, L0_attn_W_x, L0_attn_W_dt, L0_attn_dt_bias,
           L0_attn_A_log, L0_attn_D, L0_attn_W_out, L0_ffn_w1, L0_ffn_w3,
           L0_ffn_w2):
    c = cfg_full()
    cd = derived(c)
    T, DM = cd["T"], cd["DM"]
    x = np.asarray(L0_attn_hidden_states, dtype=np.float32)
    W = dict(W_in=np.asarray(L0_attn_W_in, np.float32),
             conv_w=np.asarray(L0_attn_conv_w, np.float32),
             conv_b=np.asarray(L0_attn_conv_b, np.float32),
             W_x=np.asarray(L0_attn_W_x, np.float32),
             W_dt=np.asarray(L0_attn_W_dt, np.float32),
             dt_bias=np.asarray(L0_attn_dt_bias, np.float32),
             A_log=np.asarray(L0_attn_A_log, np.float32),
             D=np.asarray(L0_attn_D, np.float32),
             W_out=np.asarray(L0_attn_W_out, np.float32),
             w1=np.asarray(L0_ffn_w1, np.float32),
             w3=np.asarray(L0_ffn_w3, np.float32),
             w2=np.asarray(L0_ffn_w2, np.float32))

    if "full" not in _NC_CACHE:
        _NC_CACHE["full"] = build(c)
    nc = _NC_CACHE["full"]

    in_maps = [make_core_inputs(c, x, W, core // 2, core % 2)
               for core in range(8)]
    res = run_bass_kernel_spmd(nc, in_maps, core_ids=list(range(8)))
    out = np.empty((4, 2 * T, DM), dtype=np.float32)
    for core in range(8):
        b, h = core // 2, core % 2
        out[b, h * T:(h + 1) * T, :] = np.ascontiguousarray(
            res.results[core]["OUT"]).T
    return out


# version 25 (final)
# speedup vs baseline: 1.0704x; 1.0051x over previous
import math
import numpy as np
import ml_dtypes
import concourse.bass as bass
import concourse.mybir as mybir
from concourse.bass_utils import run_bass_kernel_spmd

F32 = mybir.dt.float32
BF16 = mybir.dt.bfloat16
AF = mybir.ActivationFunctionType
ALU = mybir.AluOpType
AX = mybir.AxisListType.X

NB = 512


def cfg_full():
    return dict(DM=2048, DI=4096, NS=16, RK=128, HID=5461, HIDP=5504, T=2048)


def cfg_small():
    return dict(DM=256, DI=512, NS=16, RK=128, HID=330, HIDP=384, T=512)


def derived(c):
    d = dict(c)
    d["TP"] = c["T"] + 3
    d["TPAD"] = ((d["TP"] + 127) // 128) * 128
    d["DMC"] = c["DM"] // 128
    d["DIC"] = c["DI"] // 128
    d["HC"] = c["HIDP"] // 128
    d["TB"] = c["T"] // NB
    d["EC"] = 2 * d["DIC"]
    d["HCH"] = (d["HC"] + 1) // 2
    d["EWX"] = c["RK"] + 2 * c["NS"]
    assert c["T"] % NB == 0 and c["DM"] % 128 == 0 and c["DI"] % 128 == 0
    assert c["HIDP"] % 128 == 0 and c["RK"] == 128 and 2 * c["NS"] <= 128
    return d


class _SemPool:
    def __init__(self, nc, n):
        self.nc = nc
        self.n_eng = n
        self.handles = [nc.alloc_semaphore(f"sp{i}") for i in range(n)]
        self.counts = [0] * n
        self.i = 0
        self.ndma = 0
        self.pending = set()

    def get(self):
        idx = self.i % self.n_eng
        self.i += 1
        return _PSem(self, idx)

    def get_dma(self):
        self.handles.append(self.nc.alloc_semaphore(f"dm{self.ndma}"))
        self.counts.append(0)
        self.ndma += 1
        return _PSem(self, len(self.handles) - 1)


class _PSem:
    def __init__(self, pool, idx):
        self.pool = pool
        self.idx = idx
        self.h = pool.handles[idx]
        self.base = pool.counts[idx]
        self.loc = 0

    def inc(self, instr, k=1):
        instr.then_inc(self.h, k)
        self.loc += k
        self.pool.counts[self.idx] = self.base + self.loc
        if k >= 16:
            self.pool.pending.add(self.idx)

    def tgt(self, local):
        return self.base + local


def build(c):
    c = derived(c)
    DM, DI, NS, RK, HID, HIDP = (c["DM"], c["DI"], c["NS"], c["RK"], c["HID"],
                                 c["HIDP"])
    T, TP, TPAD = c["T"], c["TP"], c["TPAD"]
    DMC, DIC, HC, TB, EC, HCH, EWX = (c["DMC"], c["DIC"], c["HC"], c["TB"],
                                      c["EC"], c["HCH"], c["EWX"])

    nc = bass.Bass("TRN2", target_bir_lowering=False)
    pool = _SemPool(nc, 32)

    XPAD = nc.dram_tensor("XPAD", [TPAD, DM], F32, kind="ExternalInput")
    XT = nc.dram_tensor("XT", [DM, T], F32, kind="ExternalInput")
    WIN = nc.dram_tensor("WIN", [DM, 2 * DI], F32, kind="ExternalInput")
    CW = nc.dram_tensor("CW", [DI, 4], F32, kind="ExternalInput")
    CBI = nc.dram_tensor("CBI", [DI], F32, kind="ExternalInput")
    WX = nc.dram_tensor("WX", [EWX, DI], F32, kind="ExternalInput")
    WDT = nc.dram_tensor("WDT", [DI, RK], F32, kind="ExternalInput")
    DTB = nc.dram_tensor("DTB", [DI], F32, kind="ExternalInput")
    ALOG = nc.dram_tensor("ALOG", [DI, NS], F32, kind="ExternalInput")
    DD = nc.dram_tensor("DD", [DI], F32, kind="ExternalInput")
    WOUT = nc.dram_tensor("WOUT", [DI, DM], F32, kind="ExternalInput")
    W1P = nc.dram_tensor("W1P", [DM, HIDP], F32, kind="ExternalInput")
    W3P = nc.dram_tensor("W3P", [DM, HIDP], F32, kind="ExternalInput")
    W2P = nc.dram_tensor("W2P", [HIDP, DM], F32, kind="ExternalInput")
    HCB = nc.dram_tensor("HCB", [DIC, DI, 128], BF16, kind="ExternalInput")
    OUT = nc.dram_tensor("OUT", [DM, T], F32, kind="ExternalOutput")

    XC = nc.dram_tensor("XCs", [DIC, 128, T], F32)
    XCB = nc.dram_tensor("XCBs", [DIC, 128, T], BF16)
    SZG = nc.dram_tensor("SZGs", [DIC, 128, T], F32)
    DLT = nc.dram_tensor("DLTs", [DIC, 128, T], F32)
    YG = nc.dram_tensor("YGs", [DIC, 128, T], BF16)
    GS = nc.dram_tensor("GSs", [HC, TB, 128, T // TB], BF16)
    FF = nc.dram_tensor("FFs", [DMC, TB, 128, T // TB], F32)
    W1T = nc.dram_tensor("W1Ts", [HC, 128, DMC * 128], BF16)
    W3T = nc.dram_tensor("W3Ts", [HC, 128, DMC * 128], BF16)
    W2T = nc.dram_tensor("W2Ts", [DMC, 128, HC * 128], BF16)
    HWQ = nc.dram_tensor("HWQs", [DMC, DIC, 128, 128], BF16)
    BCD = nc.dram_tensor("BCDs", [2 * NS, T], F32)
    BCD16 = nc.dram_tensor("BCD16s", [2 * NS, T], BF16)

    ident_bf_d = nc.inline_tensor(np.eye(128, dtype=ml_dtypes.bfloat16),
                                  "identbf")
    ident_ng_d = nc.inline_tensor(-np.eye(128, dtype=ml_dtypes.bfloat16),
                                  "identng")
    ident_f32_d = nc.inline_tensor(np.eye(128, dtype=np.float32), "identf32")
    ones_col_d = nc.inline_tensor(np.ones((128, 1), dtype=np.float32),
                                  "onescol")
    ones_row_d = nc.inline_tensor(np.ones((1, 128), dtype=np.float32),
                                  "onesrow")

    BASE = 16512
    TOPB = 229344
    off = [BASE]

    def nbytes_of(shape, dtype):
        return int(np.prod(shape[1:])) * (2 if dtype == BF16 else 4)

    def salloc(name, shape, dtype):
        at = ((off[0] + 31) // 32) * 32
        t = nc.alloc_sbuf_tensor_at(name, shape, dtype, offset=at,
                                    align_bytes=32)
        off[0] = at + nbytes_of(shape, dtype)
        return t

    ident_bf = salloc("identbf_sb", [128, 128], BF16)
    ident_ng = salloc("identng_sb", [128, 128], BF16)
    ident_f32 = salloc("identf32_sb", [128, 128], F32)
    a_sb = salloc("a_sb", [128, DIC * NS], F32)
    state = salloc("state_sb", [128, DIC * NS], F32)
    bc_sb = salloc("bc_sb", [32, T], F32)
    cw_sb = salloc("cw_sb", [128, DIC * 4], F32)
    cb_sb = salloc("cb_sb", [128, DIC], F32)
    db_sb = salloc("db_sb", [128, DIC], F32)
    d_sb = salloc("d_sb", [128, DIC], F32)
    dtr_sb = salloc("dtr_sb", [128, T], BF16)
    wxt_sb = salloc("wxt_sb", [128, DIC * EWX], BF16)
    wdtt_sb = salloc("wdtt_sb", [128, DI], BF16)
    xhalo_sb = salloc("xhalo_sb", [128, DIC * 3], F32)
    ones_col = salloc("ones_col_sb", [128, 1], F32)
    ones_row = salloc("ones_row_sb", [1, 128], F32)
    s1_col = salloc("s1_col", [128, 1], F32)
    s23_col = salloc("s23_col", [128, 1], F32)
    r1_col = salloc("r1_col", [128, 1], F32)
    r3_col = salloc("r3_col", [128, 1], F32)
    r2_col = salloc("r2_col", [128, 1], F32)
    sc_tmp = salloc("sc_tmp", [1, 8], F32)
    msums = salloc("msums", [128, 2 * HC + 4 * DMC + 8], F32)
    ln_cols = salloc("ln_cols", [128, 12], F32)
    BIG = ((off[0] + 31) // 32) * 32

    pA = nc.alloc_psum_tensor("pA", [128, 4 * NB], F32)
    pB = nc.alloc_psum_tensor("pB", [128, 4 * NB], F32)

    def big(name, shape, dtype, at):
        at = BIG + at
        assert at % 32 == 0, (name, at)
        t = nc.alloc_sbuf_tensor_at(name, shape, dtype, offset=at,
                                    align_bytes=32)
        assert at + nbytes_of(shape, dtype) <= TOPB, (
            name, at + nbytes_of(shape, dtype), TOPB)
        return t

    def bank(i):
        ps = pA if (i % 8) < 4 else pB
        return ps, ((i % 8) % 4) * NB

    def drain():
        if not pool.pending:
            return
        targets = [(pool.handles[i], pool.counts[i]) for i in pool.pending]
        pool.pending.clear()
        with nc.Block() as dblk:
            @dblk.sync
            def _(sync):
                for h, v in targets:
                    sync.wait_ge(h, v)

    class _Solo:
        def __init__(self, name):
            self.h = nc.alloc_semaphore(name)
            self.loc = 0

        def inc(self, instr, k=1):
            instr.then_inc(self.h, k)
            self.loc += k

        def tgt(self, local):
            return local

    VCH = _Solo("vch")
    SCH = _Solo("sch")

    def vfence(eng, s):
        eng.wait_ge(s.h, s.tgt(s.loc))

    s_c = pool.get_dma()
    with nc.Block() as blk:
        @blk.sync
        def _(sync):
            s_c.inc(sync.dma_start(ident_bf[:, :], ident_bf_d.ap()), 16)
            s_c.inc(sync.dma_start(ident_ng[:, :], ident_ng_d.ap()), 16)
            s_c.inc(sync.dma_start(ident_f32[:, :], ident_f32_d.ap()), 16)
            s_c.inc(sync.dma_start(ones_col[:, :], ones_col_d.ap()), 16)
            s_c.inc(sync.dma_start(ones_row[:, :], ones_row_d.ap()), 16)
            s_c.inc(sync.dma_start(
                a_sb[:, :].rearrange("p (g n) -> p g n", n=NS),
                ALOG.ap().rearrange("(g p) n -> p g n", p=128)), 16)
            s_c.inc(sync.dma_start(
                cw_sb[:, :].rearrange("p (g k) -> p g k", k=4),
                CW.ap().rearrange("(g p) k -> p g k", p=128)), 16)
            s_c.inc(sync.dma_start(
                cb_sb[:, :], CBI.ap().rearrange("(g p) -> p g", p=128),
                allow_slow_non_contiguous=True), 16)
            s_c.inc(sync.dma_start(
                db_sb[:, :], DTB.ap().rearrange("(g p) -> p g", p=128),
                allow_slow_non_contiguous=True), 16)
            s_c.inc(sync.dma_start(
                d_sb[:, :], DD.ap().rearrange("(g p) -> p g", p=128),
                allow_slow_non_contiguous=True), 16)

        @blk.scalar
        def _(scalar):
            scalar.wait_ge(s_c.h, s_c.tgt(10 * 16))
            scalar.activation(a_sb[:, :], a_sb[:, :], AF.Exp)

        @blk.gpsimd
        def _(gpsimd):
            gpsimd.memset(state[:, :], 0.0)

    with nc.Block() as blk:
        @blk.vector
        def _(vector):
            vector.tensor_scalar_mul(db_sb[:, :], db_sb[:, :], -1.0)

    drain()

    wot = big("wot", [128, DIC * DM], BF16, 0)
    WOTB = DIC * DM * 2

    s_l1 = pool.get_dma()
    with nc.Block() as blk:
        @blk.gpsimd
        def _(gpsimd):
            wv = WOUT.ap().rearrange("(j p) m -> p j m", p=128)
            wd = wot[:, :].rearrange("p (j m) -> p j m", m=DM)
            q1 = max(DIC // 4, 1)
            for k in range(0, DIC, q1):
                s_l1.inc(gpsimd.dma_start(wd[:, k:k + q1], wv[:, k:k + q1]),
                         16)

    drain()

    hcb_buf = [big(f"hcb_{i}", [128, DIC * 128], BF16,
                   WOTB + i * DIC * 128 * 2) for i in range(2)]
    hw_st = [big(f"hwst_{i}", [128, DM], BF16,
                 WOTB + 2 * DIC * 128 * 2 + i * DM * 2) for i in range(2)]

    s_l2 = [pool.get_dma(), pool.get_dma()]
    s_m2 = pool.get()
    s_e2 = pool.get()
    s_s2 = [pool.get_dma(), pool.get_dma()]
    dmb = [(s, min(NB, DM - s)) for s in range(0, DM, NB)]
    with nc.Block() as blk:
        @blk.sync
        def _(sync):
            for ci in range(DIC):
                if ci >= 2:
                    sync.wait_ge(s_m2.h, s_m2.tgt(ci - 1))
                s_l2[ci % 2].inc(sync.dma_start(
                    hcb_buf[ci % 2][:, :].rearrange("p (cc m) -> p cc m",
                                                    m=128),
                    HCB.ap()[ci].rearrange("(cc p) m -> p cc m", p=128)), 16)

        @blk.tensor
        def _(tensor):
            tensor.wait_ge(s_l1.h, s_l1.tgt(s_l1.loc))
            for ci in range(DIC):
                tensor.wait_ge(s_l2[ci % 2].h,
                               s_l2[ci % 2].tgt((ci // 2 + 1) * 16))
                if ci >= 2:
                    tensor.wait_ge(s_e2.h, s_e2.tgt(ci - 1))
                ps = pA if ci % 2 == 0 else pB
                last = None
                for cc in range(DIC):
                    st = hcb_buf[ci % 2][:, cc * 128:(cc + 1) * 128]
                    for (bs, bn) in dmb:
                        last = tensor.matmul(
                            ps[:, bs:bs + bn], st,
                            wot[:, cc * DM + bs:cc * DM + bs + bn],
                            start=(cc == 0), stop=(cc == DIC - 1))
                s_m2.inc(last)

        @blk.scalar
        def _(scalar):
            for ci in range(DIC):
                scalar.wait_ge(s_m2.h, s_m2.tgt(ci + 1))
                if ci >= 2:
                    scalar.wait_ge(s_s2[ci % 2].h,
                                   s_s2[ci % 2].tgt((ci - 2) // 2 * 16 + 16))
                ps = pA if ci % 2 == 0 else pB
                s_e2.inc(scalar.activation(hw_st[ci % 2][:, :], ps[:, :DM],
                                           AF.Copy))

        @blk.gpsimd
        def _(gpsimd):
            for ci in range(DIC):
                gpsimd.wait_ge(s_e2.h, s_e2.tgt(ci + 1))
                s_s2[ci % 2].inc(gpsimd.dma_start(
                    HWQ.ap()[:, ci].rearrange("cc p m -> p cc m"),
                    hw_st[ci % 2][:, :].rearrange("p (cc m) -> p cc m", m=128),
                ), 16)

    drain()

    xn_f = big("xn_f", [128, DMC * TPAD], BF16, 0)
    XNO = DMC * TPAD * 2
    QW = max(HIDP, DI, DIC * RK, DM)
    QFW = max(DM, 128 * ((HIDP // 128 + 2) // 3))
    QPW = max(QFW, DM)
    NQB = 4
    QF = [big(f"q_f32_{i}", [128, QFW], F32, XNO + i * QFW * 4)
          for i in range(NQB)]
    O1 = XNO + NQB * QFW * 4
    QPOS = [big(f"q_pos{i}", [128, QPW], BF16, O1 + i * QPW * 2)
            for i in range(3)]
    QNEG = [big(f"q_neg{i}", [128, QPW], BF16, O1 + (3 + i) * QPW * 2)
            for i in range(3)]
    TSW = max(HC * 128, DMC * 128, DIC * RK)
    TSTG = [big(f"tstg{i}", [128, TSW], BF16,
                O1 + 6 * QPW * 2 + i * ((TSW * 2 + 31) // 32) * 32)
            for i in range(2)]
    WXP1 = big("wxp1", [128, DI], F32, XNO)
    WXP2 = big("wxp2", [128, DI], F32, XNO + DI * 4)
    WDTH = big("wdth", [128, DIC * RK], F32,
               XNO + 2 * DI * 4 if 2 * DI * 4 + DIC * RK * 4 <= NQB * QFW * 4
               else O1)
    QEND = O1 + 6 * QPW * 2 + 2 * ((TSW * 2 + 31) // 32) * 32
    OVL = O1
    if OVL + 5 * DM * 4 > QEND:
        OVL = QEND
    x_tile = [big(f"x_t{i}", [128, DM], F32, OVL + i * DM * 4)
              for i in range(2)]
    xm_tile = big("xm_t", [128, DM], F32, OVL + 2 * DM * 4)
    xnt_tile = [big(f"xnt_{i}", [128, DM], F32, OVL + (3 + i) * DM * 4)
                for i in range(2)]

    NT = TPAD // 128
    s_l3 = [pool.get_dma(), pool.get_dma()]
    s_v3 = pool.get()
    s_a3 = pool.get()
    s_n3 = pool.get()
    s_m3 = pool.get()
    s_e3 = pool.get()
    egroups = [(j0, min(4, DMC - j0)) for j0 in range(0, DMC, 4)]
    NEV = len(egroups)
    B3_EV_TOTAL = NT * NEV

    qch = []
    for ch in range(HC):
        qch.append((W1P, W1T, ch, "1", r1_col))
    for ch in range(HC):
        qch.append((W3P, W3T, ch, "3", r3_col))
    for ch in range(HIDP // 128):
        qch.append((W2P, W2T, ch, "2", r2_col))
    NMEAN = len(qch)
    NSCHED = 2 * NMEAN
    FD = DM

    def qload_src(Ws, ch, kind):
        if kind == "2":
            return Ws.ap()[ch * 128:(ch + 1) * 128, :]
        v = Ws.ap().rearrange("(j p) h -> p j h", p=128)
        return v[:, :, ch * 128:(ch + 1) * 128]

    s_ql = [pool.get_dma() for _ in range(NQB)]
    s_qdone = pool.get()
    s_m4 = pool.get()
    s_v4 = pool.get()
    s_scl = pool.get()
    s_ts = [pool.get_dma(), pool.get_dma()]
    s_wx = pool.get_dma()
    s_wm = pool.get()
    s_we = pool.get()

    with nc.Block() as blk:
        @blk.sync
        def _(sync):
            for i in range(NT):
                if i >= 2:
                    sync.wait_ge(s_v3.h, s_v3.tgt((i - 1) * 2))
                s_l3[i % 2].inc(
                    sync.dma_start(x_tile[i % 2][:, :],
                                   XPAD.ap()[i * 128:(i + 1) * 128, :]), 16)
            for rep in range(2):
                for ci_, (Ws, Wd, ch, kind, rc) in enumerate(qch):
                    gidx = rep * NMEAN + ci_
                    if gidx >= NQB:
                        sync.wait_ge(s_qdone.h,
                                     s_qdone.tgt(gidx - NQB + 1))
                    dst = QF[gidx % NQB][:, :FD]
                    if kind != "2":
                        dst = dst.rearrange("p (j h) -> p j h", h=128)
                    s_ql[gidx % NQB].inc(
                        sync.dma_start(dst, qload_src(Ws, ch, kind)), 16)
            sync.wait_ge(s_qdone.h, s_qdone.tgt(NSCHED))
            s_wx.inc(sync.dma_start(WXP1[:, :], WX.ap()[0:128, :]), 16)
            s_wx.inc(sync.dma_start(WXP2[0:EWX - 128, :],
                                    WX.ap()[128:EWX, :]), 16)
            s_wx.inc(sync.dma_start(
                WDTH[:, :].rearrange("p (g r) -> p g r", r=RK),
                WDT.ap().rearrange("(g p) r -> p g r", p=128)), 16)

        @blk.vector
        def _(vector):
            for i in range(NT):
                p4 = (i % 2) * 4
                xt = x_tile[i % 2]
                vector.wait_ge(s_l3[i % 2].h,
                               s_l3[i % 2].tgt((i // 2 + 1) * 16))
                if i >= 2:
                    vector.wait_ge(s_a3.h, s_a3.tgt(i - 1))
                VCH.inc(vector.tensor_reduce(ln_cols[:, p4:p4 + 1], xt[:, :],
                                             AX, ALU.add))
                vfence(vector, VCH)
                VCH.inc(vector.tensor_scalar_mul(ln_cols[:, p4 + 1:p4 + 2],
                                                 ln_cols[:, p4:p4 + 1],
                                                 1.0 / DM))
                vfence(vector, VCH)
                s_v3.inc(vector.tensor_scalar(xm_tile[:, :], xt[:, :],
                                              ln_cols[:, p4 + 1:p4 + 2], None,
                                              ALU.subtract))
                vfence(vector, s_v3)
                VCH.inc(vector.scalar_tensor_tensor(
                    x_tile[i % 2][:, :], xm_tile[:, :], 1.0, xm_tile[:, :],
                    ALU.mult, ALU.mult, accum_out=ln_cols[:, p4 + 2:p4 + 3]))
                vfence(vector, VCH)
                s_v3.inc(vector.tensor_scalar(ln_cols[:, p4 + 3:p4 + 4],
                                              ln_cols[:, p4 + 2:p4 + 3],
                                              1.0 / DM, 1e-5,
                                              ALU.mult, ALU.add))
                vector.wait_ge(s_a3.h, s_a3.tgt(i + 1))
                if i >= 2:
                    vector.wait_ge(s_m3.h, s_m3.tgt((i - 1) * DMC))
                s_n3.inc(vector.tensor_scalar(xnt_tile[i % 2][:, :],
                                              xm_tile[:, :],
                                              ln_cols[:, 8 + i % 2:9 + i % 2],
                                              None, ALU.mult))
            for gidx in range(NMEAN):
                vector.wait_ge(s_ql[gidx % NQB].h,
                               s_ql[gidx % NQB].tgt((gidx // NQB + 1) * 16))
                s_qdone.inc(vector.tensor_reduce(
                    msums[:, gidx:gidx + 1], QF[gidx % NQB][:, :FD], AX,
                    ALU.add, apply_absolute_value=True))
            vfence(vector, s_qdone)
            base = 0
            last = None
            for wi, nch in enumerate([HC, HC, HIDP // 128]):
                last = vector.tensor_reduce(
                    msums[:, NMEAN + wi:NMEAN + wi + 1],
                    msums[:, base:base + nch], AX, ALU.add)
                base += nch
            s_v4.inc(last)
            vector.wait_ge(s_m4.h, s_m4.tgt(1))
            VCH.inc(vector.tensor_scalar_mul(sc_tmp[0:1, 0:3], pA[0:1, 0:3],
                                             1.0 / (HID * DM)))
            vfence(vector, VCH)
            VCH.inc(vector.tensor_mul(sc_tmp[0:1, 3:4], sc_tmp[0:1, 1:2],
                                      sc_tmp[0:1, 2:3]))
            vfence(vector, VCH)
            s_v4.inc(vector.reciprocal(sc_tmp[0:1, 4:7], sc_tmp[0:1, 0:3]))
            vector.wait_ge(s_e3.h, s_e3.tgt(B3_EV_TOTAL))
            vector.wait_ge(s_scl.h, s_scl.tgt(1))
            for qi, (Ws, Wd, ch, kind, rc) in enumerate(qch):
                gidx = NMEAN + qi
                vector.wait_ge(s_ql[gidx % NQB].h,
                               s_ql[gidx % NQB].tgt((gidx // NQB + 1) * 16))
                if qi >= 2:
                    pp = qi % 2
                    vector.wait_ge(s_ts[pp].h,
                                   s_ts[pp].tgt(((qi - 2) // 2 + 1) * 16))
                if qi >= 3:
                    vector.wait_ge(s_qdone.h, s_qdone.tgt(NMEAN + qi - 2))
                VCH.inc(vector.tensor_scalar(QPOS[qi % 3][:, :FD],
                                             QF[gidx % NQB][:, :FD],
                                             rc[:, 0:1], 0.5,
                                             ALU.mult, ALU.is_gt))
                VCH.inc(vector.tensor_scalar(QNEG[qi % 3][:, :FD],
                                             QF[gidx % NQB][:, :FD],
                                             rc[:, 0:1], -0.5,
                                             ALU.mult, ALU.is_lt))
                vfence(vector, VCH)
                s_qdone.inc(vector.tensor_sub(TSTG[qi % 2][:, :FD],
                                              QPOS[qi % 3][:, :FD],
                                              QNEG[qi % 3][:, :FD]))
            nb_ = 0
            for g in range(DIC):
                for part in range(3):
                    ps, sl = bank(nb_ % 4)
                    vector.wait_ge(s_wm.h, s_wm.tgt(nb_ + 1))
                    if part == 0:
                        s_we.inc(vector.tensor_copy(
                            wxt_sb[:, g * EWX:g * EWX + 128],
                            ps[:, sl:sl + 128]))
                    elif part == 1:
                        s_we.inc(vector.tensor_copy(
                            wxt_sb[:, g * EWX + 128:(g + 1) * EWX],
                            ps[:, sl:sl + EWX - 128]))
                    else:
                        s_we.inc(vector.tensor_copy(
                            wdtt_sb[:, g * 128:(g + 1) * 128],
                            ps[:, sl:sl + 128]))
                    nb_ += 1

        @blk.scalar
        def _(scalar):
            def evac3(i):
                for gi_, (j0, jn) in enumerate(egroups):
                    ev = i * NEV + gi_
                    sl0 = (ev % 2) * NB
                    scalar.wait_ge(s_m3.h, s_m3.tgt(i * DMC + j0 + jn))
                    dst = xn_f[:, :].rearrange("p (j t) -> p j t", t=TPAD)
                    s_e3.inc(scalar.activation(
                        dst[:, j0:j0 + jn, i * 128:(i + 1) * 128],
                        pB[:, sl0:sl0 + jn * 128].rearrange(
                            "p (j m) -> p j m", m=128),
                        AF.Copy))

            for i in range(NT):
                p4 = (i % 2) * 4
                scalar.wait_ge(s_v3.h, s_v3.tgt(i * 2 + 2))
                SCH.inc(scalar.activation(ln_cols[:, 10:11],
                                          ln_cols[:, p4 + 3:p4 + 4], AF.Ln))
                vfence(scalar, SCH)
                s_a3.inc(scalar.activation(ln_cols[:, 8 + i % 2:9 + i % 2],
                                           ln_cols[:, 10:11], AF.Exp,
                                           scale=-0.5))
                if i >= 1:
                    evac3(i - 1)
            evac3(NT - 1)
            scalar.wait_ge(s_m4.h, s_m4.tgt(2))
            scalar.activation(s1_col[:, :], pB[:, 2 * NB:2 * NB + 1], AF.Copy)
            scalar.activation(s23_col[:, :], pB[:, 2 * NB + 1:2 * NB + 2],
                              AF.Copy)
            scalar.activation(r1_col[:, :], pB[:, 2 * NB + 2:2 * NB + 3],
                              AF.Copy)
            scalar.activation(r3_col[:, :], pB[:, 2 * NB + 3:2 * NB + 4],
                              AF.Copy)
            s_scl.inc(scalar.activation(r2_col[:, :],
                                        pB[:, 2 * NB + 4:2 * NB + 5],
                                        AF.Copy))

        @blk.tensor
        def _(tensor):
            for i in range(NT):
                tensor.wait_ge(s_n3.h, s_n3.tgt(i + 1))
                for gi_, (j0, jn) in enumerate(egroups):
                    ev = i * NEV + gi_
                    sl0 = (ev % 2) * NB
                    if ev >= 2:
                        tensor.wait_ge(s_e3.h, s_e3.tgt(ev - 1))
                    for j in range(j0, j0 + jn):
                        s_m3.inc(tensor.transpose(
                            pB[:, sl0 + (j - j0) * 128:sl0 + (j - j0 + 1) * 128],
                            xnt_tile[i % 2][:, j * 128:(j + 1) * 128],
                            ident_f32[:, :]))
            tensor.wait_ge(s_v4.h, s_v4.tgt(1))
            s_m4.inc(tensor.matmul(pA[0:1, 0:3], ones_col[:, :],
                                   msums[:, NMEAN:NMEAN + 3],
                                   start=True, stop=True))
            tensor.wait_ge(s_v4.h, s_v4.tgt(2))
            last = None
            for k, src_i in enumerate([0, 3, 4, 5, 6]):
                last = tensor.matmul(pB[:, 2 * NB + k:2 * NB + k + 1],
                                     ones_row[:, :],
                                     sc_tmp[0:1, src_i:src_i + 1],
                                     start=True, stop=True)
            s_m4.inc(last)
            tensor.wait_ge(s_wx.h, s_wx.tgt(48))
            nb_ = 0
            for g in range(DIC):
                for part in range(3):
                    ps, sl = bank(nb_ % 4)
                    if nb_ >= 4:
                        tensor.wait_ge(s_we.h, s_we.tgt(nb_ - 3))
                    if part == 0:
                        s_wm.inc(tensor.transpose(
                            ps[:, sl:sl + 128],
                            WXP1[:, g * 128:(g + 1) * 128], ident_f32[:, :]))
                    elif part == 1:
                        s_wm.inc(tensor.transpose(
                            ps[:, sl:sl + EWX - 128],
                            WXP2[0:EWX - 128, g * 128:(g + 1) * 128],
                            ident_f32[0:EWX - 128, 0:EWX - 128]))
                    else:
                        s_wm.inc(tensor.transpose(
                            ps[:, sl:sl + 128],
                            WDTH[:, g * RK:(g + 1) * RK], ident_f32[:, :]))
                    nb_ += 1

        @blk.gpsimd
        def _(gpsimd):
            for qi, (Ws, Wd, ch, kind, rc) in enumerate(qch):
                gpsimd.wait_ge(s_qdone.h, s_qdone.tgt(NMEAN + qi + 1))
                if kind == "2":
                    dst = Wd.ap()[:, :, ch * 128:(ch + 1) * 128].rearrange(
                        "c p m -> p c m")
                    srcv = TSTG[qi % 2][:, :FD].rearrange(
                        "p (c m) -> p c m", m=128)
                    s_ts[qi % 2].inc(gpsimd.dma_start(dst, srcv), 16)
                else:
                    s_ts[qi % 2].inc(gpsimd.dma_start(
                        Wd.ap()[ch], TSTG[qi % 2][:, :FD]), 16)

    drain()

    drain()

    st_buf = [big(f"p3st_{i}", [128, DMC * 128], BF16, XNO + i * DMC * 128 * 2)
              for i in range(2)]
    P3O = XNO + 2 * DMC * 128 * 2
    XBW = ((TP + 7) // 8) * 8
    xbuf = [big(f"p3xb_{i}", [128, XBW], F32, P3O + i * XBW * 4)
            for i in range(2)]
    P3O2 = P3O + 2 * XBW * 4
    xcacc = [big(f"p3acc_{i}", [128, T], F32, P3O2 + i * T * 4)
             for i in range(2)]
    xcf = [big(f"p3xc_{i}", [128, T], F32, P3O2 + (2 + i) * T * 4)
           for i in range(2)]
    xcb = [big(f"p3xcb_{i}", [128, T], BF16, P3O2 + 4 * T * 4 + i * T * 2)
           for i in range(2)]
    zst = [big(f"p3z_{i}", [128, T], F32,
               P3O2 + 4 * T * 4 + 2 * T * 2 + i * T * 4) for i in range(2)]

    s_l6 = [pool.get_dma(), pool.get_dma()]
    s_m6 = pool.get()
    s_f6 = pool.get()
    s_cv6 = pool.get()
    s_si6 = pool.get()
    s_st6x = [pool.get_dma(), pool.get_dma()]
    s_st6z = [pool.get_dma(), pool.get_dma()]
    s_h6 = pool.get()
    s_hv6 = pool.get()
    with nc.Block() as blk:
        @blk.tensor
        def _(tensor):
            xnr = xn_f[:, :].rearrange("p (j t) -> p j t", t=TPAD)
            for ci in range(EC):
                tensor.wait_ge(s_l6[ci % 2].h,
                               s_l6[ci % 2].tgt((ci // 2 + 1) * 16))
                if ci >= 2:
                    tensor.wait_ge(s_f6.h, s_f6.tgt(ci - 1))
                if 1 <= ci <= DIC:
                    tensor.wait_ge(s_hv6.h, s_hv6.tgt(ci))
                ps = pA if ci % 2 == 0 else pB
                last = None
                for j in range(DMC):
                    st = st_buf[ci % 2][:, j * 128:(j + 1) * 128]
                    for b in range(TB):
                        last = tensor.matmul(
                            ps[:, b * NB:(b + 1) * NB], st,
                            xnr[:, j, 3 + b * NB:3 + (b + 1) * NB],
                            start=(j == 0), stop=(j == DMC - 1))
                s_m6.inc(last)
                if ci < DIC:
                    po = pB if ci % 2 == 0 else pA
                    if ci >= 1:
                        tensor.wait_ge(s_f6.h, s_f6.tgt(ci))
                    if ci >= 2:
                        tensor.wait_ge(s_hv6.h, s_hv6.tgt(ci - 1))
                    lh = None
                    for j in range(DMC):
                        lh = tensor.matmul(
                            po[:, 0:3], st_buf[ci % 2][:, j * 128:(j + 1) * 128],
                            xnr[:, j, 0:3],
                            start=(j == 0), stop=(j == DMC - 1))
                    s_h6.inc(lh)

        @blk.scalar
        def _(scalar):
            nz = 0
            for ci in range(EC):
                scalar.wait_ge(s_m6.h, s_m6.tgt(ci + 1))
                ps = pA if ci % 2 == 0 else pB
                if ci < DIC:
                    if ci >= 2:
                        scalar.wait_ge(s_cv6.h, s_cv6.tgt(ci - 1))
                    scalar.wait_ge(s_h6.h, s_h6.tgt(ci + 1))
                    po = pB if ci % 2 == 0 else pA
                    s_hv6.inc(scalar.activation(xbuf[ci % 2][:, 0:3],
                                                po[:, 0:3], AF.Copy))
                    s_f6.inc(scalar.activation(xbuf[ci % 2][:, 3:3 + T],
                                               ps[:, :T], AF.Copy))
                    scalar.wait_ge(s_cv6.h, s_cv6.tgt(ci + 1))
                    if ci >= 2:
                        scalar.wait_ge(s_st6x[ci % 2].h,
                                       s_st6x[ci % 2].tgt((ci - 2) // 2 * 32
                                                          + 32))
                    scalar.activation(xcf[ci % 2][:, :], xcacc[ci % 2][:, :],
                                      AF.Silu, bias=cb_sb[:, ci:ci + 1])
                    s_si6.inc(scalar.activation(xcb[ci % 2][:, :],
                                                xcacc[ci % 2][:, :], AF.Silu,
                                                bias=cb_sb[:, ci:ci + 1]))
                else:
                    if nz >= 2:
                        scalar.wait_ge(s_st6z[nz % 2].h,
                                       s_st6z[nz % 2].tgt((nz - 2) // 2 * 16
                                                          + 16))
                    s_f6.inc(scalar.activation(zst[nz % 2][:, :], ps[:, :T],
                                               AF.Silu))
                    nz += 1

        @blk.vector
        def _(vector):
            for ci in range(DIC):
                vector.wait_ge(s_f6.h, s_f6.tgt(ci + 1))
                if ci >= 2:
                    vector.wait_ge(s_si6.h, s_si6.tgt(ci - 1))
                xb = xbuf[ci % 2]
                acc = xcacc[ci % 2]
                VCH.inc(vector.tensor_scalar_mul(acc[:, :], xb[:, 0:T],
                                                 cw_sb[:, ci * 4:ci * 4 + 1]))
                for k in range(1, 4):
                    vfence(vector, VCH)
                    st_ = vector.scalar_tensor_tensor(
                        acc[:, :], xb[:, k:k + T],
                        cw_sb[:, ci * 4 + k:ci * 4 + k + 1],
                        acc[:, :], ALU.mult, ALU.add)
                    if k < 3:
                        VCH.inc(st_)
                    else:
                        s_cv6.inc(st_)

        @blk.gpsimd
        def _(gpsimd):
            winv6 = WIN.ap().rearrange("(j p) e -> p j e", p=128)
            for ci in range(min(2, EC)):
                s_l6[ci % 2].inc(gpsimd.dma_start(
                    st_buf[ci % 2][:, :].rearrange("p (j m) -> p j m", m=128),
                    winv6[:, :, ci * 128:(ci + 1) * 128]), 16)
            nz = 0
            for ci in range(EC):
                if ci + 2 < EC:
                    cl = ci + 2
                    gpsimd.wait_ge(s_m6.h, s_m6.tgt(cl - 1))
                    if cl - 2 < DIC:
                        gpsimd.wait_ge(s_h6.h, s_h6.tgt(cl - 1))
                    s_l6[cl % 2].inc(gpsimd.dma_start(
                        st_buf[cl % 2][:, :].rearrange("p (j m) -> p j m",
                                                       m=128),
                        winv6[:, :, cl * 128:(cl + 1) * 128]), 16)
                if ci < DIC:
                    gpsimd.wait_ge(s_si6.h, s_si6.tgt(ci + 1))
                    s_st6x[ci % 2].inc(
                        gpsimd.dma_start(XC.ap()[ci], xcf[ci % 2][:, :]), 16)
                    s_st6x[ci % 2].inc(
                        gpsimd.dma_start(XCB.ap()[ci], xcb[ci % 2][:, :]), 16)
                else:
                    gpsimd.wait_ge(s_f6.h, s_f6.tgt(ci + 1))
                    s_st6z[nz % 2].inc(
                        gpsimd.dma_start(SZG.ap()[ci - DIC],
                                         zst[nz % 2][:, :]), 16)
                    nz += 1

    drain()

    xcb_buf = [big(f"p4x_{i}", [128, T], BF16, XNO + i * T * 2)
               for i in range(2)]

    s_l9 = [pool.get_dma(), pool.get_dma()]
    s_m9 = pool.get()
    s_bc9 = pool.get_dma()
    with nc.Block() as blk:
        @blk.sync
        def _(sync):
            for g in range(DIC):
                if g >= 2:
                    sync.wait_ge(s_m9.h, s_m9.tgt(g - 1))
                s_l9[g % 2].inc(
                    sync.dma_start(xcb_buf[g % 2][:, :], XCB.ap()[g]), 16)

        @blk.tensor
        def _(tensor):
            for g in range(DIC):
                tensor.wait_ge(s_l9[g % 2].h,
                               s_l9[g % 2].tgt((g // 2 + 1) * 16))
                st1 = wxt_sb[:, g * EWX:g * EWX + 128]
                st2 = wxt_sb[:, g * EWX + 128:(g + 1) * EWX]
                last = None
                for b in range(TB):
                    tensor.matmul(pA[:, b * NB:(b + 1) * NB], st1,
                                  xcb_buf[g % 2][:, b * NB:(b + 1) * NB],
                                  start=(g == 0), stop=(g == DIC - 1))
                    last = tensor.matmul(
                        pB[0:2 * NS, b * NB:(b + 1) * NB], st2,
                        xcb_buf[g % 2][:, b * NB:(b + 1) * NB],
                        start=(g == 0), stop=(g == DIC - 1))
                s_m9.inc(last)

        @blk.scalar
        def _(scalar):
            scalar.wait_ge(s_m9.h, s_m9.tgt(DIC))
            scalar.activation(dtr_sb[:, :], pA[:, :T], AF.Copy)

        @blk.vector
        def _(vector):
            vector.wait_ge(s_m9.h, s_m9.tgt(DIC))
            s_m9.inc(vector.tensor_copy(bc_sb[:, :], pB[0:2 * NS, :T]))

        @blk.gpsimd
        def _(gpsimd):
            gpsimd.wait_ge(s_m9.h, s_m9.tgt(DIC + 1))
            s_bc9.inc(gpsimd.dma_start(BCD.ap(), bc_sb[0:2 * NS, :]), 16)
            s_bc9.inc(gpsimd.dma_start(BCD16.ap(), bc_sb[0:2 * NS, :]), 16)

    drain()

    dlt_stg = [big(f"p5d_{i}", [128, T], F32, XNO + 2 * T * 2 + i * T * 4)
               for i in range(2)]
    sg_stg = big("p5sg", [128, T], F32, XNO + 2 * T * 2 + 2 * T * 4)
    s_m10 = pool.get()
    s_e10 = pool.get()
    s_s10 = [pool.get_dma(), pool.get_dma()]
    with nc.Block() as blk:
        @blk.tensor
        def _(tensor):
            for g in range(DIC):
                if g >= 2:
                    tensor.wait_ge(s_e10.h, s_e10.tgt(g - 1))
                ps = pA if g % 2 == 0 else pB
                st = wdtt_sb[:, g * 128:(g + 1) * 128]
                last = None
                for b in range(TB):
                    last = tensor.matmul(ps[:, b * NB:(b + 1) * NB], st,
                                         dtr_sb[:, b * NB:(b + 1) * NB],
                                         start=True, stop=True)
                s_m10.inc(last)

        @blk.scalar
        def _(scalar):
            for g in range(DIC):
                scalar.wait_ge(s_m10.h, s_m10.tgt(g + 1))
                if g >= 2:
                    scalar.wait_ge(s_s10[g % 2].h,
                                   s_s10[g % 2].tgt((g - 2) // 2 * 16 + 16))
                ps = pA if g % 2 == 0 else pB
                vfence(scalar, s_e10)
                SCH.inc(scalar.activation(sg_stg[:, :], ps[:, :T],
                                          AF.Sigmoid, bias=db_sb[:, g:g + 1],
                                          scale=-1.0))
                vfence(scalar, SCH)
                s_e10.inc(scalar.activation(dlt_stg[g % 2][:, :],
                                            sg_stg[:, :], AF.Ln))

        @blk.sync
        def _(sync):
            for g in range(DIC):
                sync.wait_ge(s_e10.h, s_e10.tgt(g + 1))
                s_s10[g % 2].inc(
                    sync.dma_start(DLT.ap()[g], dlt_stg[g % 2][:, :]), 16)

    drain()

    assert DIC % 2 == 0
    TH1 = T // 4
    NU1 = 4 * HC
    CI0 = (HC + 1) // 2
    NU2 = TB * DMC
    NG = TB * DIC
    F1_END = (NU1 - 1) // 2 + 3
    def sf1(u):
        return min(u // 2, NG - 1)

    slot_f1 = [[] for _ in range(NG)]
    for u in range(NU1):
        slot_f1[sf1(u)].append(u)
    slot_f2 = [[] for _ in range(NG)]
    for v in range(NU2):
        slot_f2[min(F1_END + 2 + v // 2, NG - 1)].append(v)
    slot_w1 = [[] for _ in range(NG)]
    slot_w3 = [[] for _ in range(NG)]
    for ch in range(2, HC):
        slot_w1[max(sf1(4 * ch) - 1, 0)].append(ch)
        slot_w3[min(sf1(4 * ch) + 1, NG - 1)].append(ch)
    def sislot(u):
        return min(sf1(u) + 1, NG - 1)

    def h3slot(u):
        return min(sf1(u) + 2, NG - 1)

    def gmslot(u):
        return min(sf1(u) + 3, NG - 1)

    def evslot(v):
        return min(F1_END + 4 + v // 2, NG - 1)

    def f2slot(v):
        return min(F1_END + 2 + v // 2, NG - 1)

    slot_gload = [[] for _ in range(NG)]
    for tc in range(TB):
        slot_gload[f2slot(tc * DMC)].append(tc)
    slot_w2load = [[] for _ in range(NG)]
    for v in range(NU2):
        slot_w2load[f2slot(v)].append(v)

    slot_si = [[] for _ in range(NG)]
    for u in range(NU1):
        slot_si[sislot(u)].append(u)
    slot_h3 = [[] for _ in range(NG)]
    for u in range(NU1):
        slot_h3[h3slot(u)].append(u)
    slot_gm = [[] for _ in range(NG)]
    for u in range(NU1):
        slot_gm[gmslot(u)].append(u)
    slot_gst = slot_gm
    slot_ev2 = [[] for _ in range(NG)]
    for v in range(NU2):
        slot_ev2[evslot(v)].append(v)

    def gm_pre_(u):
        return sislot(u) < gmslot(u) and h3slot(u) < gmslot(u)

    def h3_pre(u):
        if u < 2:
            return True
        gs = gmslot(u - 2)
        return gs < h3slot(u) or (gs == h3slot(u) and gm_pre_(u - 2))

    def si_pre(u):
        if sf1(u) >= sislot(u):
            return False
        if u < 8:
            return True
        gs = gmslot(u - 8)
        return gs < sislot(u) or (gs == sislot(u) and gm_pre_(u - 8))

    def gm_pre(u):
        return gm_pre_(u)

    def ev_pre(v):
        return f2slot(v) < evslot(v)

    def h1_pre(u):
        if u < 2:
            return True
        s = sf1(u)
        return sislot(u - 2) < s or (sislot(u - 2) == s and si_pre(u - 2))

    F1B = XNO
    w1t_buf = [big(f"f1a_{i}", [128, DMC * 128], BF16, F1B + i * DMC * 256)
               for i in range(2)]
    w3t_buf = [big(f"f1b_{i}", [128, DMC * 128], BF16,
                   F1B + (2 + i) * DMC * 256) for i in range(2)]
    h1s = [big(f"f1h_{i}", [128, TH1], BF16,
               F1B + 4 * DMC * 256 + i * TH1 * 2) for i in range(8)]
    g_stg = [big(f"f1g_{i}", [128, TH1], BF16,
                 F1B + 4 * DMC * 256 + (8 + i) * TH1 * 2) for i in range(2)]
    F1SZ = 4 * DMC * 256 + 10 * TH1 * 2
    SRO = XNO + F1SZ
    Bb = big("p6bb", [128, NS * NB], BF16, SRO)
    Cb = big("p6cb", [128, NS * NB], BF16, SRO + NS * NB * 2)
    P6O = SRO + 2 * NS * NB * 2
    dlt_b = [big(f"p6dl_{i}", [128, NB], F32, P6O + i * NB * 4)
             for i in range(2)]
    xc_b = [big(f"p6xc_{i}", [128, NB], F32, P6O + (2 + i) * NB * 4)
            for i in range(2)]
    szg_b = [big(f"p6sz_{i}", [128, NB], F32, P6O + (4 + i) * NB * 4)
             for i in range(2)]
    t2_b = [big(f"p6t2_{i}", [128, NB], F32, P6O + (6 + i) * NB * 4)
            for i in range(2)]
    e_b = [big(f"p6e_{i}", [128, NB], F32, P6O + (8 + i) * NB * 4)
           for i in range(3)]
    s_bK = [big(f"p6s_{i}", [128, NB], BF16, P6O + 11 * NB * 4 + i * NB * 2)
            for i in range(3)]
    dtu_b = [big(f"p6du_{i}", [128, NB], BF16,
                 P6O + 11 * NB * 4 + (3 + i) * NB * 2) for i in range(2)]
    dbu_b = [big(f"p6db_{i}", [128, NB], BF16,
                 P6O + 11 * NB * 4 + (5 + i) * NB * 2) for i in range(2)]
    cs_b = [big(f"p6cs_{i}", [128, NB], BF16,
                P6O + 11 * NB * 4 + (7 + i) * NB * 2)
            for i in range(8)]
    yg_stg = [big(f"p6yg_{i}", [128, NB], BF16,
                  P6O + 11 * NB * 4 + (15 + i) * NB * 2)
              for i in range(2)]
    SEND = P6O + 11 * NB * 4 + 17 * NB * 2
    F2SZ = HC * NB * 2 + 2 * CI0 * 128 * 2 + 2 * NB * 4
    F2B = 0 if F2SZ <= XNO else SEND
    g_res = big("f2g", [128, HC * NB], BF16, F2B)
    w2t_buf = [big(f"f2w_{i}", [128, CI0 * 128], BF16,
                   F2B + HC * NB * 2 + i * CI0 * 128 * 2) for i in range(2)]
    f_stg = [big(f"f2f_{i}", [128, NB], F32,
                 F2B + HC * NB * 2 + 2 * CI0 * 128 * 2 + i * NB * 4)
             for i in range(2)]

    s_bb = [pool.get_dma() for _ in range(4)]
    s_bc = [pool.get_dma() for _ in range(4)]
    s_ldq = [pool.get_dma(), pool.get_dma()]
    s_ee = pool.get()
    s_sc = pool.get()
    s_cs = pool.get()
    s_im = pool.get()
    s_gc = pool.get()
    s_gt = pool.get()
    s_sy = [pool.get_dma(), pool.get_dma()]
    s_fw = [pool.get_dma(), pool.get_dma()]
    s_fw3 = [pool.get_dma(), pool.get_dma()]
    s_m7 = pool.get()
    s_h3 = pool.get()
    s_a7 = pool.get()
    s_g7 = pool.get()
    s_gs = [pool.get_dma(), pool.get_dma()]
    s_fg = pool.get_dma()
    s_f2w = [pool.get_dma(), pool.get_dma()]
    s_f2m = pool.get()
    s_f2e = pool.get()
    s_ff = [pool.get_dma(), pool.get_dma()]

    def h_blocks(W):
        return [(s, min(NB, W - s)) for s in range(0, W, NB)]

    H3O = max(TH1, NB)

    NCH6 = TB
    with nc.Block() as blk:
        @blk.gpsimd
        def _(gpsimd):
            for cc in range(NCH6):
                if cc >= 1:
                    gpsimd.wait_ge(s_cs.h, s_cs.tgt(cc * DIC * NS))
                for n in range(NS):
                    s_bb[n * 4 // NS].inc(gpsimd.dma_start(
                        Bb[:, n * NB:(n + 1) * NB],
                        BCD.ap()[n:n + 1,
                                 cc * NB:(cc + 1) * NB].broadcast_to(
                            [128, NB])), 16)
                for g in range(DIC):
                    gi = cc * DIC + g

                    def gs_store(u):
                        ch, qt = u // 4, u % 4
                        gpsimd.wait_ge(s_g7.h, s_g7.tgt(u + 1))
                        gsv = GS.ap()[ch].rearrange("q p t -> p q t")
                        if TB % 4 == 0:
                            nq = TB // 4
                            dst = gsv[:, qt * nq:(qt + 1) * nq]
                        else:
                            nq = 1
                            dst = gsv[:, 0:1, qt * TH1:(qt + 1) * TH1]
                        s_gs[u % 2].inc(gpsimd.dma_start(
                            dst,
                            g_stg[u % 2][:, :].rearrange(
                                "p (q t) -> p q t", q=nq)), 16)

                    def ff_store(v):
                        tc, c2 = v // DMC, v % DMC
                        gpsimd.wait_ge(s_f2e.h, s_f2e.tgt(v + 1))
                        s_ff[v % 2].inc(gpsimd.dma_start(FF.ap()[c2, tc],
                                                         f_stg[v % 2][:, :]),
                                        16)

                    for u in slot_gst[gi]:
                        if gm_pre(u):
                            gs_store(u)
                    for v in slot_ev2[gi]:
                        if ev_pre(v):
                            ff_store(v)
                    for n in range(NS):
                        k = gi * NS + n
                        gpsimd.wait_ge(s_sc.h, s_sc.tgt(k + 1))
                        s_gc.inc(gpsimd.tensor_copy(
                            state[:, g * NS + n:g * NS + n + 1],
                            s_bK[k % 3][:, NB - 1:NB]))
                    for u in slot_gst[gi]:
                        if not gm_pre(u):
                            gs_store(u)
                    for v in slot_ev2[gi]:
                        if not ev_pre(v):
                            ff_store(v)

        @blk.sync
        def _(sync):
            def store_yg(pg):
                sync.wait_ge(s_gt.h, s_gt.tgt(pg + 1))
                s_sy[pg % 2].inc(sync.dma_start(
                    YG.ap()[pg % DIC, :,
                            (pg // DIC) * NB:(pg // DIC + 1) * NB],
                    yg_stg[pg % 2][:, :]), 16)

            for ch in range(min(2, HC)):
                s_fw[ch % 2].inc(sync.dma_start(w1t_buf[ch % 2][:, :],
                                                W1T.ap()[ch]), 16)
                s_fw3[ch % 2].inc(sync.dma_start(w3t_buf[ch % 2][:, :],
                                                 W3T.ap()[ch]), 16)
            for cc in range(NCH6):
                for g in range(DIC):
                    gi = cc * DIC + g
                    if gi >= 2:
                        sync.wait_ge(s_gt.h, s_gt.tgt(gi - 1))
                    s_ldq[g % 2].inc(sync.dma_start(
                        dlt_b[g % 2][:, :],
                        DLT.ap()[g, :, cc * NB:(cc + 1) * NB]), 16)
                    s_ldq[g % 2].inc(sync.dma_start(
                        xc_b[g % 2][:, :],
                        XC.ap()[g, :, cc * NB:(cc + 1) * NB]), 16)
                    s_ldq[g % 2].inc(sync.dma_start(
                        szg_b[g % 2][:, :],
                        SZG.ap()[g, :, cc * NB:(cc + 1) * NB]), 16)
                    if gi >= 1:
                        store_yg(gi - 1)
                    for ch in slot_w1[gi]:
                        sync.wait_ge(s_m7.h, s_m7.tgt(4 * ch - 4))
                        s_fw[ch % 2].inc(sync.dma_start(
                            w1t_buf[ch % 2][:, :], W1T.ap()[ch]), 16)
                    for ch in slot_w3[gi]:
                        sync.wait_ge(s_h3.h, s_h3.tgt(4 * ch - 4))
                        s_fw3[ch % 2].inc(sync.dma_start(
                            w3t_buf[ch % 2][:, :], W3T.ap()[ch]), 16)
                    for tc in slot_gload[gi]:
                        if tc == 0:
                            sync.wait_ge(s_m7.h, s_m7.tgt(NU1))
                            sync.wait_ge(s_h3.h, s_h3.tgt(NU1))
                            sync.wait_ge(s_gs[0].h,
                                         s_gs[0].tgt(((NU1 + 1) // 2) * 16))
                            sync.wait_ge(s_gs[1].h,
                                         s_gs[1].tgt((NU1 // 2) * 16))
                        else:
                            sync.wait_ge(s_f2m.h, s_f2m.tgt(2 * tc * DMC))
                        for ci in range(HC):
                            s_fg.inc(sync.dma_start(
                                g_res[:, ci * NB:(ci + 1) * NB],
                                GS.ap()[ci, tc]), 16)
                    for v in slot_w2load[gi]:
                        tc, c2 = v // DMC, v % DMC
                        for p in range(2):
                            q = 2 * v + p
                            nci = CI0 if p == 0 else HC - CI0
                            if q < 2:
                                sync.wait_ge(s_m7.h, s_m7.tgt(NU1))
                                sync.wait_ge(s_h3.h, s_h3.tgt(NU1))
                            else:
                                sync.wait_ge(s_f2m.h, s_f2m.tgt(q - 1))
                            s_f2w[p].inc(sync.dma_start(
                                w2t_buf[p][:, :nci * 128],
                                W2T.ap()[c2, :,
                                         p * CI0 * 128:
                                         (p * CI0 + nci) * 128]), 16)
            store_yg(NG - 1)

        @blk.scalar
        def _(scalar):
            def silu(u):
                scalar.wait_ge(s_m7.h, s_m7.tgt(u + 1))
                if u >= 8:
                    scalar.wait_ge(s_g7.h, s_g7.tgt(u - 7))
                h1o = (u % 2) * NB
                s_a7.inc(scalar.activation(h1s[u % 8][:, :],
                                           pB[:, h1o:h1o + TH1], AF.Silu,
                                           scale=s1_col[:, 0:1]))

            def f2evac(v):
                scalar.wait_ge(s_f2m.h, s_f2m.tgt(2 * v + 2))
                if v >= 2:
                    scalar.wait_ge(s_ff[v % 2].h,
                                   s_ff[v % 2].tgt((v // 2) * 16))
                psl = (v % 4) * NB
                s_f2e.inc(scalar.activation(f_stg[v % 2][:, :],
                                            pB[:, psl:psl + NB], AF.Copy,
                                            scale=s23_col[:, 0:1]))

            for cc in range(NCH6):
                if cc >= 1:
                    scalar.wait_ge(s_cs.h, s_cs.tgt(cc * DIC * NS))
                for n in range(NS):
                    s_bc[n * 4 // NS].inc(scalar.dma_start(
                        Cb[:, n * NB:(n + 1) * NB],
                        BCD16.ap()[NS + n:NS + n + 1,
                                   cc * NB:(cc + 1) * NB].broadcast_to(
                            [128, NB])), 16)
                for g in range(DIC):
                    gi = cc * DIC + g
                    for u in slot_si[gi]:
                        if si_pre(u):
                            silu(u)
                    for v in slot_ev2[gi]:
                        if ev_pre(v):
                            f2evac(v)
                    scalar.wait_ge(s_ldq[g % 2].h,
                                   s_ldq[g % 2].tgt((gi // 2 + 1) * 48))
                    for n in range(NS):
                        k = gi * NS + n
                        if k >= 3:
                            scalar.wait_ge(s_sc.h, s_sc.tgt(k - 2))
                        s_ee.inc(scalar.activation(
                            e_b[k % 3][:, :], dlt_b[g % 2][:, :], AF.Exp,
                            scale=a_sb[:, g * NS + n:g * NS + n + 1]))
                    for u in slot_si[gi]:
                        if not si_pre(u):
                            silu(u)
                    for v in slot_ev2[gi]:
                        if not ev_pre(v):
                            f2evac(v)

        @blk.vector
        def _(vector):
            def h3reg(u):
                off = 2 * NB + (u % 2) * NB
                return pB[:, off:off + TH1]

            def gmul(u):
                vector.wait_ge(s_a7.h, s_a7.tgt(u + 1))
                vector.wait_ge(s_h3.h, s_h3.tgt(u + 1))
                if u >= 2:
                    vector.wait_ge(s_gs[u % 2].h,
                                   s_gs[u % 2].tgt((u // 2) * 16))
                s_g7.inc(vector.tensor_mul(g_stg[u % 2][:, :],
                                           h1s[u % 8][:, :], h3reg(u)))

            for cc in range(NCH6):
                for g in range(DIC):
                    gi = cc * DIC + g
                    for u in slot_gm[gi]:
                        if gm_pre(u):
                            gmul(u)
                    vector.wait_ge(s_ldq[g % 2].h,
                                   s_ldq[g % 2].tgt((gi // 2 + 1) * 48))

                    VCH.inc(vector.tensor_mul(dtu_b[g % 2][:, :],
                                              dlt_b[g % 2][:, :],
                                              xc_b[g % 2][:, :]))
                    vfence(vector, VCH)
                    for n in range(NS):
                        k = gi * NS + n
                        if g == 0 and n % (NS // 4) == 0:
                            q4 = n * 4 // NS
                            vector.wait_ge(
                                s_bb[q4].h,
                                s_bb[q4].tgt((cc + 1) * (NS // 4) * 16))
                        VCH.inc(vector.tensor_mul(dbu_b[k % 2][:, :],
                                                  dtu_b[g % 2][:, :],
                                                  Bb[:, n * NB:(n + 1) * NB]))
                        vector.wait_ge(s_ee.h, s_ee.tgt(k + 1))
                        if k >= 3:
                            vector.wait_ge(s_gc.h, s_gc.tgt(k - 2))
                        if cc >= 1:
                            vector.wait_ge(
                                s_gc.h,
                                s_gc.tgt((cc - 1) * DIC * NS + g * NS + n + 1))
                        vfence(vector, VCH)
                        s_sc.inc(vector.tensor_tensor_scan(
                            s_bK[k % 3][:, :], e_b[k % 3][:, :],
                            dbu_b[k % 2][:, :],
                            (state[:, g * NS + n:g * NS + n + 1]
                             if cc >= 1 else 0.0),
                            ALU.mult, ALU.add))
                        if k >= 8:
                            vector.wait_ge(s_im.h, s_im.tgt(k - 7))
                        if g == 0 and n % (NS // 4) == 0:
                            q4 = n * 4 // NS
                            vector.wait_ge(
                                s_bc[q4].h,
                                s_bc[q4].tgt((cc + 1) * (NS // 4) * 16))
                        vfence(vector, s_sc)
                        s_cs.inc(vector.tensor_mul(
                            cs_b[k % 8][:, :], s_bK[k % 3][:, :],
                            Cb[:, n * NB:(n + 1) * NB]))
                    vector.wait_ge(s_im.h, s_im.tgt((gi + 1) * NS))
                    vector.wait_ge(s_ldq[g % 2].h,
                                   s_ldq[g % 2].tgt((gi // 2 + 1) * 48))
                    ysl = (gi % 2) * NB
                    VCH.inc(vector.scalar_tensor_tensor(
                        t2_b[gi % 2][:, :], xc_b[g % 2][:, :],
                        d_sb[:, g:g + 1], pA[:, ysl:ysl + NB],
                        ALU.mult, ALU.add))
                    if gi >= 2:
                        vector.wait_ge(s_sy[gi % 2].h,
                                       s_sy[gi % 2].tgt((gi - 2) // 2 * 16
                                                        + 16))
                    vfence(vector, VCH)
                    s_gt.inc(vector.tensor_mul(yg_stg[gi % 2][:, :],
                                               t2_b[gi % 2][:, :],
                                               szg_b[g % 2][:, :]))
                    for u in slot_gm[gi]:
                        if not gm_pre(u):
                            gmul(u)

        @blk.tensor
        def _(tensor):
            xnr = xn_f[:, :].rearrange("p (j t) -> p j t", t=TPAD)

            def h3steps(u):
                ch, qt = u // 4, u % 4
                po = 2 * NB + (u % 2) * NB
                steps = []

                def mk(j, bs, bn, first):
                    def go():
                        if first:
                            tensor.wait_ge(
                                s_fw3[ch % 2].h,
                                s_fw3[ch % 2].tgt((ch // 2 + 1) * 16))
                            if u >= 2:
                                tensor.wait_ge(s_g7.h, s_g7.tgt(u - 1))
                        mm = tensor.matmul(
                            pB[:, po + bs:po + bs + bn],
                            w3t_buf[ch % 2][:, j * 128:(j + 1) * 128],
                            xnr[:, j, 3 + qt * TH1 + bs:
                                3 + qt * TH1 + bs + bn],
                            start=(j == 0), stop=(j == DMC - 1))
                        if j == DMC - 1 and bs == h_blocks(TH1)[-1][0]:
                            s_h3.inc(mm)
                    return go
                for j in range(DMC):
                    for (bs, bn) in h_blocks(TH1):
                        steps.append(mk(j, bs, bn, j == 0 and bs == 0))
                return steps

            def h3chain(u):
                for stp in h3steps(u):
                    stp()

            def h1steps(u):
                ch, qt = u // 4, u % 4
                po = (u % 2) * NB
                steps = []

                def mk(j, bs, bn, first):
                    def go():
                        if first:
                            tensor.wait_ge(
                                s_fw[ch % 2].h,
                                s_fw[ch % 2].tgt((ch // 2 + 1) * 16))
                            if u >= 2:
                                tensor.wait_ge(s_a7.h, s_a7.tgt(u - 1))
                        mm = tensor.matmul(
                            pB[:, po + bs:po + bs + bn],
                            w1t_buf[ch % 2][:, j * 128:(j + 1) * 128],
                            xnr[:, j, 3 + qt * TH1 + bs:
                                3 + qt * TH1 + bs + bn],
                            start=(j == 0), stop=(j == DMC - 1))
                        if j == DMC - 1 and bs == h_blocks(TH1)[-1][0]:
                            s_m7.inc(mm)
                    return go
                for j in range(DMC):
                    for (bs, bn) in h_blocks(TH1):
                        steps.append(mk(j, bs, bn, j == 0 and bs == 0))
                return steps

            def h1chain(u):
                for stp in h1steps(u):
                    stp()

            def f2steps(v):
                tc, c2 = v // DMC, v % DMC
                psl = (v % 4) * NB
                steps = []

                def mk(p, ci, first):
                    def go():
                        if first and p == 0:
                            if v == 0:
                                tensor.wait_ge(s_a7.h, s_a7.tgt(NU1))
                                tensor.wait_ge(s_g7.h, s_g7.tgt(NU1))
                            if v >= 4:
                                tensor.wait_ge(s_f2e.h, s_f2e.tgt(v - 3))
                            tensor.wait_ge(s_fg.h,
                                           s_fg.tgt((tc + 1) * HC * 16))
                        if first:
                            tensor.wait_ge(s_f2w[p].h,
                                           s_f2w[p].tgt((v + 1) * 16))
                        ci_lo = 0 if p == 0 else CI0
                        mm = tensor.matmul(
                            pB[:, psl:psl + NB],
                            w2t_buf[p][:, (ci - ci_lo) * 128:
                                       (ci - ci_lo + 1) * 128],
                            g_res[:, ci * NB:(ci + 1) * NB],
                            start=(ci == 0), stop=(ci == HC - 1))
                        if ci == (CI0 - 1 if p == 0 else HC - 1):
                            s_f2m.inc(mm)
                    return go
                for p in range(2):
                    ci_lo = 0 if p == 0 else CI0
                    ci_hi = CI0 if p == 0 else HC
                    for ci in range(ci_lo, ci_hi):
                        steps.append(mk(p, ci, ci == ci_lo))
                return steps

            for cc in range(NCH6):
                for g in range(DIC):
                    gi = cc * DIC + g
                    ysl = (gi % 2) * NB
                    qs = []
                    for u in slot_f1[gi]:
                        if h1_pre(u):
                            qs += h1steps(u)
                    for u in slot_h3[gi]:
                        if h3_pre(u):
                            qs += h3steps(u)
                    for v in slot_f2[gi]:
                        if f2slot(v) < NG - 1:
                            qs += f2steps(v)
                    qp = 0
                    rr = -(-len(qs) // NS) if qs else 0
                    if gi >= 2:
                        tensor.wait_ge(s_gt.h, s_gt.tgt(gi - 1))
                    for n in range(NS):
                        k = gi * NS + n
                        for _ in range(rr):
                            if qp < len(qs):
                                qs[qp]()
                                qp += 1
                        tensor.wait_ge(s_cs.h, s_cs.tgt(k + 1))
                        s_im.inc(tensor.matmul(
                            pA[:, ysl:ysl + NB], ident_ng[:, :],
                            cs_b[k % 8][:, :],
                            start=(n == 0), stop=(n == NS - 1)))
                    while qp < len(qs):
                        qs[qp]()
                        qp += 1
                    for u in slot_f1[gi]:
                        if not h1_pre(u):
                            h1chain(u)
                    for u in slot_h3[gi]:
                        if not h3_pre(u):
                            h3chain(u)
                    for v in slot_f2[gi]:
                        if not (f2slot(v) < NG - 1):
                            for stp in f2steps(v):
                                stp()

    drain()

    TH = T // 2
    thb = [(s, min(NB, TH - s)) for s in range(0, TH, NB)]
    yg_sb = big("p7yg", [128, DIC * TH], BF16, 0)
    P7O = DIC * TH * 2
    hwq_buf = [big(f"p7hw_{i}", [128, DIC * 128], BF16,
                   P7O + i * DIC * 128 * 2) for i in range(2)]
    P7O2 = P7O + 2 * DIC * 128 * 2
    f_tile = [big(f"p7f_{i}", [128, TH], F32, P7O2 + i * TH * 4)
              for i in range(2)]
    xt_tile = [big(f"p7x_{i}", [128, TH], F32, P7O2 + (2 + i) * TH * 4)
               for i in range(2)]
    o_stg = [big(f"p7o_{i}", [128, TH], F32, P7O2 + (4 + i) * TH * 4)
             for i in range(2)]

    s_ly12 = pool.get_dma()
    s_lh12 = [pool.get_dma(), pool.get_dma()]
    s_lf12 = [pool.get_dma(), pool.get_dma()]
    s_mm12 = pool.get()
    s_ev12 = pool.get()
    s_fin = [pool.get_dma(), pool.get_dma()]
    for hi in range(2):
        t0 = hi * TH
        b_ly = s_ly12.loc
        b_lh = [s.loc for s in s_lh12]
        b_lf = [s.loc for s in s_lf12]
        b_mm = s_mm12.loc
        b_ev = s_ev12.loc
        b_fin = [s.loc for s in s_fin]
        if hi == 1:
            drain()
        with nc.Block() as blk:
            @blk.sync
            def _(sync):
                for g in range(DIC):
                    s_ly12.inc(sync.dma_start(yg_sb[:, g * TH:(g + 1) * TH],
                                              YG.ap()[g, :, t0:t0 + TH]), 16)
                for cc in range(DMC):
                    if cc >= 2:
                        sync.wait_ge(s_mm12.h, s_mm12.tgt(b_mm + cc - 1))
                    s_lh12[cc % 2].inc(sync.dma_start(
                        hwq_buf[cc % 2][:, :].rearrange("p (j m) -> p j m",
                                                        m=128),
                        HWQ.ap()[cc].rearrange("j p m -> p j m")), 16)
                    if cc >= 2:
                        sync.wait_ge(s_ev12.h, s_ev12.tgt(b_ev + cc - 1))
                    ffv = FF.ap()[cc].rearrange("q p t -> p q t")
                    if TB % 2 == 0:
                        fsrc = ffv[:, hi * (TB // 2):(hi + 1) * (TB // 2)]
                        fq = TB // 2
                    else:
                        fsrc = ffv[:, 0:1, t0:t0 + TH]
                        fq = 1
                    s_lf12[cc % 2].inc(
                        sync.dma_start(f_tile[cc % 2][:, :].rearrange(
                            "p (q t) -> p q t", q=fq), fsrc), 16)
                    s_lf12[cc % 2].inc(sync.dma_start(
                        xt_tile[cc % 2][:, :],
                        XT.ap()[cc * 128:(cc + 1) * 128, t0:t0 + TH]), 16)

            @blk.tensor
            def _(tensor):
                tensor.wait_ge(s_ly12.h, s_ly12.tgt(b_ly + DIC * 16))
                for cc in range(DMC):
                    tensor.wait_ge(s_lh12[cc % 2].h,
                                   s_lh12[cc % 2].tgt(b_lh[cc % 2]
                                                      + (cc // 2 + 1) * 16))
                    if cc >= 2:
                        tensor.wait_ge(s_ev12.h, s_ev12.tgt(b_ev + cc - 1))
                    ps = pA if cc % 2 == 0 else pB
                    last = None
                    for j in range(DIC):
                        st = hwq_buf[cc % 2][:, j * 128:(j + 1) * 128]
                        for (bs, bn) in thb:
                            last = tensor.matmul(
                                ps[:, bs:bs + bn], st,
                                yg_sb[:, j * TH + bs:j * TH + bs + bn],
                                start=(j == 0), stop=(j == DIC - 1))
                    s_mm12.inc(last)

            @blk.vector
            def _(vector):
                for cc in range(DMC):
                    vector.wait_ge(s_mm12.h, s_mm12.tgt(b_mm + cc + 1))
                    vector.wait_ge(s_lf12[cc % 2].h,
                                   s_lf12[cc % 2].tgt(b_lf[cc % 2]
                                                      + (cc // 2 + 1) * 32))
                    if cc >= 2:
                        vector.wait_ge(s_fin[cc % 2].h,
                                       s_fin[cc % 2].tgt(b_fin[cc % 2]
                                                         + (cc - 2) // 2 * 16
                                                         + 16))
                    ps = pA if cc % 2 == 0 else pB
                    VCH.inc(vector.tensor_add(o_stg[cc % 2][:, :], ps[:, :TH],
                                              f_tile[cc % 2][:, :]))
                    vfence(vector, VCH)
                    s_ev12.inc(vector.tensor_add(o_stg[cc % 2][:, :],
                                                 o_stg[cc % 2][:, :],
                                                 xt_tile[cc % 2][:, :]))

            @blk.gpsimd
            def _(gpsimd):
                for cc in range(DMC):
                    gpsimd.wait_ge(s_ev12.h, s_ev12.tgt(b_ev + cc + 1))
                    s_fin[cc % 2].inc(gpsimd.dma_start(
                        OUT.ap()[cc * 128:(cc + 1) * 128, t0:t0 + TH],
                        o_stg[cc % 2][:, :]), 16)

    drain()

    nc.finalize()
    return nc


def _hadamard_cb(DI):
    H = np.array([[1.0]], dtype=np.float64)
    while H.shape[0] < DI:
        H = np.block([[H, H], [H, -H]])
    H = (H / math.sqrt(DI)).astype(ml_dtypes.bfloat16)
    DIC = DI // 128
    Hcb = np.empty((DIC, DI, 128), dtype=ml_dtypes.bfloat16)
    for cidx in range(DIC):
        Hcb[cidx] = np.ascontiguousarray(H[cidx * 128:(cidx + 1) * 128, :].T)
    return Hcb


def make_core_inputs(c, x_full, W, b, h):
    cd = derived(c)
    T, TPAD, DM, DI, HIDP, HID = (cd["T"], cd["TPAD"], cd["DM"], cd["DI"],
                                  cd["HIDP"], cd["HID"])
    xb = np.asarray(x_full[b], np.float32)
    xpad = np.zeros((TPAD, DM), dtype=np.float32)
    t0 = h * T
    if t0 >= 3:
        xpad[0:3] = xb[t0 - 3:t0]
    xpad[3:3 + T] = xb[t0:t0 + T]
    xt = np.ascontiguousarray(xb[t0:t0 + T].T)

    w1p = np.zeros((DM, HIDP), dtype=np.float32)
    w1p[:, :HID] = np.ascontiguousarray(W["w1"].T)
    w3p = np.zeros((DM, HIDP), dtype=np.float32)
    w3p[:, :HID] = np.ascontiguousarray(W["w3"].T)
    w2p = np.zeros((HIDP, DM), dtype=np.float32)
    w2p[:HID] = np.ascontiguousarray(W["w2"].T)
    win_t = np.ascontiguousarray(W["W_in"].T)

    return {
        "XPAD": xpad, "XT": xt,
        "WIN": win_t, "CW": W["conv_w"], "CBI": W["conv_b"],
        "WX": W["W_x"], "WDT": W["W_dt"], "DTB": W["dt_bias"],
        "ALOG": W["A_log"], "DD": W["D"],
        "WOUT": np.ascontiguousarray(W["W_out"].T),
        "W1P": w1p, "W3P": w3p, "W2P": w2p,
        "HCB": _hadamard_cb(DI),
    }


_NC_CACHE = {}


def kernel(L0_attn_hidden_states, L0_attn_W_in, L0_attn_conv_w,
           L0_attn_conv_b, L0_attn_W_x, L0_attn_W_dt, L0_attn_dt_bias,
           L0_attn_A_log, L0_attn_D, L0_attn_W_out, L0_ffn_w1, L0_ffn_w3,
           L0_ffn_w2):
    c = cfg_full()
    cd = derived(c)
    T, DM = cd["T"], cd["DM"]
    x = np.asarray(L0_attn_hidden_states, dtype=np.float32)
    W = dict(W_in=np.asarray(L0_attn_W_in, np.float32),
             conv_w=np.asarray(L0_attn_conv_w, np.float32),
             conv_b=np.asarray(L0_attn_conv_b, np.float32),
             W_x=np.asarray(L0_attn_W_x, np.float32),
             W_dt=np.asarray(L0_attn_W_dt, np.float32),
             dt_bias=np.asarray(L0_attn_dt_bias, np.float32),
             A_log=np.asarray(L0_attn_A_log, np.float32),
             D=np.asarray(L0_attn_D, np.float32),
             W_out=np.asarray(L0_attn_W_out, np.float32),
             w1=np.asarray(L0_ffn_w1, np.float32),
             w3=np.asarray(L0_ffn_w3, np.float32),
             w2=np.asarray(L0_ffn_w2, np.float32))

    if "full" not in _NC_CACHE:
        _NC_CACHE["full"] = build(c)
    nc = _NC_CACHE["full"]

    in_maps = [make_core_inputs(c, x, W, core // 2, core % 2)
               for core in range(8)]
    res = run_bass_kernel_spmd(nc, in_maps, core_ids=list(range(8)))
    out = np.empty((4, 2 * T, DM), dtype=np.float32)
    for core in range(8):
        b, h = core // 2, core % 2
        out[b, h * T:(h + 1) * T, :] = np.ascontiguousarray(
            res.results[core]["OUT"]).T
    return out
